# version 03
# baseline (speedup 1.0000x reference)
import numpy as np
import ml_dtypes
import concourse.bass as bass
import concourse.mybir as mybir
from concourse.bass_utils import run_bass_kernel_spmd

F32 = mybir.dt.float32
BF16 = mybir.dt.bfloat16
AF = mybir.ActivationFunctionType
ALU = mybir.AluOpType
AX = mybir.AxisListType

NT = 32
LN_LVL = 6
S = 4096
D = 1024
HID = 4096


class Sem:
    def __init__(self, nc, es, name):
        self.h = es.enter_context(nc.semaphore(name))
        self.n = 0

    def inc(self, instr, k=1):
        instr.then_inc(self.h, k)
        self.n += k
        return self.n


def wait(eng, sem, tok=None):
    if tok is None:
        tok = sem.n
    if tok > 0:
        eng.wait_ge(sem.h, tok)


class MK:
    def __init__(self, nc, es, stop_after=None):
        self.nc = nc
        self.es = es
        self.stop_after = stop_after

    def declare(self):
        nc, es = self.nc, self.es
        dram = lambda n, s, d, k="ExternalInput": nc.dram_tensor(n, s, d, kind=k).ap()
        self.x0 = dram("x0", [D, S], F32)
        self.wqa = dram("wqa", [128, 8 * HID], F32)
        self.wka = dram("wka", [128, 8 * HID], F32)
        self.wva = dram("wva", [128, 8 * HID], F32)
        self.woa = dram("woa", [128, 32 * D], F32)
        self.boa = dram("boa", [128, 8], F32)
        self.wqb = dram("wqb", [128, 8 * D], F32)
        self.wkb = dram("wkb", [128, 8 * D], F32)
        self.wvb = dram("wvb", [128, 8 * D], F32)
        self.wob = dram("wob", [128, 8 * D], F32)
        self.w1t = dram("w1t", [128, 8 * HID], F32)
        self.w2t = dram("w2t", [128, 32 * D], F32)
        self.b1 = dram("b1", [128, 32], F32)
        self.b2 = dram("b2", [128, 8], F32)
        self.ident = dram("ident", [128, 128], F32)
        self.cmask = dram("cmask", [128, 128], F32)
        self.out = dram("out", [D, S], F32, "ExternalOutput")

        scr = lambda n, s, d: nc.dram_tensor(n, s, d, kind="Internal").ap()
        self.xn1s = scr("xn1s", [D, S], BF16)
        self.sqs = scr("sqs", [64, NT * 8192], BF16)
        self.svs = scr("svs", [64, NT * 8192], BF16)
        self.x1s = scr("x1s", [D, S], F32)
        self.x2s = scr("x2s", [D, S], F32)
        self.x3s = scr("x3s", [D, S], F32)
        self.qts = scr("qts", [128, 8 * S], BF16)
        self.w1ss = scr("w1ss", [128, 8 * HID], BF16)
        self.w2ss = scr("w2ss", [128, 32 * D], BF16)
        self.ar1s = scr("ar1s", [1, 4096], F32)
        self.ar2s = scr("ar2s", [1, 1024], F32)
        self.a1s = scr("a1s", [128, 32], F32)
        self.a2s = scr("a2s", [128, 8], F32)

        sb = lambda n, s, d: es.enter_context(nc.sbuf_tensor(n, s, d))
        self.identb = sb("identb", [128, 128], BF16)
        self.identf = sb("identf", [128, 128], F32)
        self.cmaskb = sb("cmaskb", [128, 128], BF16)
        self.onesb = sb("onesb", [128, 1], BF16)
        self.onesf = sb("onesf", [1, 128], F32)
        self.c1024 = sb("c1024", [1, 1], F32)
        self.cneg = sb("cneg", [1, 1], F32)
        self.epsln = sb("epsln", [1, 1], F32)
        self.eps6 = sb("eps6", [128, 1], F32)
        self.zero128 = sb("zero128", [128, 1], F32)
        self.c32 = sb("c32", [128, 1], F32)
        self.c1024c = sb("c1024c", [128, 1], F32)
        self.c4096c = sb("c4096c", [128, 1], F32)
        self.c0125 = sb("c0125", [128, 1], F32)
        self.cm05 = sb("cm05", [1, 1], F32)
        self.bo_sb = sb("bo_sb", [128, 8], F32)
        self.b1_sb = sb("b1_sb", [128, 32], F32)
        self.b2_sb = sb("b2_sb", [128, 8], F32)
        self.al1_sb = sb("al1_sb", [128, 32], F32)
        self.al2_sb = sb("al2_sb", [128, 8], F32)
        self.qn = sb("qn", [128, 64], F32)
        self.qnr = sb("qnr", [128, 64], F32)
        self.negm = sb("negm", [128, 8], F32)
        self.nm1 = sb("nm1", [128, 1], F32)
        self.ebias = sb("ebias", [128, 1], F32)
        self.edenom = sb("edenom", [128, 1], F32)
        self.erd = sb("erd", [128, 1], F32)
        self.statrow = sb("statrow", [1, 256], F32)
        self.r_m = sb("r_m", [1, 128], F32)
        self.r_ms = sb("r_ms", [1, 128], F32)
        self.r_var = sb("r_var", [1, 128], F32)
        self.r_std = sb("r_std", [1, 128], F32)
        self.r_a = sb("r_a", [1, 128], F32)
        self.r_t = sb("r_t", [1, 128], F32)
        self.r_b = sb("r_b", [1, 128], F32)

        self.aW = sb("aW", [128, 65536], BF16)
        self.aX = sb("aX", [128, 34816], BF16)

        ps = lambda n, s, d: es.enter_context(nc.psum_tensor(n, s, d))
        self.psA = [ps("psA0", [128, 512], F32), ps("psA1", [128, 512], F32)]
        self.psT = [ps("psT0", [128, 1024], BF16), ps("psT1", [128, 1024], BF16)]
        self.psP = [ps("psP0", [128, 512], F32), ps("psP1", [128, 512], F32)]
        self.psY = ps("psY", [128, 1024], F32)

        sm = lambda n: Sem(self.nc, es, n)
        self.sd = sm("sd")
        self.sg = sm("sg")
        self.sT = sm("sT")
        self.sV = sm("sV")
        self.sS = sm("sS")
        self.sy = sm("sy")
        self.sp = sm("sp")

    def vw(self, arena, off, parts, cols, dtype):
        assert off % 4 == 0
        nb = cols * (2 if dtype == BF16 else 4) // 2
        a = arena[0:parts, off // 2: off // 2 + nb]
        if dtype != BF16:
            a = a.bitcast(dtype)
        return a

    def wA(self, cols=32768, dtype=BF16, parts=128, off=0):
        return self.vw(self.aW, off, parts, cols, dtype)

    def wB(self, cols=32768, dtype=BF16, parts=128, off=0):
        return self.vw(self.aW, 65536, parts, cols, dtype) if off == 0 else \
            self.vw(self.aW, 65536 + off, parts, cols, dtype)

    def barrier(self, block):
        mk = self

        @block.tensor
        def _(tensor):
            mk.sT.inc(tensor.drain())

        @block.vector
        def _(vector):
            mk.sV.inc(vector.drain())

        @block.scalar
        def _(scalar):
            mk.sS.inc(scalar.drain())

        @block.sync
        def _(sync):
            mk.sy.inc(sync.nop())

        @block.gpsimd
        def _(gpsimd):
            mk.sp.inc(gpsimd.drain())
        t = dict(T=mk.sT.n, V=mk.sV.n, S=mk.sS.n, y=mk.sy.n, p=mk.sp.n,
                 d=mk.sd.n, g=mk.sg.n)

        @block.tensor
        def _(tensor):
            wait(tensor, mk.sV, t['V']); wait(tensor, mk.sS, t['S'])
            wait(tensor, mk.sy, t['y']); wait(tensor, mk.sp, t['p'])
            wait(tensor, mk.sd, t['d']); wait(tensor, mk.sg, t['g'])

        @block.vector
        def _(vector):
            wait(vector, mk.sV)
            wait(vector, mk.sT, t['T']); wait(vector, mk.sS, t['S'])
            wait(vector, mk.sy, t['y']); wait(vector, mk.sp, t['p'])
            wait(vector, mk.sd, t['d']); wait(vector, mk.sg, t['g'])

        @block.scalar
        def _(scalar):
            wait(scalar, mk.sS)
            wait(scalar, mk.sT, t['T']); wait(scalar, mk.sV, t['V'])
            wait(scalar, mk.sy, t['y']); wait(scalar, mk.sp, t['p'])
            wait(scalar, mk.sd, t['d']); wait(scalar, mk.sg, t['g'])

        @block.sync
        def _(sync):
            wait(sync, mk.sT, t['T']); wait(sync, mk.sV, t['V'])
            wait(sync, mk.sS, t['S']); wait(sync, mk.sp, t['p'])
            wait(sync, mk.sd, t['d']); wait(sync, mk.sg, t['g'])

        @block.gpsimd
        def _(gpsimd):
            wait(gpsimd, mk.sT, t['T']); wait(gpsimd, mk.sV, t['V'])
            wait(gpsimd, mk.sS, t['S']); wait(gpsimd, mk.sy, t['y'])
            wait(gpsimd, mk.sd, t['d']); wait(gpsimd, mk.sg, t['g'])

    def p0(self, block):
        mk = self

        @block.sync
        def _(sync):
            for dst, src in ((mk.identf, mk.ident), (mk.bo_sb, mk.boa),
                             (mk.b1_sb, mk.b1), (mk.b2_sb, mk.b2)):
                wait(sync, mk.sd)
                i = sync.dma_start(out=dst[:, :], in_=src[:, :])
                mk.sd.inc(i, 16)

        @block.gpsimd
        def _(gpsimd):
            wait(gpsimd, mk.sg)
            i = gpsimd.dma_start(out=mk.identb[:], in_=mk.ident[:, :])
            mk.sg.inc(i, 16)
            wait(gpsimd, mk.sg)
            i = gpsimd.dma_start(out=mk.cmaskb[:], in_=mk.cmask[:, :])
            mk.sg.inc(i, 16)

        @block.vector
        def _(vector):
            wait(vector, mk.sV)
            vector.memset(mk.onesb[:, 0:1], 1.0)
            vector.memset(mk.onesf[0:1, :], 1.0)
            vector.memset(mk.c1024[0:1, 0:1], 1.0 / 1024.0)
            vector.memset(mk.cneg[0:1, 0:1], -1.0)
            vector.memset(mk.epsln[0:1, 0:1], 1e-5)
            vector.memset(mk.eps6[:, 0:1], 1e-6)
            vector.memset(mk.zero128[:, 0:1], 0.0)
            vector.memset(mk.c32[:, 0:1], float(D) ** -0.5)
            vector.memset(mk.c1024c[:, 0:1], 1.0 / 1024.0)
            vector.memset(mk.c4096c[:, 0:1], 1.0 / 4096.0)
            vector.memset(mk.cm05[0:1, 0:1], -0.5)
            i = vector.memset(mk.c0125[:, 0:1], 0.125)
            mk.sV.inc(i)

    def load_w(self, block, dst, src, n_parts=4):
        mk = self
        cols = src.shape[1]
        step = cols // n_parts

        @block.gpsimd
        def _(gpsimd):
            wait(gpsimd, mk.sg)
            for i_ in range(n_parts):
                ins = gpsimd.dma_start(out=dst[:, i_ * step:(i_ + 1) * step],
                                       in_=src[:, i_ * step:(i_ + 1) * step])
                mk.sg.inc(ins, 16)
        return mk.sg.n

    def emit_ln(self, block, x3, xnb_out, extra_wait, out_wait, lnps=None):
        mk = self
        lvl = LN_LVL
        lnps = lnps if lnps is not None else mk.psY
        lnt = self.vw(self.aX, 61440, 128, 1024, F32)
        xbb = self.vw(self.aX, 65536, 128, 1024, BF16)
        xsqb = self.vw(self.aX, 67584, 128, 1024, BF16)
        xbb3 = xbb.rearrange("q (c p) -> q c p", c=8)
        xsq3 = xsqb.rearrange("q (c p) -> q c p", c=8)

        @block.vector
        def _(vector):
            wait(vector, mk.sV)
            extra_wait(vector)
            i = vector.tensor_copy(out=xbb3, in_=x3)
            mk.sV.inc(i)
        t_xbb = mk.sV.n

        @block.gpsimd
        def _(gpsimd):
            wait(gpsimd, mk.sp)
            wait(gpsimd, mk.sV, t_xbb)
            i = gpsimd.tensor_mul(out=xsqb[:, :], in0=xbb[:, :], in1=xbb[:, :])
            mk.sp.inc(i)
        t_xsq = mk.sp.n
        if lvl < 2:
            return mk.sV.n

        @block.tensor
        def _(tensor):
            wait(tensor, mk.sV, t_xbb)
            wait(tensor, mk.sp, t_xsq)
            xb3 = xbb.rearrange("q (c p) -> q c p", c=8)
            xs3 = xsqb.rearrange("q (c p) -> q c p", c=8)
            for c in range(8):
                tensor.matmul(lnps[0:1, 0:128], mk.onesb[:, 0:1], xb3[:, c, :],
                              start=(c == 0), stop=(c == 7))
            for c in range(8):
                i = tensor.matmul(lnps[0:1, 128:256], mk.onesb[:, 0:1], xs3[:, c, :],
                                  start=(c == 0), stop=(c == 7))
            mk.sT.inc(i)
        t_stats = mk.sT.n

        @block.vector
        def _(vector):
            wait(vector, mk.sV)
            wait(vector, mk.sT, t_stats)
            mk.sV.inc(vector.tensor_scalar(out=mk.r_m[0:1, :], in0=mk.psY[0:1, 0:128],
                                           scalar1=mk.c1024[0:1, 0:1], scalar2=None, op0=ALU.mult))
            wait(vector, mk.sV)
            mk.sV.inc(vector.tensor_mul(out=mk.r_ms[0:1, :], in0=mk.r_m[0:1, :], in1=mk.r_m[0:1, :]))
            wait(vector, mk.sV)
            i = vector.scalar_tensor_tensor(out=mk.r_var[0:1, :], in0=mk.psY[0:1, 128:256],
                                            scalar=mk.c1024[0:1, 0:1], in1=mk.r_ms[0:1, :],
                                            op0=ALU.mult, op1=ALU.subtract)
            mk.sV.inc(i)
        t_var = mk.sV.n
        if lvl < 4:
            return mk.sV.n

        @block.scalar
        def _(scalar):
            wait(scalar, mk.sS)
            wait(scalar, mk.sV, t_var)
            mk.sS.inc(scalar.activation(out=mk.r_std[0:1, :], in_=mk.r_var[0:1, :], func=AF.Ln,
                                        bias=mk.epsln[0:1, 0:1]))
            wait(scalar, mk.sS)
            i = scalar.activation(out=mk.r_a[0:1, :], in_=mk.r_std[0:1, :], func=AF.Exp,
                                  scale=mk.cm05[0:1, 0:1])
            mk.sS.inc(i)
        t_std = mk.sS.n

        @block.vector
        def _(vector):
            wait(vector, mk.sV)
            wait(vector, mk.sS, t_std)
            i = vector.scalar_tensor_tensor(out=mk.r_b[0:1, :], in0=mk.r_m[0:1, :],
                                            scalar=mk.cneg[0:1, 0:1], in1=mk.r_a[0:1, :],
                                            op0=ALU.mult, op1=ALU.mult)
            mk.sV.inc(i)
        t_ab = mk.sV.n

        @block.tensor
        def _(tensor):
            wait(tensor, mk.sV, t_ab)
            tensor.matmul(lnps[:, 256:384], mk.onesf[0:1, :], mk.r_a[0:1, :],
                          start=True, stop=True)
            i = tensor.matmul(lnps[:, 384:512], mk.onesf[0:1, :], mk.r_b[0:1, :],
                              start=True, stop=True)
            mk.sT.inc(i)
        t_rep = mk.sT.n
        if lvl < 6:
            return mk.sV.n

        @block.vector
        def _(vector):
            wait(vector, mk.sV)
            wait(vector, mk.sT, t_rep)
            out_wait(vector)
            arep = lnps[:, 256:384].unsqueeze(1).broadcast_to([128, 8, 128])
            brep = lnps[:, 384:512].unsqueeze(1).broadcast_to([128, 8, 128])
            l3 = lnt.rearrange("q (c p) -> q c p", c=8)
            mk.sV.inc(vector.tensor_tensor(out=l3, in0=x3, in1=arep, op=ALU.mult))
            wait(vector, mk.sV)
            i = vector.tensor_tensor(out=xnb_out, in0=l3, in1=brep, op=ALU.add)
            mk.sV.inc(i)
        return mk.sV.n

    def p1(self, block):
        mk = self
        t_wk = self.load_w(block, self.wA(), mk.wka)
        t_wq = self.load_w(block, self.wB(), mk.wqa)
        self.t_wq = t_wq
        wk3 = self.wA().rearrange("q (c f) -> q c f", c=8)
        ksum = self.vw(self.aX, 4096, 128, NT * 64, F32)
        xts = [self.vw(self.aX, 12288, 128, 1024, F32),
               self.vw(self.aX, 16384, 128, 1024, F32)]
        xnbs = [self.vw(self.aX, 20480, 128, 1024, BF16),
                self.vw(self.aX, 22528, 128, 1024, BF16)]
        kst = self.vw(self.aX, 24576, 128, 512, F32)

        xt_free = [0, 0]
        xnb_free = [0, 0]
        xnb_spill = [0, 0]
        bankV = [0, 0]
        x0v = mk.x0.rearrange("(c p) s -> p c s", p=128)
        xn1v = mk.xn1s.rearrange("(c p) s -> p c s", p=128)
        for t in range(NT):
            pi = t % 2

            @block.sync
            def _(sync):
                wait(sync, mk.sV, xt_free[pi])
                wait(sync, mk.sd)
                i = sync.dma_start(out=xts[pi].rearrange("q (c p) -> q c p", c=8),
                                   in_=x0v[:, :, t * 128:(t + 1) * 128])
                mk.sd.inc(i, 16)
            t_x = mk.sd.n

            xn_dst = xnbs[pi].rearrange("q (c p) -> q c p", c=8)

            def extra_wait(eng, _t=t_x):
                wait(eng, mk.sd, _t)

            def out_wait(eng, _a=xnb_free[pi], _b=xnb_spill[pi]):
                wait(eng, mk.sT, _a)
                wait(eng, mk.sd, _b)

            t_norm = self.emit_ln(block, xts[pi].rearrange("q (c p) -> q c p", c=8), xn_dst, extra_wait, out_wait)
            xt_free[pi] = t_norm

            @block.sync
            def _(sync):
                wait(sync, mk.sV, t_norm)
                wait(sync, mk.sd)
                i = sync.dma_start(out=xn1v[:, :, t * 128:(t + 1) * 128], in_=xn_dst)
                mk.sd.inc(i, 16)
            xnb_spill[pi] = mk.sd.n

            xnb3 = xnbs[pi].rearrange("q (c p) -> q c p", c=8)
            for nck in range(8):
                b = nck % 2

                @block.tensor
                def _(tensor):
                    if t == 0 and nck == 0:
                        wait(tensor, mk.sg, t_wk)
                    wait(tensor, mk.sV, t_norm)
                    wait(tensor, mk.sV, bankV[b])
                    for c in range(8):
                        i = tensor.matmul(mk.psA[b][:, 0:512], xnb3[:, c, :],
                                          wk3[:, c, nck * 512:(nck + 1) * 512],
                                          start=(c == 0), stop=(c == 7))
                    mk.sT.inc(i)
                t_proj = mk.sT.n

                @block.vector
                def _(vector):
                    wait(vector, mk.sV)
                    wait(vector, mk.sT, t_proj)
                    mk.sV.inc(vector.tensor_scalar(out=kst[:, 0:512], in0=mk.psA[b][:, 0:512],
                                                   scalar1=mk.zero128[:, 0:1], scalar2=None, op0=ALU.max))
                    wait(vector, mk.sV)
                    i = vector.reduce_sum(out=ksum[:, t * 64 + nck * 8: t * 64 + nck * 8 + 8],
                                          in_=kst.rearrange("q (a b) -> q a b", b=64),
                                          axis=AX.X)
                    mk.sV.inc(i)
                bankV[b] = mk.sV.n
            xnb_free[pi] = mk.sT.n

        @block.vector
        def _(vector):
            wait(vector, mk.sV)
            wait(vector, mk.sV)
            i = vector.tensor_scalar(out=ksum[:, :], in0=ksum[:, :],
                                     scalar1=mk.c0125[:, 0:1], scalar2=None, op0=ALU.mult)
            mk.sV.inc(i)
        self.t_ksum = mk.sV.n

    def p2(self, block):
        mk = self
        self.t_wv = self.load_w(block, self.wA(), mk.wva)
        wq3 = self.wB().rearrange("q (c f) -> q c f", c=8)
        ksum = self.vw(self.aX, 4096, 128, NT * 64, F32)
        xnbs = [self.vw(self.aX, 0, 128, 1024, BF16),
                self.vw(self.aX, 2048, 128, 1024, BF16)]
        qp = self.vw(self.aX, 12288, 128, 4096, F32)
        qb = self.vw(self.aX, 28672, 128, 4096, BF16)
        sqt = [self.vw(self.aX, 36864, 64, 8192, BF16),
               self.vw(self.aX, 53248, 64, 8192, BF16)]
        xn1v = mk.xn1s.rearrange("(c p) s -> p c s", p=128)

        xnb_ld = [0, 0]
        xnb_free = [0, 0]
        bankV = [0, 0]
        qb_free = 0
        psT_free = [0, 0]
        psT_eng = [None, None]
        sq_spill = [0, 0]
        for t in range(NT):
            pi = t % 2

            @block.sync
            def _(sync):
                wait(sync, mk.sT, xnb_free[pi])
                wait(sync, mk.sd)
                i = sync.dma_start(out=xnbs[pi].rearrange("q (c p) -> q c p", c=8),
                                   in_=xn1v[:, :, t * 128:(t + 1) * 128])
                mk.sd.inc(i, 16)
            xnb_ld[pi] = mk.sd.n

            xnb3 = xnbs[pi].rearrange("q (c p) -> q c p", c=8)
            for nck in range(8):
                b = nck % 2

                @block.tensor
                def _(tensor):
                    wait(tensor, mk.sd, xnb_ld[pi])
                    if t == 0 and nck == 0:
                        wait(tensor, mk.sg, self.t_wq)
                    wait(tensor, mk.sV, bankV[b])
                    for c in range(8):
                        i = tensor.matmul(mk.psA[b][:, 0:512], xnb3[:, c, :],
                                          wq3[:, c, nck * 512:(nck + 1) * 512],
                                          start=(c == 0), stop=(c == 7))
                    mk.sT.inc(i)
                t_proj = mk.sT.n

                @block.vector
                def _(vector):
                    wait(vector, mk.sV)
                    wait(vector, mk.sT, t_proj)
                    ks = ksum[:, t * 64:(t + 1) * 64].unsqueeze(1).broadcast_to([128, 8, 64])
                    mk.sV.inc(vector.scalar_tensor_tensor(
                        out=qp[:, nck * 512:(nck + 1) * 512].rearrange("q (a b) -> q a b", b=64),
                        in0=mk.psA[b][:, 0:512].rearrange("q (a b) -> q a b", b=64),
                        scalar=mk.zero128[:, 0:1], in1=ks, op0=ALU.max, op1=ALU.mult))
                    wait(vector, mk.sV)
                    i = vector.reduce_sum(out=mk.qn[:, nck * 8:(nck + 1) * 8],
                                          in_=qp[:, nck * 512:(nck + 1) * 512]
                                          .rearrange("q (a b) -> q a b", b=64),
                                          axis=AX.X)
                    mk.sV.inc(i)
                bankV[b] = mk.sV.n
            xnb_free[pi] = mk.sT.n

            @block.vector
            def _(vector):
                wait(vector, mk.sV)
                wait(vector, mk.sV)
                mk.sV.inc(vector.tensor_scalar(out=mk.qnr[:, :], in0=mk.qn[:, :],
                                               scalar1=mk.eps6[:, 0:1], scalar2=None, op0=ALU.add))
                wait(vector, mk.sV)
                mk.sV.inc(vector.reciprocal(out=mk.qnr[:, :], in_=mk.qnr[:, :]))
                wait(vector, mk.sV)
                wait(vector, mk.sT, qb_free)
                qr = mk.qnr[:, :].unsqueeze(2).broadcast_to([128, 64, 64])
                i = vector.tensor_tensor(out=qb.rearrange("q (a b) -> q a b", b=64),
                                         in0=qp.rearrange("q (a b) -> q a b", b=64),
                                         in1=qr, op=ALU.mult)
                mk.sV.inc(i)
            t_qb = mk.sV.n

            for w in range(8):
                ps_ = w % 2

                @block.tensor
                def _(tensor):
                    wait(tensor, mk.sV, t_qb)
                    if psT_eng[ps_] is not None:
                        wait(tensor, psT_eng[ps_], psT_free[ps_])
                    for u in range(8):
                        i = tensor.transpose(mk.psT[ps_][0:64, u * 128:(u + 1) * 128],
                                             qb[:, (w * 8 + u) * 64:(w * 8 + u + 1) * 64],
                                             mk.identb[:, :])
                    mk.sT.inc(i)
                t_w = mk.sT.n

                if w % 2 == 0:
                    @block.vector
                    def _(vector):
                        wait(vector, mk.sV)
                        wait(vector, mk.sT, t_w)
                        wait(vector, mk.sd, sq_spill[pi])
                        i = vector.tensor_copy(out=sqt[pi][:, w * 1024:(w + 1) * 1024],
                                               in_=mk.psT[ps_][0:64, :])
                        mk.sV.inc(i)
                    psT_free[ps_], psT_eng[ps_] = mk.sV.n, mk.sV
                else:
                    @block.vector
                    def _(vector):
                        wait(vector, mk.sV)
                        wait(vector, mk.sT, t_w)
                        wait(vector, mk.sd, sq_spill[pi])
                        i = vector.tensor_copy(out=sqt[pi][:, w * 1024:(w + 1) * 1024],
                                        in_=mk.psT[ps_][0:64, :])
                        mk.sV.inc(i)
                    psT_free[ps_], psT_eng[ps_] = mk.sV.n, mk.sV
            qb_free = mk.sT.n

            @block.sync
            def _(sync):
                wait(sync, mk.sV, mk.sV.n)
                wait(sync, mk.sS, mk.sS.n)
                wait(sync, mk.sd)
                i = sync.dma_start(out=mk.sqs[:, t * 8192:(t + 1) * 8192],
                                   in_=sqt[pi][:, :])
                mk.sd.inc(i, 16)
            sq_spill[pi] = mk.sd.n

    def p3a(self, block):
        mk = self
        self.t_wo = self.load_w(block, self.wB(), mk.woa)
        wv3 = self.wA().rearrange("q (c f) -> q c f", c=8)
        xnbs = [self.vw(self.aX, 0, 128, 1024, BF16),
                self.vw(self.aX, 2048, 128, 1024, BF16)]
        vb = self.vw(self.aX, 28672, 128, 4096, BF16)
        svt = [self.vw(self.aX, 36864, 64, 8192, BF16),
               self.vw(self.aX, 53248, 64, 8192, BF16)]
        xn1v = mk.xn1s.rearrange("(c p) s -> p c s", p=128)

        xnb_ld = [0, 0]
        xnb_free = [0, 0]
        bankE = [0, 0]
        bankEng = [None, None]
        vb_free = 0
        psT_free = [0, 0]
        psT_eng = [None, None]
        sv_spill = [0, 0]
        for t in range(NT):
            pi = t % 2

            @block.sync
            def _(sync):
                wait(sync, mk.sT, xnb_free[pi])
                wait(sync, mk.sd)
                i = sync.dma_start(out=xnbs[pi].rearrange("q (c p) -> q c p", c=8),
                                   in_=xn1v[:, :, t * 128:(t + 1) * 128])
                mk.sd.inc(i, 16)
            xnb_ld[pi] = mk.sd.n

            xnb3 = xnbs[pi].rearrange("q (c p) -> q c p", c=8)
            for nck in range(8):
                b = nck % 2

                @block.tensor
                def _(tensor):
                    wait(tensor, mk.sd, xnb_ld[pi])
                    if t == 0 and nck == 0:
                        wait(tensor, mk.sg, self.t_wv)
                    if bankEng[b] is not None:
                        wait(tensor, bankEng[b], bankE[b])
                    for c in range(8):
                        i = tensor.matmul(mk.psA[b][:, 0:512], xnb3[:, c, :],
                                          wv3[:, c, nck * 512:(nck + 1) * 512],
                                          start=(c == 0), stop=(c == 7))
                    mk.sT.inc(i)
                t_proj = mk.sT.n

                if nck % 2 == 0:
                    @block.vector
                    def _(vector):
                        wait(vector, mk.sV)
                        wait(vector, mk.sT, t_proj)
                        wait(vector, mk.sT, vb_free)
                        i = vector.tensor_copy(out=vb[:, nck * 512:(nck + 1) * 512],
                                               in_=mk.psA[b][:, 0:512])
                        mk.sV.inc(i)
                    bankE[b], bankEng[b] = mk.sV.n, mk.sV
                else:
                    @block.vector
                    def _(vector):
                        wait(vector, mk.sV)
                        wait(vector, mk.sT, t_proj)
                        wait(vector, mk.sT, vb_free)
                        i = vector.tensor_copy(out=vb[:, nck * 512:(nck + 1) * 512],
                                        in_=mk.psA[b][:, 0:512])
                        mk.sV.inc(i)
                    bankE[b], bankEng[b] = mk.sV.n, mk.sV
            xnb_free[pi] = mk.sT.n
            t_vb_v, t_vb_s = mk.sV.n, mk.sS.n

            for w in range(8):
                ps_ = w % 2

                @block.tensor
                def _(tensor):
                    wait(tensor, mk.sV, t_vb_v)
                    wait(tensor, mk.sS, t_vb_s)
                    if psT_eng[ps_] is not None:
                        wait(tensor, psT_eng[ps_], psT_free[ps_])
                    for u in range(8):
                        i = tensor.transpose(mk.psT[ps_][0:64, u * 128:(u + 1) * 128],
                                             vb[:, (w * 8 + u) * 64:(w * 8 + u + 1) * 64],
                                             mk.identb[:, :])
                    mk.sT.inc(i)
                t_w = mk.sT.n

                if w % 2 == 0:
                    @block.vector
                    def _(vector):
                        wait(vector, mk.sV)
                        wait(vector, mk.sT, t_w)
                        wait(vector, mk.sd, sv_spill[pi])
                        i = vector.tensor_copy(out=svt[pi][:, w * 1024:(w + 1) * 1024],
                                               in_=mk.psT[ps_][0:64, :])
                        mk.sV.inc(i)
                    psT_free[ps_], psT_eng[ps_] = mk.sV.n, mk.sV
                else:
                    @block.vector
                    def _(vector):
                        wait(vector, mk.sV)
                        wait(vector, mk.sT, t_w)
                        wait(vector, mk.sd, sv_spill[pi])
                        i = vector.tensor_copy(out=svt[pi][:, w * 1024:(w + 1) * 1024],
                                        in_=mk.psT[ps_][0:64, :])
                        mk.sV.inc(i)
                    psT_free[ps_], psT_eng[ps_] = mk.sV.n, mk.sV
            vb_free = mk.sT.n

            @block.sync
            def _(sync):
                wait(sync, mk.sV, mk.sV.n)
                wait(sync, mk.sS, mk.sS.n)
                wait(sync, mk.sd)
                i = sync.dma_start(out=mk.svs[:, t * 8192:(t + 1) * 8192],
                                   in_=svt[pi][:, :])
                mk.sd.inc(i, 16)
            sv_spill[pi] = mk.sd.n

    def p3b(self, block):
        mk = self
        entry_sV, entry_sS = mk.sV.n, mk.sS.n
        wo3 = self.wB().rearrange("q (g m) -> q g m", g=32)
        sqt = [self.wA(8192, BF16, 64, 0), self.wA(8192, BF16, 64, 16384)]
        svt = [self.wA(8192, BF16, 64, 32768), self.wA(8192, BF16, 64, 49152)]
        s2d = [self.vw(self.aX, 0, 128, 8192, BF16),
               self.vw(self.aX, 16384, 128, 8192, BF16)]
        xts = [self.vw(self.aX, 32768, 128, 1024, F32),
               self.vw(self.aX, 36864, 128, 1024, F32)]
        xos = [self.vw(self.aX, 40960, 128, 1024, F32),
               self.vw(self.aX, 45056, 128, 1024, F32)]
        yb = self.vw(self.aX, 49152, 128, 1024, F32)
        x0v = mk.x0.rearrange("(c p) s -> p c s", p=128)
        x1v = mk.x1s.rearrange("(c p) s -> p c s", p=128)

        sq_free = [0, 0]
        x_free = [0, 0]
        s2_mm = [0, 0]
        xo_spill = [0, 0]
        bankP = [0, 0]
        bankPe = [None, None]
        psy_sc = 0
        psy_tt = 0
        psa_free = [0, 0]
        psa_eng = [None, None]
        for t in range(NT):
            pi = t % 2

            @block.sync
            def _(sync):
                wait(sync, mk.sT, sq_free[pi])
                wait(sync, mk.sd)
                i = sync.dma_start(out=sqt[pi][:, :], in_=mk.sqs[:, t * 8192:(t + 1) * 8192])
                mk.sd.inc(i, 16)
                wait(sync, mk.sd)
                i = sync.dma_start(out=svt[pi][:, :], in_=mk.svs[:, t * 8192:(t + 1) * 8192])
                mk.sd.inc(i, 16)
                wait(sync, mk.sV, x_free[pi])
                wait(sync, mk.sd)
                i = sync.dma_start(out=xts[pi].rearrange("q (c p) -> q c p", c=8),
                                   in_=x0v[:, :, t * 128:(t + 1) * 128])
                mk.sd.inc(i, 16)
            t_loads = mk.sd.n

            sv3 = svt[pi].rearrange("d (j p) -> d j p", p=128)
            sq3 = sqt[pi].rearrange("d (i p) -> d i p", p=128)
            for pg in range(16):
                b = pg % 2

                @block.tensor
                def _(tensor):
                    wait(tensor, mk.sd, t_loads)
                    if t == 0 and pg < 2:
                        wait(tensor, mk.sV, entry_sV)
                        wait(tensor, mk.sS, entry_sS)
                        wait(tensor, mk.sg, self.t_wo)
                    if bankPe[b] is not None:
                        wait(tensor, bankPe[b], bankP[b])
                    for u in range(8):
                        p = pg * 8 + u
                        lhsT = sv3[:, :, p]
                        rhs = sq3[:, :, p]
                        tensor.matmul(mk.psP[b][0:64, u * 64:(u + 1) * 64], lhsT, rhs,
                                      start=True, stop=True)
                        i = tensor.matmul(mk.psP[b][64:128, u * 64:(u + 1) * 64], lhsT, rhs,
                                          start=True, stop=True, tile_position=(0, 64))
                    mk.sT.inc(i)
                t_pg = mk.sT.n

                if pg % 2 == 0:
                    @block.vector
                    def _(vector):
                        wait(vector, mk.sV)
                        wait(vector, mk.sT, t_pg)
                        wait(vector, mk.sT, s2_mm[pi])
                        i = vector.tensor_copy(out=s2d[pi][:, pg * 512:(pg + 1) * 512],
                                               in_=mk.psP[b][:, :])
                        mk.sV.inc(i)
                    bankP[b], bankPe[b] = mk.sV.n, mk.sV
                else:
                    @block.vector
                    def _(vector):
                        wait(vector, mk.sV)
                        wait(vector, mk.sT, t_pg)
                        wait(vector, mk.sT, s2_mm[pi])
                        i = vector.tensor_copy(out=s2d[pi][:, pg * 512:(pg + 1) * 512],
                                        in_=mk.psP[b][:, :])
                        mk.sV.inc(i)
                    bankP[b], bankPe[b] = mk.sV.n, mk.sV
            sq_free[pi] = mk.sT.n
            t_s2_v, t_s2_s = mk.sV.n, mk.sS.n

            s2e = s2d[pi][0:64, :].rearrange("j (p i) -> j p i", i=64)
            s2o = s2d[pi][64:128, :].rearrange("j (p i) -> j p i", i=64)
            for h in range(2):
                @block.tensor
                def _(tensor):
                    wait(tensor, mk.sV, t_s2_v)
                    wait(tensor, mk.sS, t_s2_s)
                    wait(tensor, mk.sV, psy_sc)
                    wait(tensor, mk.sV, psy_tt)
                    for ip in range(32):
                        i0, i1 = 2 * ip, 2 * ip + 1
                        tensor.matmul(mk.psY[:, 0:512], s2e[:, :, i0],
                                      wo3[0:64, i0 // 2, h * 512:(h + 1) * 512],
                                      start=(ip == 0), stop=(ip == 31))
                        i = tensor.matmul(mk.psY[:, 512:1024], s2o[:, :, i1],
                                          wo3[64:128, i1 // 2, h * 512:(h + 1) * 512],
                                          start=(ip == 0), stop=(ip == 31))
                    mk.sT.inc(i)
                t_wo_mm = mk.sT.n

                @block.vector
                def _(vector):
                    wait(vector, mk.sV)
                    wait(vector, mk.sT, t_wo_mm)
                    i = vector.tensor_copy(out=yb[:, h * 512:(h + 1) * 512], in_=mk.psY[:, 0:512])
                    mk.sV.inc(i)
                psy_sc = mk.sS.n

                @block.vector
                def _(vector):
                    wait(vector, mk.sV)
                    i = vector.tensor_tensor(out=yb[:, h * 512:(h + 1) * 512],
                                             in0=yb[:, h * 512:(h + 1) * 512],
                                             in1=mk.psY[:, 512:1024], op=ALU.add)
                    mk.sV.inc(i)
                psy_tt = mk.sV.n
            s2_mm[pi] = mk.sT.n
            t_yb = mk.sV.n

            for c in range(8):
                a = c % 2
                slot = c // 2

                @block.tensor
                def _(tensor):
                    wait(tensor, mk.sV, t_yb)
                    if psa_eng[a] is not None:
                        wait(tensor, psa_eng[a], psa_free[a])
                    i = tensor.matmul(mk.psA[a][:, slot * 128:(slot + 1) * 128],
                                      yb[:, c * 128:(c + 1) * 128], mk.identf[:, :],
                                      start=True, stop=True)
                    mk.sT.inc(i)
                t_tp = mk.sT.n

                @block.vector
                def _(vector):
                    wait(vector, mk.sV)
                    wait(vector, mk.sT, t_tp)
                    wait(vector, mk.sd, t_loads)
                    wait(vector, mk.sd, xo_spill[pi])
                    i = vector.scalar_tensor_tensor(
                        out=xos[pi][:, c * 128:(c + 1) * 128],
                        in0=mk.psA[a][:, slot * 128:(slot + 1) * 128],
                        scalar=mk.bo_sb[:, c:c + 1],
                        in1=xts[pi][:, c * 128:(c + 1) * 128],
                        op0=ALU.add, op1=ALU.add)
                    mk.sV.inc(i)
                psa_free[a], psa_eng[a] = mk.sV.n, mk.sV
            t_res = mk.sV.n
            x_free[pi] = t_res

            @block.sync
            def _(sync):
                wait(sync, mk.sV, t_res)
                wait(sync, mk.sd)
                i = sync.dma_start(out=x1v[:, :, t * 128:(t + 1) * 128],
                                   in_=xos[pi].rearrange("q (c p) -> q c p", c=8))
                mk.sd.inc(i, 16)
            xo_spill[pi] = mk.sd.n

    def p4(self, block, xin, xout, prep, ngroups=8, depth=3, no_reload=False):
        mk = self
        w1s3 = self.wA().rearrange("q (c f) -> q c f", c=8)
        w2s3 = self.wB().rearrange("q (c f) -> q c f", c=32)
        XG = self.vw(self.aX, 0, 128, 4096, F32)
        XG3 = XG.rearrange("q (c p) -> q c p", c=8)
        XNG = self.vw(self.aX, 16384, 128, 4096, BF16)
        XNG3 = XNG.rearrange("q (c p) -> q c p", c=8)
        HB = self.vw(self.aX, 24576, 128, 16384, BF16)
        TMP = self.vw(self.aX, 57344, 128, 512, F32)
        xinv = xin.rearrange("(c p) s -> p c s", p=128)
        xoutv = xout.rearrange("(c p) s -> p c s", p=128)

        if prep:
            self.p4_prep(block)
        elif no_reload:
            self._p4_wtok = (0, 0)
        else:
            @block.gpsimd
            def _(gpsimd):
                wait(gpsimd, mk.sg)
                for i_ in range(4):
                    ins = gpsimd.dma_start(out=self.wA()[:, i_ * 8192:(i_ + 1) * 8192],
                                           in_=mk.w1ss[:, i_ * 8192:(i_ + 1) * 8192])
                    mk.sg.inc(ins, 16)
                for i_ in range(4):
                    ins = gpsimd.dma_start(out=self.wB()[:, i_ * 8192:(i_ + 1) * 8192],
                                           in_=mk.w2ss[:, i_ * 8192:(i_ + 1) * 8192])
                    mk.sg.inc(ins, 16)
            t_w = mk.sg.n

            @block.sync
            def _(sync):
                wait(sync, mk.sd)
                i = sync.dma_start(out=mk.al1_sb[:, :], in_=mk.a1s[:, :])
                mk.sd.inc(i, 16)
                wait(sync, mk.sd)
                i = sync.dma_start(out=mk.al2_sb[:, :], in_=mk.a2s[:, :])
                mk.sd.inc(i, 16)
            t_al = mk.sd.n
            self._p4_wtok = (t_w, t_al)

        xg_free = 0
        hb_free = 0
        bankA = [0, 0]
        bankAe = [None, None]
        bankP = [0, 0]
        bankPe = [None, None]
        for grp in range(ngroups):
            @block.sync
            def _(sync):
                wait(sync, mk.sd, xg_free)
                wait(sync, mk.sd)
                i = sync.dma_start(out=XG3, in_=xinv[:, :, grp * 512:(grp + 1) * 512])
                mk.sd.inc(i, 16)
            t_xg = mk.sd.n

            t_norms = [mk.sV.n]
            for tau in range(4 if depth >= 1 else 0):
                def extra_wait(eng, _t=t_xg):
                    wait(eng, mk.sd, _t)

                def out_wait(eng, _t=hb_free):
                    wait(eng, mk.sT, _t)

                t_n = self.emit_ln(block, XG3[:, :, tau * 128:(tau + 1) * 128],
                                   XNG3[:, :, tau * 128:(tau + 1) * 128],
                                   extra_wait, out_wait)
                t_norms.append(t_n)
            t_norm = t_norms[-1]

            for hfc in range(32 if depth >= 2 else 0):
                b = hfc % 2

                @block.tensor
                def _(tensor):
                    wait(tensor, mk.sV, t_norm)
                    if grp == 0 and hfc == 0 and not prep:
                        wait(tensor, mk.sg, self._p4_wtok[0])
                    if bankAe[b] is not None:
                        wait(tensor, bankAe[b], bankA[b])
                    for c in range(8):
                        i = tensor.matmul(mk.psA[b][:, 0:512], w1s3[:, c, hfc * 128:(hfc + 1) * 128],
                                          XNG3[:, c, :], start=(c == 0), stop=(c == 7))
                    mk.sT.inc(i)
                t_h = mk.sT.n

                @block.scalar
                def _(scalar):
                    wait(scalar, mk.sS)
                    wait(scalar, mk.sT, t_h)
                    if grp == 0 and hfc == 0 and not prep:
                        wait(scalar, mk.sd, self._p4_wtok[1])
                    wait(scalar, mk.sT, hb_free)
                    i = scalar.activation(out=HB[:, hfc * 512:(hfc + 1) * 512],
                                          in_=mk.psA[b][:, 0:512], func=AF.Gelu,
                                          scale=mk.al1_sb[:, hfc:hfc + 1],
                                          bias=mk.b1_sb[:, hfc:hfc + 1])
                    mk.sS.inc(i)
                bankA[b], bankAe[b] = mk.sS.n, mk.sS
            t_hb = mk.sS.n

            for dc in range(8 if depth >= 3 else 0):
                b = dc % 2

                @block.tensor
                def _(tensor):
                    wait(tensor, mk.sS, t_hb)
                    if bankPe[b] is not None:
                        wait(tensor, bankPe[b], bankP[b])
                    for hc in range(32):
                        i = tensor.matmul(mk.psP[b][:, 0:512], w2s3[:, hc, dc * 128:(dc + 1) * 128],
                                          HB[:, hc * 512:(hc + 1) * 512],
                                          start=(hc == 0), stop=(hc == 31))
                    mk.sT.inc(i)
                t_o = mk.sT.n

                @block.scalar
                def _(scalar):
                    wait(scalar, mk.sS)
                    wait(scalar, mk.sT, t_o)
                    i = scalar.mul(out=TMP[:, :], in_=mk.psP[b][:, 0:512],
                                   mul=mk.al2_sb[:, dc:dc + 1])
                    mk.sS.inc(i)
                t_tmp = mk.sS.n

                @block.vector
                def _(vector):
                    wait(vector, mk.sV)
                    wait(vector, mk.sS, t_tmp)
                    i = vector.scalar_tensor_tensor(out=XG3[:, dc, :], in0=TMP[:, :],
                                                    scalar=mk.b2_sb[:, dc:dc + 1],
                                                    in1=XG3[:, dc, :],
                                                    op0=ALU.add, op1=ALU.add)
                    mk.sV.inc(i)
                bankP[b], bankPe[b] = mk.sV.n, mk.sV
            hb_free = mk.sT.n
            t_x2 = mk.sV.n

            @block.sync
            def _(sync):
                wait(sync, mk.sV, t_x2)
                wait(sync, mk.sd)
                i = sync.dma_start(out=xoutv[:, :, grp * 512:(grp + 1) * 512], in_=XG3)
                mk.sd.inc(i, 16)
            xg_free = mk.sd.n

    def p4_prep(self, block):
        mk = self
        STG = self.vw(self.aX, 20480, 128, 4096, F32)
        ABSB = self.vw(self.aX, 36864, 128, 4096, BF16)
        AROW = self.vw(self.aX, 53248, 1, 4096, F32)
        w1s3 = self.wA().rearrange("q (c f) -> q c f", c=8)
        w2s3 = self.wB().rearrange("q (c f) -> q c f", c=32)

        @block.vector
        def _(vector):
            wait(vector, mk.sV)
            i = vector.memset(AROW[0:1, :], 0.0)
            mk.sV.inc(i)
        t_z = mk.sV.n

        stg_free = 0
        absb_free = 0
        psy_free = [0, 0]
        for c in range(8):
            @block.sync
            def _(sync):
                wait(sync, mk.sS, stg_free)
                wait(sync, mk.sd)
                i = sync.dma_start(out=STG[:, :], in_=mk.w1t[:, c * 4096:(c + 1) * 4096])
                mk.sd.inc(i, 16)
            t_stg = mk.sd.n

            @block.scalar
            def _(scalar):
                wait(scalar, mk.sS)
                wait(scalar, mk.sd, t_stg)
                scalar.activation(out=w1s3[:, c, :], in_=STG[:, :], func=AF.Sign)
                wait(scalar, mk.sT, absb_free)
                i = scalar.activation(out=ABSB[:, :], in_=STG[:, :], func=AF.Abs)
                mk.sS.inc(i)
            stg_free = mk.sS.n
            t_abs = mk.sS.n

            for sub in range(8):
                b = sub % 2

                @block.tensor
                def _(tensor):
                    wait(tensor, mk.sS, t_abs)
                    wait(tensor, mk.sV, psy_free[b])
                    i = tensor.matmul(mk.psY[0:1, b * 512:(b + 1) * 512], mk.onesb[:, 0:1],
                                      ABSB[:, sub * 512:(sub + 1) * 512], start=True, stop=True)
                    mk.sT.inc(i)
                t_mm = mk.sT.n

                @block.vector
                def _(vector):
                    wait(vector, mk.sV)
                    wait(vector, mk.sT, t_mm)
                    if c == 0 and sub < 2:
                        wait(vector, mk.sV, t_z)
                    wait(vector, mk.sV)
                    i = vector.tensor_tensor(out=AROW[0:1, sub * 512:(sub + 1) * 512],
                                             in0=AROW[0:1, sub * 512:(sub + 1) * 512],
                                             in1=mk.psY[0:1, b * 512:(b + 1) * 512], op=ALU.add)
                    mk.sV.inc(i)
                psy_free[b] = mk.sV.n
            absb_free = mk.sT.n
        t_arow = mk.sV.n

        @block.sync
        def _(sync):
            wait(sync, mk.sV, t_arow)
            wait(sync, mk.sd)
            i = sync.dma_start(out=mk.ar1s[0:1, :], in_=AROW[0:1, :])
            mk.sd.inc(i, 16)
            wait(sync, mk.sd)
            i = sync.dma_start(out=mk.al1_sb[:, :],
                               in_=mk.ar1s.rearrange("o (s p) -> (o p) s", p=128),
                               allow_slow_non_contiguous=True)
            mk.sd.inc(i, 16)
        t_a1d = mk.sd.n

        @block.vector
        def _(vector):
            wait(vector, mk.sV)
            wait(vector, mk.sd, t_a1d)
            mk.sV.inc(vector.tensor_scalar(out=mk.al1_sb[:, :], in0=mk.al1_sb[:, :],
                                           scalar1=mk.c1024c[:, 0:1], scalar2=None, op0=ALU.mult))
            i = vector.memset(AROW[0:1, 0:1024], 0.0)
            mk.sV.inc(i)
        t_a1 = mk.sV.n

        stg_free = mk.sS.n
        absb_free = mk.sT.n
        psy_free = [0, 0]
        for c in range(32):
            @block.sync
            def _(sync):
                wait(sync, mk.sS, stg_free)
                wait(sync, mk.sd)
                i = sync.dma_start(out=STG[:, 0:1024], in_=mk.w2t[:, c * 1024:(c + 1) * 1024])
                mk.sd.inc(i, 16)
            t_stg = mk.sd.n

            @block.scalar
            def _(scalar):
                wait(scalar, mk.sS)
                wait(scalar, mk.sd, t_stg)
                scalar.activation(out=w2s3[:, c, :], in_=STG[:, 0:1024], func=AF.Sign)
                wait(scalar, mk.sT, absb_free)
                i = scalar.activation(out=ABSB[:, 0:1024], in_=STG[:, 0:1024], func=AF.Abs)
                mk.sS.inc(i)
            stg_free = mk.sS.n
            t_abs = mk.sS.n

            for sub in range(2):
                b = sub % 2

                @block.tensor
                def _(tensor):
                    wait(tensor, mk.sS, t_abs)
                    wait(tensor, mk.sV, psy_free[b])
                    i = tensor.matmul(mk.psY[0:1, b * 512:(b + 1) * 512], mk.onesb[:, 0:1],
                                      ABSB[:, sub * 512:(sub + 1) * 512], start=True, stop=True)
                    mk.sT.inc(i)
                t_mm = mk.sT.n

                @block.vector
                def _(vector):
                    wait(vector, mk.sV)
                    wait(vector, mk.sT, t_mm)
                    if c == 0 and sub < 2:
                        wait(vector, mk.sV, t_a1)
                    wait(vector, mk.sV)
                    i = vector.tensor_tensor(out=AROW[0:1, sub * 512:(sub + 1) * 512],
                                             in0=AROW[0:1, sub * 512:(sub + 1) * 512],
                                             in1=mk.psY[0:1, b * 512:(b + 1) * 512], op=ALU.add)
                    mk.sV.inc(i)
                psy_free[b] = mk.sV.n
            absb_free = mk.sT.n
        t_arow2 = mk.sV.n

        @block.sync
        def _(sync):
            wait(sync, mk.sV, t_arow2)
            wait(sync, mk.sd)
            i = sync.dma_start(out=mk.ar2s[0:1, :], in_=AROW[0:1, 0:1024])
            mk.sd.inc(i, 16)
            wait(sync, mk.sd)
            i = sync.dma_start(out=mk.al2_sb[:, :],
                               in_=mk.ar2s.rearrange("o (s p) -> (o p) s", p=128),
                               allow_slow_non_contiguous=True)
            mk.sd.inc(i, 16)
        t_a2d = mk.sd.n

        @block.vector
        def _(vector):
            wait(vector, mk.sV)
            wait(vector, mk.sd, t_a2d)
            i = vector.tensor_scalar(out=mk.al2_sb[:, :], in0=mk.al2_sb[:, :],
                                     scalar1=mk.c4096c[:, 0:1], scalar2=None, op0=ALU.mult)
            mk.sV.inc(i)
        t_a2 = mk.sV.n

        @block.sync
        def _(sync):
            wait(sync, mk.sS, mk.sS.n)
            for i_ in range(4):
                wait(sync, mk.sd)
                i = sync.dma_start(out=mk.w1ss[:, i_ * 8192:(i_ + 1) * 8192],
                                   in_=self.wA()[:, i_ * 8192:(i_ + 1) * 8192])
                mk.sd.inc(i, 16)
            for i_ in range(4):
                wait(sync, mk.sd)
                i = sync.dma_start(out=mk.w2ss[:, i_ * 8192:(i_ + 1) * 8192],
                                   in_=self.wB()[:, i_ * 8192:(i_ + 1) * 8192])
                mk.sd.inc(i, 16)
            wait(sync, mk.sV, t_a2)
            wait(sync, mk.sd)
            i = sync.dma_start(out=mk.a1s[:, :], in_=mk.al1_sb[:, :])
            mk.sd.inc(i, 16)
            wait(sync, mk.sd)
            i = sync.dma_start(out=mk.a2s[:, :], in_=mk.al2_sb[:, :])
            mk.sd.inc(i, 16)

    def p5a(self, block):
        mk = self
        XG = self.vw(self.aX, 0, 128, 4096, F32)
        XG3 = XG.rearrange("q (c p) -> q c p", c=8)
        XNG = self.vw(self.aX, 16384, 128, 4096, BF16)
        XNG3 = XNG.rearrange("q (c p) -> q c p", c=8)
        WSM = self.vw(self.aX, 24576, 128, 8192, BF16)
        WSM3 = WSM.rearrange("q (c f) -> q c f", c=8)
        QTG = self.vw(self.aX, 40960, 128, 4096, BF16)
        QTG3 = QTG.rearrange("q (f p) -> q f p", f=8)
        x2v = mk.x2s.rearrange("(c p) s -> p c s", p=128)
        xn1v = mk.xn1s.rearrange("(c p) s -> p c s", p=128)
        kT3 = self.wA().rearrange("q (f s) -> q f s", f=8)
        vb3 = self.wB().rearrange("q (t f) -> q t f", t=32)

        t_w = self.load_w(block, WSM, mk.wqb, 2)
        xg_rd_v = 0
        xg_rd_s = 0
        xng_mm = 0
        xng_sp = 0
        qtg_sp = 0
        bankA = [0, 0]
        bankAe = [None, None]
        for grp in range(8):
            @block.sync
            def _(sync):
                wait(sync, mk.sV, xg_rd_v)
                wait(sync, mk.sS, xg_rd_s)
                wait(sync, mk.sd)
                i = sync.dma_start(out=XG3, in_=x2v[:, :, grp * 512:(grp + 1) * 512])
                mk.sd.inc(i, 16)
            t_xg = mk.sd.n

            for tau in range(4):
                def extra_wait(eng, _t=t_xg):
                    wait(eng, mk.sd, _t)

                def out_wait(eng, _a=xng_mm, _b=xng_sp):
                    wait(eng, mk.sT, _a)
                    wait(eng, mk.sd, _b)

                t_n = self.emit_ln(block, XG3[:, :, tau * 128:(tau + 1) * 128],
                                   XNG3[:, :, tau * 128:(tau + 1) * 128],
                                   extra_wait, out_wait)
            t_norm = t_n
            xg_rd_v = t_norm
            xg_rd_s = mk.sS.n

            @block.sync
            def _(sync):
                wait(sync, mk.sV, t_norm)
                wait(sync, mk.sd)
                i = sync.dma_start(out=xn1v[:, :, grp * 512:(grp + 1) * 512], in_=XNG3)
                mk.sd.inc(i, 16)
            xng_sp = mk.sd.n

            for fc in range(8):
                b = fc % 2

                @block.tensor
                def _(tensor):
                    wait(tensor, mk.sV, t_norm)
                    if grp == 0 and fc == 0:
                        wait(tensor, mk.sg, t_w)
                    if bankAe[b] is not None:
                        wait(tensor, bankAe[b], bankA[b])
                    for c in range(8):
                        i = tensor.matmul(mk.psA[b][:, 0:512], WSM3[:, c, fc * 128:(fc + 1) * 128],
                                          XNG3[:, c, :], start=(c == 0), stop=(c == 7))
                    mk.sT.inc(i)
                t_p = mk.sT.n

                if fc % 2 == 0:
                    @block.vector
                    def _(vector):
                        wait(vector, mk.sV)
                        wait(vector, mk.sT, t_p)
                        wait(vector, mk.sd, qtg_sp)
                        i = vector.tensor_copy(out=QTG[:, fc * 512:(fc + 1) * 512],
                                               in_=mk.psA[b][:, 0:512])
                        mk.sV.inc(i)
                    bankA[b], bankAe[b] = mk.sV.n, mk.sV
                else:
                    @block.vector
                    def _(vector):
                        wait(vector, mk.sV)
                        wait(vector, mk.sT, t_p)
                        wait(vector, mk.sd, qtg_sp)
                        i = vector.tensor_copy(out=QTG[:, fc * 512:(fc + 1) * 512],
                                        in_=mk.psA[b][:, 0:512])
                        mk.sV.inc(i)
                    bankA[b], bankAe[b] = mk.sV.n, mk.sV
            xng_mm = mk.sT.n

            @block.sync
            def _(sync):
                wait(sync, mk.sV, mk.sV.n)
                wait(sync, mk.sS, mk.sS.n)
                wait(sync, mk.sd)
                i = sync.dma_start(out=mk.qts.rearrange("r (f s) -> r f s", f=8)[:, :, grp * 512:(grp + 1) * 512],
                                   in_=QTG3)
                mk.sd.inc(i, 16)
            qtg_sp = mk.sd.n
        self.barrier(block)

        t_w = self.load_w(block, WSM, mk.wkb, 2)
        xng_free = 0
        xng_ld = 0
        bankA = [0, 0]
        bankAe = [None, None]
        for grp in range(8):
            @block.sync
            def _(sync):
                wait(sync, mk.sT, xng_free)
                wait(sync, mk.sd)
                i = sync.dma_start(out=XNG3, in_=xn1v[:, :, grp * 512:(grp + 1) * 512])
                mk.sd.inc(i, 16)
            xng_ld = mk.sd.n

            for fc in range(8):
                b = fc % 2

                @block.tensor
                def _(tensor):
                    wait(tensor, mk.sd, xng_ld)
                    if grp == 0 and fc == 0:
                        wait(tensor, mk.sg, t_w)
                    if bankAe[b] is not None:
                        wait(tensor, bankAe[b], bankA[b])
                    for c in range(8):
                        i = tensor.matmul(mk.psA[b][:, 0:512], WSM3[:, c, fc * 128:(fc + 1) * 128],
                                          XNG3[:, c, :], start=(c == 0), stop=(c == 7))
                    mk.sT.inc(i)
                t_p = mk.sT.n

                if fc % 2 == 0:
                    @block.vector
                    def _(vector):
                        wait(vector, mk.sV)
                        wait(vector, mk.sT, t_p)
                        i = vector.tensor_copy(out=kT3[:, fc, grp * 512:(grp + 1) * 512],
                                               in_=mk.psA[b][:, 0:512])
                        mk.sV.inc(i)
                    bankA[b], bankAe[b] = mk.sV.n, mk.sV
                else:
                    @block.vector
                    def _(vector):
                        wait(vector, mk.sV)
                        wait(vector, mk.sT, t_p)
                        i = vector.tensor_copy(out=kT3[:, fc, grp * 512:(grp + 1) * 512],
                                        in_=mk.psA[b][:, 0:512])
                        mk.sV.inc(i)
                    bankA[b], bankAe[b] = mk.sV.n, mk.sV
            xng_free = mk.sT.n
        self.barrier(block)

        t_w = self.load_w(block, WSM, mk.wvb, 2)
        bankP = [0, 0]
        bankPe = [None, None]
        for grp in range(8):
            @block.sync
            def _(sync):
                wait(sync, mk.sT, xng_free)
                wait(sync, mk.sd)
                i = sync.dma_start(out=XNG3, in_=xn1v[:, :, grp * 512:(grp + 1) * 512])
                mk.sd.inc(i, 16)
            xng_ld = mk.sd.n

            for tau in range(4):
                tt = grp * 4 + tau
                for half in range(2):
                    b = half

                    @block.tensor
                    def _(tensor):
                        wait(tensor, mk.sd, xng_ld)
                        if grp == 0 and tau == 0 and half == 0:
                            wait(tensor, mk.sg, t_w)
                        if bankPe[b] is not None:
                            wait(tensor, bankPe[b], bankP[b])
                        for c in range(8):
                            i = tensor.matmul(mk.psP[b][:, 0:512],
                                              XNG3[:, c, tau * 128:(tau + 1) * 128],
                                              WSM3[:, c, half * 512:(half + 1) * 512],
                                              start=(c == 0), stop=(c == 7))
                        mk.sT.inc(i)
                    t_p = mk.sT.n

                    if half == 0:
                        @block.vector
                        def _(vector):
                            wait(vector, mk.sV)
                            wait(vector, mk.sT, t_p)
                            i = vector.tensor_copy(out=vb3[:, tt, 0:512], in_=mk.psP[b][:, 0:512])
                            mk.sV.inc(i)
                        bankP[b], bankPe[b] = mk.sV.n, mk.sV
                    else:
                        @block.vector
                        def _(vector):
                            wait(vector, mk.sV)
                            wait(vector, mk.sT, t_p)
                            i = vector.tensor_copy(out=vb3[:, tt, 512:1024], in_=mk.psP[b][:, 0:512])
                            mk.sV.inc(i)
                        bankP[b], bankPe[b] = mk.sV.n, mk.sV
            xng_free = mk.sT.n

    def p5b(self, block):
        mk = self
        WOB = self.vw(self.aX, 0, 128, 8192, BF16)
        WOB3 = WOB.rearrange("q (c f) -> q c f", c=8)
        QT = [self.vw(self.aX, 16384, 128, 1024, BF16),
              self.vw(self.aX, 18432, 128, 1024, BF16)]
        SCO = self.vw(self.aX, 20480, 128, 4096, BF16)
        E = self.vw(self.aX, 28672, 128, 4096, BF16)
        OUTB = self.vw(self.aX, 36864, 128, 1024, BF16)
        OUTTG = self.vw(self.aX, 38912, 128, 4096, BF16)
        XTG = self.vw(self.aX, 47104, 128, 4096, F32)
        XTG4 = XTG.rearrange("q (t c p) -> q t c p", t=4, c=8)
        kT3 = self.wA().rearrange("q (f s) -> q f s", f=8)
        vb3 = self.wB().rearrange("q (t f) -> q t f", t=32)
        x2v = mk.x2s.rearrange("(c p) s -> p c s", p=128)
        x3v = mk.x3s.rearrange("(c p) s -> p c s", p=128)
        qtsv = mk.qts.rearrange("r (f s) -> r f s", f=8)

        t_w = self.load_w(block, WOB, mk.wob, 2)
        qt_free = [0, 0]
        xtg_sp = 0
        xtg_res = 0
        bankA = [[0, 0], [0, 0]]
        sco_free = 0
        e_free = 0
        ent_mm = 0
        psT_free = [0, 0]
        psT_eng = [None, None]
        psy_evac = 0
        outb_tp = 0
        outtg_mm = 0
        bankP = [0, 0]
        bankPe = [None, None]
        for qg in range(8):
            @block.sync
            def _(sync):
                wait(sync, mk.sd, xtg_sp)
                wait(sync, mk.sV, xtg_res)
                for tau4 in range(4):
                    wait(sync, mk.sd)
                    i = sync.dma_start(out=XTG4[:, tau4],
                                       in_=x2v[:, :, (qg * 4 + tau4) * 128:(qg * 4 + tau4 + 1) * 128])
                    mk.sd.inc(i, 16)
            t_xtg = mk.sd.n

            for tau in range(4):
                qi = qg * 4 + tau
                pi = tau % 2
                kp = (qi + 1) * 128
                nkc = (kp + 511) // 512

                @block.sync
                def _(sync):
                    wait(sync, mk.sT, qt_free[pi])
                    wait(sync, mk.sd)
                    i = sync.dma_start(out=QT[pi].rearrange("q (f p) -> q f p", f=8),
                                       in_=qtsv[:, :, qi * 128:(qi + 1) * 128])
                    mk.sd.inc(i, 16)
                t_qt = mk.sd.n

                QT3 = QT[pi].rearrange("q (f p) -> q f p", f=8)
                for kc in range(nkc):
                    b = kc % 2
                    klen = min(512, kp - kc * 512)

                    @block.tensor
                    def _(tensor):
                        wait(tensor, mk.sd, t_qt)
                        if qg == 0 and tau == 0 and kc == 0:
                            wait(tensor, mk.sg, t_w)
                        wait(tensor, mk.sV, bankA[b][0])
                        wait(tensor, mk.sS, bankA[b][1])
                        for fc in range(8):
                            i = tensor.matmul(mk.psA[b][:, 0:klen], QT3[:, fc, :],
                                              kT3[:, fc, kc * 512:kc * 512 + klen],
                                              start=(fc == 0), stop=(fc == 7))
                        mk.sT.inc(i)
                    t_sc = mk.sT.n

                    @block.vector
                    def _(vector):
                        wait(vector, mk.sV)
                        wait(vector, mk.sT, t_sc)
                        wait(vector, mk.sT, ent_mm)
                        i = vector.tensor_copy(out=SCO[:, kc * 512:kc * 512 + klen],
                                        in_=mk.psA[b][:, 0:klen])
                        mk.sV.inc(i)
                    t_cp = mk.sS.n

                    @block.vector
                    def _(vector):
                        wait(vector, mk.sV)
                        wait(vector, mk.sT, t_sc)
                        i = vector.reduce_max(out=mk.negm[:, kc:kc + 1], in_=mk.psA[b][:, 0:klen],
                                              axis=AX.X, negate=True)
                        mk.sV.inc(i)
                    bankA[b] = [mk.sV.n, t_cp]

                @block.vector
                def _(vector):
                    wait(vector, mk.sV)
                    wait(vector, mk.sS, mk.sS.n)
                    wait(vector, mk.sV)
                    mk.sV.inc(vector.tensor_tensor(out=SCO[:, qi * 128:qi * 128 + 128],
                                                   in0=SCO[:, qi * 128:qi * 128 + 128],
                                                   in1=mk.cmaskb[:, :], op=ALU.add))
                    mk.sV.inc(vector.tensor_reduce(out=mk.nm1[:, 0:1], in_=mk.negm[:, 0:nkc],
                                                   axis=AX.X, op=ALU.min))
                    wait(vector, mk.sV)
                    i = vector.tensor_mul(out=mk.ebias[:, 0:1], in0=mk.nm1[:, 0:1],
                                          in1=mk.c32[:, 0:1])
                    mk.sV.inc(i)
                t_msk = mk.sV.n

                @block.scalar
                def _(scalar):
                    wait(scalar, mk.sS)
                    wait(scalar, mk.sV, t_msk)
                    wait(scalar, mk.sT, e_free)
                    i = scalar.activation(out=E[:, 0:kp], in_=SCO[:, 0:kp], func=AF.Exp,
                                          scale=mk.c32[:, 0:1], bias=mk.ebias[:, 0:1],
                                          accum_out=mk.edenom[:, 0:1])
                    mk.sS.inc(i)
                t_exp = mk.sS.n
                sco_free = t_exp

                @block.vector
                def _(vector):
                    wait(vector, mk.sV)
                    wait(vector, mk.sS, t_exp)
                    mk.sV.inc(vector.reciprocal(out=mk.erd[:, 0:1], in_=mk.edenom[:, 0:1]))
                    wait(vector, mk.sV)
                    i = vector.tensor_scalar(out=E[:, 0:kp], in0=E[:, 0:kp],
                                             scalar1=mk.erd[:, 0:1], scalar2=None, op0=ALU.mult)
                    mk.sV.inc(i)
                t_en = mk.sV.n

                nkb = qi + 1
                for w0 in range(0, nkb, 8):
                    ps_ = (w0 // 8) % 2
                    nw = min(8, nkb - w0)

                    @block.tensor
                    def _(tensor):
                        wait(tensor, mk.sV, t_en)
                        if psT_eng[ps_] is not None:
                            wait(tensor, psT_eng[ps_], psT_free[ps_])
                        for u in range(nw):
                            kb = w0 + u
                            i = tensor.transpose(mk.psT[ps_][:, u * 128:(u + 1) * 128],
                                                 E[:, kb * 128:(kb + 1) * 128],
                                                 mk.identb[:, :])
                        mk.sT.inc(i)
                    t_tw = mk.sT.n

                    if ps_ == 0:
                        @block.vector
                        def _(vector):
                            wait(vector, mk.sV)
                            wait(vector, mk.sT, t_tw)
                            wait(vector, mk.sS, sco_free)
                            wait(vector, mk.sT, ent_mm)
                            i = vector.tensor_copy(out=SCO[:, w0 * 128:(w0 + nw) * 128],
                                                   in_=mk.psT[ps_][:, 0:nw * 128])
                            mk.sV.inc(i)
                        psT_free[ps_], psT_eng[ps_] = mk.sV.n, mk.sV
                    else:
                        @block.vector
                        def _(vector):
                            wait(vector, mk.sV)
                            wait(vector, mk.sT, t_tw)
                            wait(vector, mk.sT, ent_mm)
                            i = vector.tensor_copy(out=SCO[:, w0 * 128:(w0 + nw) * 128],
                                            in_=mk.psT[ps_][:, 0:nw * 128])
                            mk.sV.inc(i)
                        psT_free[ps_], psT_eng[ps_] = mk.sV.n, mk.sV
                e_free = mk.sT.n
                t_ent_v, t_ent_s = mk.sV.n, mk.sS.n

                @block.tensor
                def _(tensor):
                    wait(tensor, mk.sV, t_ent_v)
                    wait(tensor, mk.sS, t_ent_s)
                    wait(tensor, mk.sV, psy_evac)
                    for kb in range(nkb):
                        for half in range(2):
                            i = tensor.matmul(mk.psY[:, half * 512:(half + 1) * 512],
                                              SCO[:, kb * 128:(kb + 1) * 128],
                                              vb3[:, kb, half * 512:(half + 1) * 512],
                                              start=(kb == 0), stop=(kb == nkb - 1))
                    mk.sT.inc(i)
                ent_mm = mk.sT.n

                @block.vector
                def _(vector):
                    wait(vector, mk.sV)
                    wait(vector, mk.sT, ent_mm)
                    wait(vector, mk.sT, outb_tp)
                    i = vector.tensor_copy(out=OUTB[:, :], in_=mk.psY[:, 0:1024])
                    mk.sV.inc(i)
                psy_evac = mk.sV.n
                t_outb = mk.sV.n

                for vc in range(8):
                    ps_ = vc % 2

                    @block.tensor
                    def _(tensor):
                        wait(tensor, mk.sV, t_outb)
                        if psT_eng[ps_] is not None:
                            wait(tensor, psT_eng[ps_], psT_free[ps_])
                        i = tensor.transpose(mk.psT[ps_][:, 0:128],
                                             OUTB[:, vc * 128:(vc + 1) * 128],
                                             mk.identb[:, :])
                        mk.sT.inc(i)
                    t_tp = mk.sT.n

                    if ps_ == 0:
                        @block.vector
                        def _(vector):
                            wait(vector, mk.sV)
                            wait(vector, mk.sT, t_tp)
                            wait(vector, mk.sT, outtg_mm)
                            i = vector.tensor_copy(
                                out=OUTTG[:, vc * 512 + tau * 128: vc * 512 + (tau + 1) * 128],
                                in_=mk.psT[ps_][:, 0:128])
                            mk.sV.inc(i)
                        psT_free[ps_], psT_eng[ps_] = mk.sV.n, mk.sV
                    else:
                        @block.vector
                        def _(vector):
                            wait(vector, mk.sV)
                            wait(vector, mk.sT, t_tp)
                            wait(vector, mk.sT, outtg_mm)
                            i = vector.tensor_copy(
                                out=OUTTG[:, vc * 512 + tau * 128: vc * 512 + (tau + 1) * 128],
                                in_=mk.psT[ps_][:, 0:128])
                            mk.sV.inc(i)
                        psT_free[ps_], psT_eng[ps_] = mk.sV.n, mk.sV
                outb_tp = mk.sT.n
                qt_free[pi] = mk.sT.n
            t_outtg_v, t_outtg_s = mk.sV.n, mk.sS.n

            for mc in range(8):
                b = mc % 2

                @block.tensor
                def _(tensor):
                    wait(tensor, mk.sV, t_outtg_v)
                    wait(tensor, mk.sS, t_outtg_s)
                    if bankPe[b] is not None:
                        wait(tensor, bankPe[b], bankP[b])
                    for vc in range(8):
                        i = tensor.matmul(mk.psP[b][:, 0:512], WOB3[:, vc, mc * 128:(mc + 1) * 128],
                                          OUTTG[:, vc * 512:(vc + 1) * 512],
                                          start=(vc == 0), stop=(vc == 7))
                    mk.sT.inc(i)
                t_y = mk.sT.n

                @block.vector
                def _(vector):
                    wait(vector, mk.sV)
                    wait(vector, mk.sT, t_y)
                    wait(vector, mk.sd, t_xtg)
                    ps4 = mk.psP[b][:, 0:512].rearrange("q (t p) -> q t p", t=4)
                    i = vector.tensor_tensor(out=XTG4[:, :, mc, :], in0=ps4,
                                             in1=XTG4[:, :, mc, :], op=ALU.add)
                    mk.sV.inc(i)
                bankP[b], bankPe[b] = mk.sV.n, mk.sV
            outtg_mm = mk.sT.n
            xtg_res = mk.sV.n

            @block.sync
            def _(sync):
                wait(sync, mk.sV, xtg_res)
                for tau4 in range(4):
                    wait(sync, mk.sd)
                    i = sync.dma_start(out=x3v[:, :, (qg * 4 + tau4) * 128:(qg * 4 + tau4 + 1) * 128],
                                       in_=XTG4[:, tau4])
                    mk.sd.inc(i, 16)
            xtg_sp = mk.sd.n


def build_nc(stop_after=None, dbg=None):
    from contextlib import ExitStack
    nc = bass.Bass("TRN2", target_bir_lowering=False)
    es = ExitStack()
    m = MK(nc, es, stop_after)
    m.declare()
    dbg_out = None
    if dbg is not None:
        dbg_out = nc.dram_tensor("dbg", [128, 4096], F32, kind="ExternalOutput").ap()
    with nc.Block() as block:
        m.p0(block)
        phases = []
        phases.append(("p1", lambda: m.p1(block)))
        phases.append(("p2", lambda: m.p2(block)))
        phases.append(("p3a", lambda: m.p3a(block)))
        phases.append(("p3b", lambda: m.p3b(block)))
        phases.append(("p4prep", lambda: m.p4_prep(block)))
        phases.append(("p4", lambda: m.p4(block, m.x1s, m.x2s, prep=False)))
        phases.append(("p5a", lambda: m.p5a(block)))
        phases.append(("p5b", lambda: m.p5b(block)))
        phases.append(("p6", lambda: m.p4(block, m.x3s, m.out, prep=False)))
        for name, fn in phases:
            fn()
            m.barrier(block)
            if stop_after == name:
                break

        if dbg is not None:
            src_map = {
                "xn1": (m.xn1s, BF16), "sq": (m.sqs, BF16), "sv": (m.svs, BF16),
                "x1": (m.x1s, F32), "x2": (m.x2s, F32), "x3": (m.x3s, F32),
                "qts": (m.qts, BF16), "out": (m.out, F32),
                "ksum": (None, None), "w1ss": (m.w1ss, BF16), "a1": (None, None),
            }
            if dbg == "xng2":
                XNGd = m.vw(m.aX, 16384, 128, 4096, BF16)
                HBd = m.vw(m.aX, 24576, 128, 16384, BF16)

                @block.gpsimd
                def _(gpsimd):
                    wait(gpsimd, m.sg)
                    i = gpsimd.dma_start(out=dbg_out[:, 0:2048], in_=XNGd[:, 2048:4096])
                    m.sg.inc(i, 16)
                    wait(gpsimd, m.sg)
                    i = gpsimd.dma_start(out=dbg_out[:, 2048:4096], in_=HBd[:, 2048:4096])
                    m.sg.inc(i, 16)
            elif dbg == "w1sig":
                @block.gpsimd
                def _(gpsimd):
                    wait(gpsimd, m.sg)
                    i = gpsimd.dma_start(out=dbg_out[:, 0:4096], in_=m.wA()[:, 0:4096])
                    m.sg.inc(i, 16)
            elif dbg == "xng_hb":
                XNGd = m.vw(m.aX, 16384, 128, 4096, BF16)
                HBd = m.vw(m.aX, 24576, 128, 16384, BF16)

                @block.gpsimd
                def _(gpsimd):
                    wait(gpsimd, m.sg)
                    i = gpsimd.dma_start(out=dbg_out[:, 0:2048], in_=XNGd[:, 0:2048])
                    m.sg.inc(i, 16)
                    wait(gpsimd, m.sg)
                    i = gpsimd.dma_start(out=dbg_out[:, 2048:4096], in_=HBd[:, 0:2048])
                    m.sg.inc(i, 16)
            elif dbg == "ksum":
                ks = m.vw(m.aX, 4096, 128, NT * 64, F32)

                @block.sync
                def _(sync):
                    wait(sync, m.sd)
                    i = sync.dma_start(out=dbg_out[:, 0:2048], in_=ks[:, :])
                    m.sd.inc(i, 16)
            elif dbg == "a1":
                AROWd = m.vw(m.aX, 53248, 1, 4096, F32)
                ABSBd = m.vw(m.aX, 36864, 128, 4096, BF16)

                @block.sync
                def _(sync):
                    wait(sync, m.sd)
                    i = sync.dma_start(out=dbg_out[:, 0:32], in_=m.al1_sb[:, :])
                    m.sd.inc(i, 16)
                    wait(sync, m.sd)
                    i = sync.dma_start(out=dbg_out[:, 32:40], in_=m.al2_sb[:, :])
                    m.sd.inc(i, 16)
                    wait(sync, m.sd)
                    i = sync.dma_start(out=dbg_out[0:1, 1024:2048], in_=AROWd[0:1, 0:1024])
                    m.sd.inc(i, 16)

                @block.gpsimd
                def _(gpsimd):
                    wait(gpsimd, m.sd)
                    wait(gpsimd, m.sg)
                    i = gpsimd.dma_start(out=dbg_out[:, 2048:2560], in_=ABSBd[:, 0:512])
                    m.sg.inc(i, 16)
            else:
                src, sdt = src_map[dbg]
                rows, cols = src.shape
                bounce = m.vw(m.aX, 0, 128, 4096, sdt)
                n_el = rows * cols
                srcf = src.rearrange("a b -> (a b)").rearrange("(n p f) -> n p f", p=128, f=4096)
                nch = n_el // (128 * 4096)
                nch = min(nch, 8)
                dbf = dbg_out.rearrange("p (n f) -> n p f", n=8) if nch > 1 else None

                @block.gpsimd
                def _(gpsimd):
                    for ci in range(nch):
                        wait(gpsimd, m.sg, m.sg.n)
                        wait(gpsimd, m.sd, m.sd.n)
                        wait(gpsimd, m.sg)
                        i = gpsimd.dma_start(out=bounce[:, :], in_=srcf[ci, :, :])
                        m.sg.inc(i, 16)
                        wait(gpsimd, m.sg, m.sg.n)
                        if nch > 1:
                            wait(gpsimd, m.sg)
                            i = gpsimd.dma_start(out=dbf[ci, :, 0:512], in_=bounce[:, 0:512])
                        else:
                            wait(gpsimd, m.sg)
                            i = gpsimd.dma_start(out=dbg_out[:, :], in_=bounce[:, :])
                        m.sg.inc(i, 16)

        @block.sync
        def _(sync):
            wait(sync, m.sd, m.sd.n)
            wait(sync, m.sg, m.sg.n)
    es.close()
    return nc


DPERM = np.arange(4096).reshape(64, 64).T.reshape(-1)


def prep_inputs(inputs):
    f = np.ascontiguousarray
    w = {}
    w["wqa"] = f(inputs["L0_attn_a_Wq"].reshape(8, 128, 4096).transpose(1, 0, 2).reshape(128, 32768))
    wk = inputs["L0_attn_a_Wk"][:, DPERM]
    w["wka"] = f(wk.reshape(8, 128, 4096).transpose(1, 0, 2).reshape(128, 32768))
    w["wva"] = f(inputs["L0_attn_a_Wv"].reshape(8, 128, 4096).transpose(1, 0, 2).reshape(128, 32768))
    w["woa"] = f(inputs["L0_attn_a_Wo"].reshape(32, 128, 1024).transpose(1, 0, 2).reshape(128, 32768))
    w["boa"] = f(inputs["L0_attn_a_bo"].reshape(8, 128).T)
    w["wqb"] = f(inputs["L0_attn_b_Wq"].reshape(8, 128, 1024).transpose(1, 0, 2).reshape(128, 8192))
    w["wkb"] = f(inputs["L0_attn_b_Wk"].reshape(8, 128, 1024).transpose(1, 0, 2).reshape(128, 8192))
    w["wvb"] = f(inputs["L0_attn_b_Wv"].reshape(8, 128, 1024).transpose(1, 0, 2).reshape(128, 8192))
    w["wob"] = f(inputs["L0_attn_b_Wo"].reshape(8, 128, 1024).transpose(1, 0, 2).reshape(128, 8192))
    w["w1t"] = f(inputs["L0_ffn_w1"].T.reshape(8, 128, 4096).transpose(1, 0, 2).reshape(128, 32768))
    w["w2t"] = f(inputs["L0_ffn_w2"].T.reshape(32, 128, 1024).transpose(1, 0, 2).reshape(128, 32768))
    w["b1"] = f(inputs["L0_ffn_b1"].reshape(32, 128).T)
    w["b2"] = f(inputs["L0_ffn_b2"].reshape(8, 128).T)
    w["ident"] = np.eye(128, dtype=np.float32)
    cm = np.triu(np.ones((128, 128), dtype=np.float32), 1) * (-1e30)
    w["cmask"] = cm
    return w


def kernel(**inputs):
    w = prep_inputs(inputs)
    x = inputs["L0_attn_a_x"]
    nc = build_nc()
    in_maps = []
    for c in range(8):
        m = dict(w)
        m["x0"] = np.ascontiguousarray(x[c].T)
        in_maps.append(m)
    res = run_bass_kernel_spmd(nc, in_maps, core_ids=list(range(8)))
    outs = [np.ascontiguousarray(r["out"].T) for r in res.results]
    return np.stack(outs, 0)


# version 04
# speedup vs baseline: 1.1200x; 1.1200x over previous
import numpy as np
import ml_dtypes
import concourse.bass as bass
import concourse.mybir as mybir
from concourse.bass_utils import run_bass_kernel_spmd

F32 = mybir.dt.float32
BF16 = mybir.dt.bfloat16
AF = mybir.ActivationFunctionType
ALU = mybir.AluOpType
AX = mybir.AxisListType

NT = 32
LN_LVL = 6
S = 4096
D = 1024
HID = 4096


class Sem:
    def __init__(self, nc, es, name):
        self.h = es.enter_context(nc.semaphore(name))
        self.n = 0

    def inc(self, instr, k=1):
        instr.then_inc(self.h, k)
        self.n += k
        return self.n


def wait(eng, sem, tok=None):
    if tok is None:
        tok = sem.n
    if tok > 0:
        eng.wait_ge(sem.h, tok)


class MK:
    def __init__(self, nc, es, stop_after=None):
        self.nc = nc
        self.es = es
        self.stop_after = stop_after

    def declare(self):
        nc, es = self.nc, self.es
        dram = lambda n, s, d, k="ExternalInput": nc.dram_tensor(n, s, d, kind=k).ap()
        self.x0 = dram("x0", [D, S], F32)
        self.wqa = dram("wqa", [128, 8 * HID], F32)
        self.wka = dram("wka", [128, 8 * HID], F32)
        self.wva = dram("wva", [128, 8 * HID], F32)
        self.woa = dram("woa", [128, 32 * D], F32)
        self.boa = dram("boa", [128, 8], F32)
        self.wqb = dram("wqb", [128, 8 * D], F32)
        self.wkb = dram("wkb", [128, 8 * D], F32)
        self.wvb = dram("wvb", [128, 8 * D], F32)
        self.wob = dram("wob", [128, 8 * D], F32)
        self.w1t = dram("w1t", [128, 8 * HID], F32)
        self.w2t = dram("w2t", [128, 32 * D], F32)
        self.b1 = dram("b1", [128, 32], F32)
        self.b2 = dram("b2", [128, 8], F32)
        self.ident = dram("ident", [128, 128], F32)
        self.cmask = dram("cmask", [128, 128], F32)
        self.out = dram("out", [D, S], F32, "ExternalOutput")

        scr = lambda n, s, d: nc.dram_tensor(n, s, d, kind="Internal").ap()
        self.xn1s = scr("xn1s", [D, S], BF16)
        self.sqs = scr("sqs", [64, NT * 8192], BF16)
        self.svs = scr("svs", [64, NT * 8192], BF16)
        self.x1s = scr("x1s", [D, S], F32)
        self.x2s = scr("x2s", [D, S], F32)
        self.x3s = scr("x3s", [D, S], F32)
        self.qts = scr("qts", [128, 8 * S], BF16)
        self.w1ss = scr("w1ss", [128, 8 * HID], BF16)
        self.w2ss = scr("w2ss", [128, 32 * D], BF16)
        self.ar1s = scr("ar1s", [1, 4096], F32)
        self.ar2s = scr("ar2s", [1, 1024], F32)
        self.a1s = scr("a1s", [128, 32], F32)
        self.a2s = scr("a2s", [128, 8], F32)

        sb = lambda n, s, d: es.enter_context(nc.sbuf_tensor(n, s, d))
        self.identb = sb("identb", [128, 128], BF16)
        self.identf = sb("identf", [128, 128], F32)
        self.cmaskb = sb("cmaskb", [128, 128], BF16)
        self.onesb = sb("onesb", [128, 1], BF16)
        self.onesf = sb("onesf", [1, 128], F32)
        self.c1024 = sb("c1024", [1, 1], F32)
        self.cneg = sb("cneg", [1, 1], F32)
        self.epsln = sb("epsln", [1, 1], F32)
        self.eps6 = sb("eps6", [128, 1], F32)
        self.zero128 = sb("zero128", [128, 1], F32)
        self.c32 = sb("c32", [128, 1], F32)
        self.c1024c = sb("c1024c", [128, 1], F32)
        self.c4096c = sb("c4096c", [128, 1], F32)
        self.c0125 = sb("c0125", [128, 1], F32)
        self.cm05 = sb("cm05", [1, 1], F32)
        self.bo_sb = sb("bo_sb", [128, 8], F32)
        self.b1_sb = sb("b1_sb", [128, 32], F32)
        self.b2_sb = sb("b2_sb", [128, 8], F32)
        self.al1_sb = sb("al1_sb", [128, 32], F32)
        self.al2_sb = sb("al2_sb", [128, 8], F32)
        self.qn = sb("qn", [128, 64], F32)
        self.qnr = sb("qnr", [128, 64], F32)
        self.negm = sb("negm", [128, 8], F32)
        self.nm1 = sb("nm1", [128, 1], F32)
        self.ebias = sb("ebias", [128, 1], F32)
        self.edenom = sb("edenom", [128, 1], F32)
        self.erd = sb("erd", [128, 1], F32)
        self.statrow = sb("statrow", [1, 256], F32)
        self.r_m = sb("r_m", [1, 128], F32)
        self.r_ms = sb("r_ms", [1, 128], F32)
        self.r_var = sb("r_var", [1, 128], F32)
        self.r_std = sb("r_std", [1, 128], F32)
        self.r_a = sb("r_a", [1, 128], F32)
        self.r_t = sb("r_t", [1, 128], F32)
        self.r_b = sb("r_b", [1, 128], F32)

        self.aW = sb("aW", [128, 65536], BF16)
        self.aX = sb("aX", [128, 34816], BF16)

        ps = lambda n, s, d: es.enter_context(nc.psum_tensor(n, s, d))
        self.psA = [ps("psA0", [128, 512], F32), ps("psA1", [128, 512], F32)]
        self.psT = [ps("psT0", [128, 1024], BF16), ps("psT1", [128, 1024], BF16)]
        self.psP = [ps("psP0", [128, 512], F32), ps("psP1", [128, 512], F32)]
        self.psY = ps("psY", [128, 1024], F32)

        sm = lambda n: Sem(self.nc, es, n)
        self.sd = sm("sd")
        self.sg = sm("sg")
        self.sT = sm("sT")
        self.sV = sm("sV")
        self.sS = sm("sS")
        self.sy = sm("sy")
        self.sp = sm("sp")

    def vw(self, arena, off, parts, cols, dtype):
        assert off % 4 == 0
        nb = cols * (2 if dtype == BF16 else 4) // 2
        a = arena[0:parts, off // 2: off // 2 + nb]
        if dtype != BF16:
            a = a.bitcast(dtype)
        return a

    def wA(self, cols=32768, dtype=BF16, parts=128, off=0):
        return self.vw(self.aW, off, parts, cols, dtype)

    def wB(self, cols=32768, dtype=BF16, parts=128, off=0):
        return self.vw(self.aW, 65536, parts, cols, dtype) if off == 0 else \
            self.vw(self.aW, 65536 + off, parts, cols, dtype)

    def barrier(self, block):
        mk = self

        @block.tensor
        def _(tensor):
            mk.sT.inc(tensor.drain())

        @block.vector
        def _(vector):
            mk.sV.inc(vector.drain())

        @block.scalar
        def _(scalar):
            mk.sS.inc(scalar.drain())

        @block.sync
        def _(sync):
            mk.sy.inc(sync.nop())

        @block.gpsimd
        def _(gpsimd):
            mk.sp.inc(gpsimd.drain())
        t = dict(T=mk.sT.n, V=mk.sV.n, S=mk.sS.n, y=mk.sy.n, p=mk.sp.n,
                 d=mk.sd.n, g=mk.sg.n)

        @block.tensor
        def _(tensor):
            wait(tensor, mk.sV, t['V']); wait(tensor, mk.sS, t['S'])
            wait(tensor, mk.sy, t['y']); wait(tensor, mk.sp, t['p'])
            wait(tensor, mk.sd, t['d']); wait(tensor, mk.sg, t['g'])

        @block.vector
        def _(vector):
            wait(vector, mk.sV)
            wait(vector, mk.sT, t['T']); wait(vector, mk.sS, t['S'])
            wait(vector, mk.sy, t['y']); wait(vector, mk.sp, t['p'])
            wait(vector, mk.sd, t['d']); wait(vector, mk.sg, t['g'])

        @block.scalar
        def _(scalar):
            wait(scalar, mk.sS)
            wait(scalar, mk.sT, t['T']); wait(scalar, mk.sV, t['V'])
            wait(scalar, mk.sy, t['y']); wait(scalar, mk.sp, t['p'])
            wait(scalar, mk.sd, t['d']); wait(scalar, mk.sg, t['g'])

        @block.sync
        def _(sync):
            wait(sync, mk.sT, t['T']); wait(sync, mk.sV, t['V'])
            wait(sync, mk.sS, t['S']); wait(sync, mk.sp, t['p'])
            wait(sync, mk.sd, t['d']); wait(sync, mk.sg, t['g'])

        @block.gpsimd
        def _(gpsimd):
            wait(gpsimd, mk.sT, t['T']); wait(gpsimd, mk.sV, t['V'])
            wait(gpsimd, mk.sS, t['S']); wait(gpsimd, mk.sy, t['y'])
            wait(gpsimd, mk.sd, t['d']); wait(gpsimd, mk.sg, t['g'])

    def p0(self, block):
        mk = self

        @block.sync
        def _(sync):
            for dst, src in ((mk.identf, mk.ident), (mk.bo_sb, mk.boa),
                             (mk.b1_sb, mk.b1), (mk.b2_sb, mk.b2)):
                wait(sync, mk.sd)
                i = sync.dma_start(out=dst[:, :], in_=src[:, :])
                mk.sd.inc(i, 16)

        @block.gpsimd
        def _(gpsimd):
            wait(gpsimd, mk.sg)
            i = gpsimd.dma_start(out=mk.identb[:], in_=mk.ident[:, :])
            mk.sg.inc(i, 16)
            wait(gpsimd, mk.sg)
            i = gpsimd.dma_start(out=mk.cmaskb[:], in_=mk.cmask[:, :])
            mk.sg.inc(i, 16)

        @block.vector
        def _(vector):
            wait(vector, mk.sV)
            vector.memset(mk.onesb[:, 0:1], 1.0)
            vector.memset(mk.onesf[0:1, :], 1.0)
            vector.memset(mk.c1024[0:1, 0:1], 1.0 / 1024.0)
            vector.memset(mk.cneg[0:1, 0:1], -1.0)
            vector.memset(mk.epsln[0:1, 0:1], 1e-5)
            vector.memset(mk.eps6[:, 0:1], 1e-6)
            vector.memset(mk.zero128[:, 0:1], 0.0)
            vector.memset(mk.c32[:, 0:1], float(D) ** -0.5)
            vector.memset(mk.c1024c[:, 0:1], 1.0 / 1024.0)
            vector.memset(mk.c4096c[:, 0:1], 1.0 / 4096.0)
            vector.memset(mk.cm05[0:1, 0:1], -0.5)
            i = vector.memset(mk.c0125[:, 0:1], 0.125)
            mk.sV.inc(i)

    def load_w(self, block, dst, src, n_parts=4):
        mk = self
        cols = src.shape[1]
        step = cols // n_parts

        @block.gpsimd
        def _(gpsimd):
            wait(gpsimd, mk.sg)
            for i_ in range(n_parts):
                ins = gpsimd.dma_start(out=dst[:, i_ * step:(i_ + 1) * step],
                                       in_=src[:, i_ * step:(i_ + 1) * step])
                mk.sg.inc(ins, 16)
        return mk.sg.n

    def emit_ln(self, block, x3, xnb_out, extra_wait, out_wait, lnps=None):
        mk = self
        lvl = LN_LVL
        lnps = lnps if lnps is not None else mk.psY
        xbb = self.vw(self.aX, 65536, 128, 1024, BF16)
        xsqb = self.vw(self.aX, 67584, 128, 1024, BF16)
        xbb3 = xbb.rearrange("q (c p) -> q c p", c=8)
        xsq3 = xsqb.rearrange("q (c p) -> q c p", c=8)

        @block.vector
        def _(vector):
            wait(vector, mk.sV)
            extra_wait(vector)
            i = vector.tensor_copy(out=xbb3, in_=x3)
            mk.sV.inc(i)
        t_xbb = mk.sV.n

        @block.gpsimd
        def _(gpsimd):
            wait(gpsimd, mk.sp)
            wait(gpsimd, mk.sV, t_xbb)
            i = gpsimd.tensor_mul(out=xsqb[:, :], in0=xbb[:, :], in1=xbb[:, :])
            mk.sp.inc(i)
        t_xsq = mk.sp.n
        if lvl < 2:
            return mk.sV.n

        @block.tensor
        def _(tensor):
            wait(tensor, mk.sV, t_xbb)
            wait(tensor, mk.sp, t_xsq)
            xb3 = xbb.rearrange("q (c p) -> q c p", c=8)
            xs3 = xsqb.rearrange("q (c p) -> q c p", c=8)
            for c in range(8):
                tensor.matmul(lnps[0:1, 0:128], mk.onesb[:, 0:1], xb3[:, c, :],
                              start=(c == 0), stop=(c == 7))
            for c in range(8):
                i = tensor.matmul(lnps[0:1, 128:256], mk.onesb[:, 0:1], xs3[:, c, :],
                                  start=(c == 0), stop=(c == 7))
            mk.sT.inc(i)
        t_stats = mk.sT.n

        @block.vector
        def _(vector):
            wait(vector, mk.sV)
            wait(vector, mk.sT, t_stats)
            mk.sV.inc(vector.tensor_scalar(out=mk.r_m[0:1, :], in0=mk.psY[0:1, 0:128],
                                           scalar1=mk.c1024[0:1, 0:1], scalar2=None, op0=ALU.mult))
            wait(vector, mk.sV)
            mk.sV.inc(vector.tensor_mul(out=mk.r_ms[0:1, :], in0=mk.r_m[0:1, :], in1=mk.r_m[0:1, :]))
            wait(vector, mk.sV)
            i = vector.scalar_tensor_tensor(out=mk.r_var[0:1, :], in0=mk.psY[0:1, 128:256],
                                            scalar=mk.c1024[0:1, 0:1], in1=mk.r_ms[0:1, :],
                                            op0=ALU.mult, op1=ALU.subtract)
            mk.sV.inc(i)
        t_var = mk.sV.n
        if lvl < 4:
            return mk.sV.n

        @block.scalar
        def _(scalar):
            wait(scalar, mk.sS)
            wait(scalar, mk.sV, t_var)
            mk.sS.inc(scalar.activation(out=mk.r_std[0:1, :], in_=mk.r_var[0:1, :], func=AF.Ln,
                                        bias=mk.epsln[0:1, 0:1]))
            wait(scalar, mk.sS)
            i = scalar.activation(out=mk.r_a[0:1, :], in_=mk.r_std[0:1, :], func=AF.Exp,
                                  scale=mk.cm05[0:1, 0:1])
            mk.sS.inc(i)
        t_std = mk.sS.n

        @block.vector
        def _(vector):
            wait(vector, mk.sV)
            i = vector.tensor_scalar(out=mk.r_b[0:1, :], in0=mk.r_m[0:1, :],
                                     scalar1=mk.cneg[0:1, 0:1], scalar2=None, op0=ALU.mult)
            mk.sV.inc(i)
        t_ab = mk.sV.n

        @block.tensor
        def _(tensor):
            wait(tensor, mk.sV, t_ab)
            wait(tensor, mk.sS, t_std)
            tensor.matmul(lnps[:, 256:384], mk.onesf[0:1, :], mk.r_a[0:1, :],
                          start=True, stop=True)
            i = tensor.matmul(lnps[:, 384:512], mk.onesf[0:1, :], mk.r_b[0:1, :],
                              start=True, stop=True)
            mk.sT.inc(i)
        t_rep = mk.sT.n
        if lvl < 6:
            return mk.sV.n

        @block.vector
        def _(vector):
            wait(vector, mk.sV)
            wait(vector, mk.sT, t_rep)
            out_wait(vector)
            arep = lnps[:, 256:384].unsqueeze(1).broadcast_to([128, 8, 128])
            brep = lnps[:, 384:512].unsqueeze(1).broadcast_to([128, 8, 128])
            mk.sV.inc(vector.tensor_tensor(out=xnb_out, in0=x3, in1=brep, op=ALU.add))
            wait(vector, mk.sV)
            i = vector.tensor_tensor(out=xnb_out, in0=xnb_out, in1=arep, op=ALU.mult)
            mk.sV.inc(i)
        return mk.sV.n

    def p1(self, block):
        mk = self
        t_wk = self.load_w(block, self.wA(), mk.wka)
        t_wq = self.load_w(block, self.wB(), mk.wqa)
        self.t_wq = t_wq
        wk3 = self.wA().rearrange("q (c f) -> q c f", c=8)
        ksum = self.vw(self.aX, 4096, 128, NT * 64, F32)
        xts = [self.vw(self.aX, 12288, 128, 1024, F32),
               self.vw(self.aX, 16384, 128, 1024, F32)]
        xnbs = [self.vw(self.aX, 20480, 128, 1024, BF16),
                self.vw(self.aX, 22528, 128, 1024, BF16)]
        kst = self.vw(self.aX, 24576, 128, 512, F32)

        xt_free = [0, 0]
        xnb_free = [0, 0]
        xnb_spill = [0, 0]
        bankV = [0, 0, 0, 0]
        pb4 = [mk.psA[0], mk.psA[1], mk.psP[0], mk.psP[1]]
        x0v = mk.x0.rearrange("(c p) s -> p c s", p=128)
        xn1v = mk.xn1s.rearrange("(c p) s -> p c s", p=128)
        for t in range(NT):
            pi = t % 2

            @block.sync
            def _(sync):
                wait(sync, mk.sV, xt_free[pi])
                wait(sync, mk.sd)
                i = sync.dma_start(out=xts[pi].rearrange("q (c p) -> q c p", c=8),
                                   in_=x0v[:, :, t * 128:(t + 1) * 128])
                mk.sd.inc(i, 16)
            t_x = mk.sd.n

            xn_dst = xnbs[pi].rearrange("q (c p) -> q c p", c=8)

            def extra_wait(eng, _t=t_x):
                wait(eng, mk.sd, _t)

            def out_wait(eng, _a=xnb_free[pi], _b=xnb_spill[pi]):
                wait(eng, mk.sT, _a)
                wait(eng, mk.sd, _b)

            t_norm = self.emit_ln(block, xts[pi].rearrange("q (c p) -> q c p", c=8), xn_dst, extra_wait, out_wait)
            xt_free[pi] = t_norm

            @block.sync
            def _(sync):
                wait(sync, mk.sV, t_norm)
                wait(sync, mk.sd)
                i = sync.dma_start(out=xn1v[:, :, t * 128:(t + 1) * 128], in_=xn_dst)
                mk.sd.inc(i, 16)
            xnb_spill[pi] = mk.sd.n

            xnb3 = xnbs[pi].rearrange("q (c p) -> q c p", c=8)
            for nck in range(8):
                b = nck % 2

                @block.tensor
                def _(tensor):
                    if t == 0 and nck == 0:
                        wait(tensor, mk.sg, t_wk)
                    wait(tensor, mk.sV, t_norm)
                    wait(tensor, mk.sV, bankV[b])
                    for c in range(8):
                        i = tensor.matmul(mk.psA[b][:, 0:512], xnb3[:, c, :],
                                          wk3[:, c, nck * 512:(nck + 1) * 512],
                                          start=(c == 0), stop=(c == 7))
                    mk.sT.inc(i)
                t_proj = mk.sT.n

                @block.vector
                def _(vector):
                    wait(vector, mk.sV)
                    wait(vector, mk.sT, t_proj)
                    mk.sV.inc(vector.tensor_scalar(out=kst[:, 0:512], in0=mk.psA[b][:, 0:512],
                                                   scalar1=mk.zero128[:, 0:1], scalar2=None, op0=ALU.max))
                    wait(vector, mk.sV)
                    i = vector.reduce_sum(out=ksum[:, t * 64 + nck * 8: t * 64 + nck * 8 + 8],
                                          in_=kst.rearrange("q (a b) -> q a b", b=64),
                                          axis=AX.X)
                    mk.sV.inc(i)
                bankV[b] = mk.sV.n
            xnb_free[pi] = mk.sT.n

        @block.vector
        def _(vector):
            wait(vector, mk.sV)
            wait(vector, mk.sV)
            i = vector.tensor_scalar(out=ksum[:, :], in0=ksum[:, :],
                                     scalar1=mk.c0125[:, 0:1], scalar2=None, op0=ALU.mult)
            mk.sV.inc(i)
        self.t_ksum = mk.sV.n

    def p2(self, block):
        mk = self
        self.t_wv = self.load_w(block, self.wA(), mk.wva)
        wq3 = self.wB().rearrange("q (c f) -> q c f", c=8)
        ksum = self.vw(self.aX, 4096, 128, NT * 64, F32)
        xnbs = [self.vw(self.aX, 0, 128, 1024, BF16),
                self.vw(self.aX, 2048, 128, 1024, BF16)]
        qp = self.vw(self.aX, 12288, 128, 4096, F32)
        qb = self.vw(self.aX, 28672, 128, 4096, BF16)
        sqt = [self.vw(self.aX, 36864, 64, 8192, BF16),
               self.vw(self.aX, 53248, 64, 8192, BF16)]
        xn1v = mk.xn1s.rearrange("(c p) s -> p c s", p=128)

        xnb_ld = [0, 0]
        xnb_free = [0, 0]
        bankV = [0, 0, 0, 0]
        pb4 = [mk.psA[0], mk.psA[1], mk.psP[0], mk.psP[1]]
        t_qb_prev = 0
        qb_mm = 0
        psT_free = [0, 0]
        psT_eng = [None, None]
        sq_spill = [0, 0]
        for t in range(NT + 1):
            pi = t % 2
            if t < NT:
                @block.sync
                def _(sync):
                    wait(sync, mk.sT, xnb_free[pi])
                    wait(sync, mk.sd)
                    i = sync.dma_start(out=xnbs[pi].rearrange("q (c p) -> q c p", c=8),
                                       in_=xn1v[:, :, t * 128:(t + 1) * 128])
                    mk.sd.inc(i, 16)
                xnb_ld[pi] = mk.sd.n

                xnb3 = xnbs[pi].rearrange("q (c p) -> q c p", c=8)
                for nck in range(8):
                    b = nck % 4

                    @block.tensor
                    def _(tensor):
                        if nck == 0:
                            wait(tensor, mk.sd, xnb_ld[pi])
                        if t == 0 and nck == 0:
                            wait(tensor, mk.sg, self.t_wq)
                            wait(tensor, mk.sV, self.t_ksum)
                        wait(tensor, mk.sV, bankV[b])
                        for c in range(8):
                            i = tensor.matmul(pb4[b][:, 0:512], xnb3[:, c, :],
                                              wq3[:, c, nck * 512:(nck + 1) * 512],
                                              start=(c == 0), stop=(c == 7))
                        mk.sT.inc(i)
                    t_proj = mk.sT.n

                    @block.vector
                    def _(vector):
                        wait(vector, mk.sV)
                        wait(vector, mk.sT, t_proj)
                        ks = ksum[:, t * 64:(t + 1) * 64].unsqueeze(1).broadcast_to([128, 8, 64])
                        mk.sV.inc(vector.scalar_tensor_tensor(
                            out=qp[:, nck * 512:(nck + 1) * 512].rearrange("q (a b) -> q a b", b=64),
                            in0=pb4[b][:, 0:512].rearrange("q (a b) -> q a b", b=64),
                            scalar=mk.zero128[:, 0:1], in1=ks, op0=ALU.max, op1=ALU.mult))
                        wait(vector, mk.sV)
                        i = vector.reduce_sum(out=mk.qn[:, nck * 8:(nck + 1) * 8],
                                              in_=qp[:, nck * 512:(nck + 1) * 512]
                                              .rearrange("q (a b) -> q a b", b=64),
                                              axis=AX.X)
                        mk.sV.inc(i)
                    bankV[b] = mk.sV.n
                xnb_free[pi] = mk.sT.n


            if t > 0:
                tp = t - 1
                pp = tp % 2
                for w in range(8):
                    ps_ = w % 2

                    @block.tensor
                    def _(tensor):
                        if w == 0:
                            wait(tensor, mk.sV, t_qb_prev)
                        if psT_eng[ps_] is not None:
                            wait(tensor, psT_eng[ps_], psT_free[ps_])
                        for u in range(8):
                            i = tensor.transpose(mk.psT[ps_][0:64, u * 128:(u + 1) * 128],
                                                 qb[:, (w * 8 + u) * 64:(w * 8 + u + 1) * 64],
                                                 mk.identb[:, :])
                        mk.sT.inc(i)
                    t_w = mk.sT.n

                    @block.scalar
                    def _(scalar):
                        wait(scalar, mk.sS)
                        wait(scalar, mk.sT, t_w)
                        wait(scalar, mk.sd, sq_spill[pp])
                        i = scalar.copy(out=sqt[pp][:, w * 1024:(w + 1) * 1024],
                                        in_=mk.psT[ps_][0:64, :])
                        mk.sS.inc(i)
                    psT_free[ps_], psT_eng[ps_] = mk.sS.n, mk.sS
                qb_mm = mk.sT.n

                @block.sync
                def _(sync):
                    wait(sync, mk.sS, mk.sS.n)
                    wait(sync, mk.sd)
                    i = sync.dma_start(out=mk.sqs[:, tp * 8192:(tp + 1) * 8192],
                                       in_=sqt[pp][:, :])
                    mk.sd.inc(i, 16)
                sq_spill[pp] = mk.sd.n

            if t < NT:
                @block.vector
                def _(vector):
                    wait(vector, mk.sV)
                    mk.sV.inc(vector.tensor_scalar(out=mk.qnr[:, :], in0=mk.qn[:, :],
                                                   scalar1=mk.eps6[:, 0:1], scalar2=None, op0=ALU.add))
                    wait(vector, mk.sV)
                    mk.sV.inc(vector.reciprocal(out=mk.qnr[:, :], in_=mk.qnr[:, :]))
                    wait(vector, mk.sV)
                    wait(vector, mk.sT, qb_mm)
                    qr = mk.qnr[:, :].unsqueeze(2).broadcast_to([128, 64, 64])
                    i = vector.tensor_tensor(out=qb.rearrange("q (a b) -> q a b", b=64),
                                             in0=qp.rearrange("q (a b) -> q a b", b=64),
                                             in1=qr, op=ALU.mult)
                    mk.sV.inc(i)
                t_qb_prev = mk.sV.n

    def p3a(self, block):
        mk = self
        self.t_wo = self.load_w(block, self.wB(), mk.woa)
        wv3 = self.wA().rearrange("q (c f) -> q c f", c=8)
        entry_sV, entry_sS = mk.sV.n, mk.sS.n
        xnbs = [self.vw(self.aX, 0, 128, 1024, BF16),
                self.vw(self.aX, 2048, 128, 1024, BF16)]
        vb = self.vw(self.aX, 28672, 128, 4096, BF16)
        svt = [self.vw(self.aX, 36864, 64, 8192, BF16),
               self.vw(self.aX, 53248, 64, 8192, BF16)]
        xn1v = mk.xn1s.rearrange("(c p) s -> p c s", p=128)

        xnb_ld = [0, 0]
        xnb_free = [0, 0]
        bankE = [0, 0, 0, 0]
        bankEng = [None, None, None, None]
        pb4 = [mk.psA[0], mk.psA[1], mk.psP[0], mk.psP[1]]
        vb_done_v = 0
        vb_done_s = 0
        vb_mm = 0
        psT_free = [0, 0]
        psT_eng = [None, None]
        sv_spill = [0, 0]
        for t in range(NT + 1):
            pi = t % 2

            if t > 0:
                tp = t - 1
                pp = tp % 2
                tv, ts_ = vb_done_v, vb_done_s
                for w in range(8):
                    ps_ = w % 2

                    @block.tensor
                    def _(tensor):
                        if w == 0:
                            wait(tensor, mk.sV, tv)
                            wait(tensor, mk.sS, ts_)
                        if psT_eng[ps_] is not None:
                            wait(tensor, psT_eng[ps_], psT_free[ps_])
                        for u in range(8):
                            i = tensor.transpose(mk.psT[ps_][0:64, u * 128:(u + 1) * 128],
                                                 vb[:, (w * 8 + u) * 64:(w * 8 + u + 1) * 64],
                                                 mk.identb[:, :])
                        mk.sT.inc(i)
                    t_w = mk.sT.n

                    @block.scalar
                    def _(scalar):
                        wait(scalar, mk.sS)
                        wait(scalar, mk.sT, t_w)
                        wait(scalar, mk.sd, sv_spill[pp])
                        i = scalar.copy(out=svt[pp][:, w * 1024:(w + 1) * 1024],
                                        in_=mk.psT[ps_][0:64, :])
                        mk.sS.inc(i)
                    psT_free[ps_], psT_eng[ps_] = mk.sS.n, mk.sS
                vb_mm = mk.sT.n

                @block.sync
                def _(sync):
                    wait(sync, mk.sS, mk.sS.n)
                    wait(sync, mk.sd)
                    i = sync.dma_start(out=mk.svs[:, tp * 8192:(tp + 1) * 8192],
                                       in_=svt[pp][:, :])
                    mk.sd.inc(i, 16)
                sv_spill[pp] = mk.sd.n

            if t < NT:
                @block.sync
                def _(sync):
                    wait(sync, mk.sT, xnb_free[pi])
                    wait(sync, mk.sd)
                    i = sync.dma_start(out=xnbs[pi].rearrange("q (c p) -> q c p", c=8),
                                       in_=xn1v[:, :, t * 128:(t + 1) * 128])
                    mk.sd.inc(i, 16)
                xnb_ld[pi] = mk.sd.n

                xnb3 = xnbs[pi].rearrange("q (c p) -> q c p", c=8)
                for nck in range(8):
                    b = nck % 4

                    @block.tensor
                    def _(tensor):
                        if nck == 0:
                            wait(tensor, mk.sd, xnb_ld[pi])
                        if t == 0 and nck == 0:
                            wait(tensor, mk.sg, self.t_wv)
                            wait(tensor, mk.sV, entry_sV)
                            wait(tensor, mk.sS, entry_sS)
                        if bankEng[b] is not None:
                            wait(tensor, bankEng[b], bankE[b])
                        for c in range(8):
                            i = tensor.matmul(pb4[b][:, 0:512], xnb3[:, c, :],
                                              wv3[:, c, nck * 512:(nck + 1) * 512],
                                              start=(c == 0), stop=(c == 7))
                        mk.sT.inc(i)
                    t_proj = mk.sT.n

                    if nck % 2 == 0:
                        @block.vector
                        def _(vector):
                            wait(vector, mk.sV)
                            wait(vector, mk.sT, t_proj)
                            if nck == 0:
                                wait(vector, mk.sT, vb_mm)
                            i = vector.tensor_copy(out=vb[:, nck * 512:(nck + 1) * 512],
                                                   in_=pb4[b][:, 0:512])
                            mk.sV.inc(i)
                        bankE[b], bankEng[b] = mk.sV.n, mk.sV
                    else:
                        @block.scalar
                        def _(scalar):
                            wait(scalar, mk.sS)
                            wait(scalar, mk.sT, t_proj)
                            if nck == 1:
                                wait(scalar, mk.sT, vb_mm)
                            i = scalar.copy(out=vb[:, nck * 512:(nck + 1) * 512],
                                            in_=pb4[b][:, 0:512])
                            mk.sS.inc(i)
                        bankE[b], bankEng[b] = mk.sS.n, mk.sS
                xnb_free[pi] = mk.sT.n
                vb_done_v, vb_done_s = mk.sV.n, mk.sS.n

    def p3b(self, block):
        mk = self
        entry_sV, entry_sS = mk.sV.n, mk.sS.n
        wo3 = self.wB().rearrange("q (g m) -> q g m", g=32)
        sqt = [self.wA(8192, BF16, 64, 0), self.wA(8192, BF16, 64, 16384)]
        svt = [self.wA(8192, BF16, 64, 32768), self.wA(8192, BF16, 64, 49152)]
        s2d = [self.vw(self.aX, 0, 128, 8192, BF16),
               self.vw(self.aX, 16384, 128, 8192, BF16)]
        xts = [self.vw(self.aX, 32768, 128, 1024, F32),
               self.vw(self.aX, 36864, 128, 1024, F32)]
        xos = [self.vw(self.aX, 40960, 128, 1024, F32),
               self.vw(self.aX, 45056, 128, 1024, F32)]
        yb = self.vw(self.aX, 49152, 128, 1024, F32)
        x0v = mk.x0.rearrange("(c p) s -> p c s", p=128)
        x1v = mk.x1s.rearrange("(c p) s -> p c s", p=128)

        sq_free = [0, 0]
        x_free = [0, 0]
        s2_mm = [0, 0]
        xo_spill = [0, 0]
        bankP = [0, 0]
        bankPe = [None, None]
        psy_sc = 0
        psy_tt = 0
        psa_free = [0, 0]
        psa_eng = [None, None]
        for t in range(NT):
            pi = t % 2

            @block.sync
            def _(sync):
                wait(sync, mk.sT, sq_free[pi])
                wait(sync, mk.sd)
                i = sync.dma_start(out=sqt[pi][:, :], in_=mk.sqs[:, t * 8192:(t + 1) * 8192])
                mk.sd.inc(i, 16)
                wait(sync, mk.sd)
                i = sync.dma_start(out=svt[pi][:, :], in_=mk.svs[:, t * 8192:(t + 1) * 8192])
                mk.sd.inc(i, 16)
                wait(sync, mk.sV, x_free[pi])
                wait(sync, mk.sd)
                i = sync.dma_start(out=xts[pi].rearrange("q (c p) -> q c p", c=8),
                                   in_=x0v[:, :, t * 128:(t + 1) * 128])
                mk.sd.inc(i, 16)
            t_loads = mk.sd.n

            sv3 = svt[pi].rearrange("d (j p) -> d j p", p=128)
            sq3 = sqt[pi].rearrange("d (i p) -> d i p", p=128)
            for pg in range(16):
                b = pg % 2

                @block.tensor
                def _(tensor):
                    wait(tensor, mk.sd, t_loads)
                    if t == 0 and pg < 2:
                        wait(tensor, mk.sV, entry_sV)
                        wait(tensor, mk.sS, entry_sS)
                        wait(tensor, mk.sg, self.t_wo)
                    if bankPe[b] is not None:
                        wait(tensor, bankPe[b], bankP[b])
                    for u in range(8):
                        p = pg * 8 + u
                        lhsT = sv3[:, :, p]
                        rhs = sq3[:, :, p]
                        tensor.matmul(mk.psP[b][0:64, u * 64:(u + 1) * 64], lhsT, rhs,
                                      start=True, stop=True)
                        i = tensor.matmul(mk.psP[b][64:128, u * 64:(u + 1) * 64], lhsT, rhs,
                                          start=True, stop=True, tile_position=(0, 64))
                    mk.sT.inc(i)
                t_pg = mk.sT.n

                if pg % 2 == 0:
                    @block.vector
                    def _(vector):
                        wait(vector, mk.sV)
                        wait(vector, mk.sT, t_pg)
                        wait(vector, mk.sT, s2_mm[pi])
                        i = vector.tensor_copy(out=s2d[pi][:, pg * 512:(pg + 1) * 512],
                                               in_=mk.psP[b][:, :])
                        mk.sV.inc(i)
                    bankP[b], bankPe[b] = mk.sV.n, mk.sV
                else:
                    @block.vector
                    def _(vector):
                        wait(vector, mk.sV)
                        wait(vector, mk.sT, t_pg)
                        wait(vector, mk.sT, s2_mm[pi])
                        i = vector.tensor_copy(out=s2d[pi][:, pg * 512:(pg + 1) * 512],
                                        in_=mk.psP[b][:, :])
                        mk.sV.inc(i)
                    bankP[b], bankPe[b] = mk.sV.n, mk.sV
            sq_free[pi] = mk.sT.n
            t_s2_v, t_s2_s = mk.sV.n, mk.sS.n

            s2e = s2d[pi][0:64, :].rearrange("j (p i) -> j p i", i=64)
            s2o = s2d[pi][64:128, :].rearrange("j (p i) -> j p i", i=64)
            for h in range(2):
                @block.tensor
                def _(tensor):
                    wait(tensor, mk.sV, t_s2_v)
                    wait(tensor, mk.sS, t_s2_s)
                    wait(tensor, mk.sV, psy_sc)
                    wait(tensor, mk.sV, psy_tt)
                    for ip in range(32):
                        i0, i1 = 2 * ip, 2 * ip + 1
                        tensor.matmul(mk.psY[:, 0:512], s2e[:, :, i0],
                                      wo3[0:64, i0 // 2, h * 512:(h + 1) * 512],
                                      start=(ip == 0), stop=(ip == 31))
                        i = tensor.matmul(mk.psY[:, 512:1024], s2o[:, :, i1],
                                          wo3[64:128, i1 // 2, h * 512:(h + 1) * 512],
                                          start=(ip == 0), stop=(ip == 31))
                    mk.sT.inc(i)
                t_wo_mm = mk.sT.n

                @block.vector
                def _(vector):
                    wait(vector, mk.sV)
                    wait(vector, mk.sT, t_wo_mm)
                    i = vector.tensor_copy(out=yb[:, h * 512:(h + 1) * 512], in_=mk.psY[:, 0:512])
                    mk.sV.inc(i)
                psy_sc = mk.sS.n

                @block.vector
                def _(vector):
                    wait(vector, mk.sV)
                    i = vector.tensor_tensor(out=yb[:, h * 512:(h + 1) * 512],
                                             in0=yb[:, h * 512:(h + 1) * 512],
                                             in1=mk.psY[:, 512:1024], op=ALU.add)
                    mk.sV.inc(i)
                psy_tt = mk.sV.n
            s2_mm[pi] = mk.sT.n
            t_yb = mk.sV.n

            for c in range(8):
                a = c % 2
                slot = c // 2

                @block.tensor
                def _(tensor):
                    wait(tensor, mk.sV, t_yb)
                    if psa_eng[a] is not None:
                        wait(tensor, psa_eng[a], psa_free[a])
                    i = tensor.matmul(mk.psA[a][:, slot * 128:(slot + 1) * 128],
                                      yb[:, c * 128:(c + 1) * 128], mk.identf[:, :],
                                      start=True, stop=True)
                    mk.sT.inc(i)
                t_tp = mk.sT.n

                @block.vector
                def _(vector):
                    wait(vector, mk.sV)
                    wait(vector, mk.sT, t_tp)
                    wait(vector, mk.sd, t_loads)
                    wait(vector, mk.sd, xo_spill[pi])
                    i = vector.scalar_tensor_tensor(
                        out=xos[pi][:, c * 128:(c + 1) * 128],
                        in0=mk.psA[a][:, slot * 128:(slot + 1) * 128],
                        scalar=mk.bo_sb[:, c:c + 1],
                        in1=xts[pi][:, c * 128:(c + 1) * 128],
                        op0=ALU.add, op1=ALU.add)
                    mk.sV.inc(i)
                psa_free[a], psa_eng[a] = mk.sV.n, mk.sV
            t_res = mk.sV.n
            x_free[pi] = t_res

            @block.sync
            def _(sync):
                wait(sync, mk.sV, t_res)
                wait(sync, mk.sd)
                i = sync.dma_start(out=x1v[:, :, t * 128:(t + 1) * 128],
                                   in_=xos[pi].rearrange("q (c p) -> q c p", c=8))
                mk.sd.inc(i, 16)
            xo_spill[pi] = mk.sd.n

    def p4(self, block, xin, xout, prep, ngroups=8, depth=3, no_reload=False):
        mk = self
        w1s3 = self.wA().rearrange("q (c f) -> q c f", c=8)
        w2s3 = self.wB().rearrange("q (c f) -> q c f", c=32)
        XG = self.vw(self.aX, 0, 128, 4096, F32)
        XG3 = XG.rearrange("q (c p) -> q c p", c=8)
        XNG = self.vw(self.aX, 16384, 128, 4096, BF16)
        XNG3 = XNG.rearrange("q (c p) -> q c p", c=8)
        HB = self.vw(self.aX, 24576, 128, 16384, BF16)
        TMP = self.vw(self.aX, 57344, 128, 512, F32)
        xinv = xin.rearrange("(c p) s -> p c s", p=128)
        xoutv = xout.rearrange("(c p) s -> p c s", p=128)

        if prep:
            self.p4_prep(block)
        elif no_reload:
            self._p4_wtok = (0, 0)
        else:
            @block.gpsimd
            def _(gpsimd):
                wait(gpsimd, mk.sg)
                for i_ in range(4):
                    ins = gpsimd.dma_start(out=self.wA()[:, i_ * 8192:(i_ + 1) * 8192],
                                           in_=mk.w1ss[:, i_ * 8192:(i_ + 1) * 8192])
                    mk.sg.inc(ins, 16)
                for i_ in range(4):
                    ins = gpsimd.dma_start(out=self.wB()[:, i_ * 8192:(i_ + 1) * 8192],
                                           in_=mk.w2ss[:, i_ * 8192:(i_ + 1) * 8192])
                    mk.sg.inc(ins, 16)
            t_w = mk.sg.n

            @block.sync
            def _(sync):
                wait(sync, mk.sd)
                i = sync.dma_start(out=mk.al1_sb[:, :], in_=mk.a1s[:, :])
                mk.sd.inc(i, 16)
                wait(sync, mk.sd)
                i = sync.dma_start(out=mk.al2_sb[:, :], in_=mk.a2s[:, :])
                mk.sd.inc(i, 16)
            t_al = mk.sd.n
            self._p4_wtok = (t_w, t_al)

        xg_free = 0
        hb_free = 0
        bank4 = [0, 0, 0, 0]
        bank4e = [None, None, None, None]
        pb4 = [mk.psA[0], mk.psA[1], mk.psP[0], mk.psP[1]]
        for grp in range(ngroups):
            @block.sync
            def _(sync):
                wait(sync, mk.sd, xg_free)
                wait(sync, mk.sd)
                i = sync.dma_start(out=XG3, in_=xinv[:, :, grp * 512:(grp + 1) * 512])
                mk.sd.inc(i, 16)
            t_xg = mk.sd.n

            t_norms = [mk.sV.n]
            for tau in range(4 if depth >= 1 else 0):
                def extra_wait(eng, _t=t_xg):
                    wait(eng, mk.sd, _t)

                def out_wait(eng, _t=hb_free):
                    wait(eng, mk.sT, _t)

                t_n = self.emit_ln(block, XG3[:, :, tau * 128:(tau + 1) * 128],
                                   XNG3[:, :, tau * 128:(tau + 1) * 128],
                                   extra_wait, out_wait)
                t_norms.append(t_n)
            t_norm = t_norms[-1]

            for hfc in range(32 if depth >= 2 else 0):
                b = hfc % 4

                @block.tensor
                def _(tensor):
                    if hfc == 0:
                        wait(tensor, mk.sV, t_norm)
                    if grp == 0 and hfc == 0 and not prep:
                        wait(tensor, mk.sg, self._p4_wtok[0])
                    if bank4e[b] is not None:
                        wait(tensor, bank4e[b], bank4[b])
                    for c in range(8):
                        i = tensor.matmul(pb4[b][:, 0:512], w1s3[:, c, hfc * 128:(hfc + 1) * 128],
                                          XNG3[:, c, :], start=(c == 0), stop=(c == 7))
                    mk.sT.inc(i)
                t_h = mk.sT.n

                @block.scalar
                def _(scalar):
                    wait(scalar, mk.sS)
                    wait(scalar, mk.sT, t_h)
                    if grp == 0 and hfc == 0 and not prep:
                        wait(scalar, mk.sd, self._p4_wtok[1])
                    if hfc == 0:
                        wait(scalar, mk.sT, hb_free)
                    i = scalar.activation(out=HB[:, hfc * 512:(hfc + 1) * 512],
                                          in_=pb4[b][:, 0:512], func=AF.Gelu,
                                          scale=mk.al1_sb[:, hfc:hfc + 1],
                                          bias=mk.b1_sb[:, hfc:hfc + 1])
                    mk.sS.inc(i)
                bank4[b], bank4e[b] = mk.sS.n, mk.sS
            t_hb = mk.sS.n

            for dc in range(8 if depth >= 3 else 0):
                b = dc % 4

                @block.tensor
                def _(tensor):
                    if dc == 0:
                        wait(tensor, mk.sS, t_hb)
                    if bank4e[b] is not None:
                        wait(tensor, bank4e[b], bank4[b])
                    for hc in range(32):
                        i = tensor.matmul(pb4[b][:, 0:512], w2s3[:, hc, dc * 128:(dc + 1) * 128],
                                          HB[:, hc * 512:(hc + 1) * 512],
                                          start=(hc == 0), stop=(hc == 31))
                    mk.sT.inc(i)
                t_o = mk.sT.n

                @block.vector
                def _(vector):
                    wait(vector, mk.sV)
                    wait(vector, mk.sT, t_o)
                    mk.sV.inc(vector.scalar_tensor_tensor(out=TMP[:, :], in0=pb4[b][:, 0:512],
                                                          scalar=mk.al2_sb[:, dc:dc + 1],
                                                          in1=XG3[:, dc, :],
                                                          op0=ALU.mult, op1=ALU.add))
                    wait(vector, mk.sV)
                    i = vector.tensor_scalar(out=XG3[:, dc, :], in0=TMP[:, :],
                                             scalar1=mk.b2_sb[:, dc:dc + 1], scalar2=None,
                                             op0=ALU.add)
                    mk.sV.inc(i)
                bank4[b], bank4e[b] = mk.sV.n, mk.sV
            hb_free = mk.sT.n
            t_x2 = mk.sV.n

            @block.sync
            def _(sync):
                wait(sync, mk.sV, t_x2)
                wait(sync, mk.sd)
                i = sync.dma_start(out=xoutv[:, :, grp * 512:(grp + 1) * 512], in_=XG3)
                mk.sd.inc(i, 16)
            xg_free = mk.sd.n

    def p4_prep(self, block):
        mk = self
        STG = self.vw(self.aX, 20480, 128, 4096, F32)
        ABSB = self.vw(self.aX, 36864, 128, 4096, BF16)
        AROW = self.vw(self.aX, 53248, 1, 4096, F32)
        w1s3 = self.wA().rearrange("q (c f) -> q c f", c=8)
        w2s3 = self.wB().rearrange("q (c f) -> q c f", c=32)

        @block.vector
        def _(vector):
            wait(vector, mk.sV)
            i = vector.memset(AROW[0:1, :], 0.0)
            mk.sV.inc(i)
        t_z = mk.sV.n

        stg_free = 0
        absb_free = 0
        psy_free = [0, 0]
        for c in range(8):
            @block.sync
            def _(sync):
                wait(sync, mk.sS, stg_free)
                wait(sync, mk.sd)
                i = sync.dma_start(out=STG[:, :], in_=mk.w1t[:, c * 4096:(c + 1) * 4096])
                mk.sd.inc(i, 16)
            t_stg = mk.sd.n

            @block.scalar
            def _(scalar):
                wait(scalar, mk.sS)
                wait(scalar, mk.sd, t_stg)
                scalar.activation(out=w1s3[:, c, :], in_=STG[:, :], func=AF.Sign)
                wait(scalar, mk.sT, absb_free)
                i = scalar.activation(out=ABSB[:, :], in_=STG[:, :], func=AF.Abs)
                mk.sS.inc(i)
            stg_free = mk.sS.n
            t_abs = mk.sS.n

            for sub in range(8):
                b = sub % 2

                @block.tensor
                def _(tensor):
                    wait(tensor, mk.sS, t_abs)
                    wait(tensor, mk.sV, psy_free[b])
                    i = tensor.matmul(mk.psY[0:1, b * 512:(b + 1) * 512], mk.onesb[:, 0:1],
                                      ABSB[:, sub * 512:(sub + 1) * 512], start=True, stop=True)
                    mk.sT.inc(i)
                t_mm = mk.sT.n

                @block.vector
                def _(vector):
                    wait(vector, mk.sV)
                    wait(vector, mk.sT, t_mm)
                    if c == 0 and sub < 2:
                        wait(vector, mk.sV, t_z)
                    wait(vector, mk.sV)
                    i = vector.tensor_tensor(out=AROW[0:1, sub * 512:(sub + 1) * 512],
                                             in0=AROW[0:1, sub * 512:(sub + 1) * 512],
                                             in1=mk.psY[0:1, b * 512:(b + 1) * 512], op=ALU.add)
                    mk.sV.inc(i)
                psy_free[b] = mk.sV.n
            absb_free = mk.sT.n
        t_arow = mk.sV.n

        @block.sync
        def _(sync):
            wait(sync, mk.sV, t_arow)
            wait(sync, mk.sd)
            i = sync.dma_start(out=mk.ar1s[0:1, :], in_=AROW[0:1, :])
            mk.sd.inc(i, 16)
            wait(sync, mk.sd)
            i = sync.dma_start(out=mk.al1_sb[:, :],
                               in_=mk.ar1s.rearrange("o (s p) -> (o p) s", p=128),
                               allow_slow_non_contiguous=True)
            mk.sd.inc(i, 16)
        t_a1d = mk.sd.n

        @block.vector
        def _(vector):
            wait(vector, mk.sV)
            wait(vector, mk.sd, t_a1d)
            mk.sV.inc(vector.tensor_scalar(out=mk.al1_sb[:, :], in0=mk.al1_sb[:, :],
                                           scalar1=mk.c1024c[:, 0:1], scalar2=None, op0=ALU.mult))
            i = vector.memset(AROW[0:1, 0:1024], 0.0)
            mk.sV.inc(i)
        t_a1 = mk.sV.n

        stg_free = mk.sS.n
        absb_free = mk.sT.n
        psy_free = [0, 0]
        for c in range(32):
            @block.sync
            def _(sync):
                wait(sync, mk.sS, stg_free)
                wait(sync, mk.sd)
                i = sync.dma_start(out=STG[:, 0:1024], in_=mk.w2t[:, c * 1024:(c + 1) * 1024])
                mk.sd.inc(i, 16)
            t_stg = mk.sd.n

            @block.scalar
            def _(scalar):
                wait(scalar, mk.sS)
                wait(scalar, mk.sd, t_stg)
                scalar.activation(out=w2s3[:, c, :], in_=STG[:, 0:1024], func=AF.Sign)
                wait(scalar, mk.sT, absb_free)
                i = scalar.activation(out=ABSB[:, 0:1024], in_=STG[:, 0:1024], func=AF.Abs)
                mk.sS.inc(i)
            stg_free = mk.sS.n
            t_abs = mk.sS.n

            for sub in range(2):
                b = sub % 2

                @block.tensor
                def _(tensor):
                    wait(tensor, mk.sS, t_abs)
                    wait(tensor, mk.sV, psy_free[b])
                    i = tensor.matmul(mk.psY[0:1, b * 512:(b + 1) * 512], mk.onesb[:, 0:1],
                                      ABSB[:, sub * 512:(sub + 1) * 512], start=True, stop=True)
                    mk.sT.inc(i)
                t_mm = mk.sT.n

                @block.vector
                def _(vector):
                    wait(vector, mk.sV)
                    wait(vector, mk.sT, t_mm)
                    if c == 0 and sub < 2:
                        wait(vector, mk.sV, t_a1)
                    wait(vector, mk.sV)
                    i = vector.tensor_tensor(out=AROW[0:1, sub * 512:(sub + 1) * 512],
                                             in0=AROW[0:1, sub * 512:(sub + 1) * 512],
                                             in1=mk.psY[0:1, b * 512:(b + 1) * 512], op=ALU.add)
                    mk.sV.inc(i)
                psy_free[b] = mk.sV.n
            absb_free = mk.sT.n
        t_arow2 = mk.sV.n

        @block.sync
        def _(sync):
            wait(sync, mk.sV, t_arow2)
            wait(sync, mk.sd)
            i = sync.dma_start(out=mk.ar2s[0:1, :], in_=AROW[0:1, 0:1024])
            mk.sd.inc(i, 16)
            wait(sync, mk.sd)
            i = sync.dma_start(out=mk.al2_sb[:, :],
                               in_=mk.ar2s.rearrange("o (s p) -> (o p) s", p=128),
                               allow_slow_non_contiguous=True)
            mk.sd.inc(i, 16)
        t_a2d = mk.sd.n

        @block.vector
        def _(vector):
            wait(vector, mk.sV)
            wait(vector, mk.sd, t_a2d)
            i = vector.tensor_scalar(out=mk.al2_sb[:, :], in0=mk.al2_sb[:, :],
                                     scalar1=mk.c4096c[:, 0:1], scalar2=None, op0=ALU.mult)
            mk.sV.inc(i)
        t_a2 = mk.sV.n

        @block.sync
        def _(sync):
            wait(sync, mk.sS, mk.sS.n)
            for i_ in range(4):
                wait(sync, mk.sd)
                i = sync.dma_start(out=mk.w1ss[:, i_ * 8192:(i_ + 1) * 8192],
                                   in_=self.wA()[:, i_ * 8192:(i_ + 1) * 8192])
                mk.sd.inc(i, 16)
            for i_ in range(4):
                wait(sync, mk.sd)
                i = sync.dma_start(out=mk.w2ss[:, i_ * 8192:(i_ + 1) * 8192],
                                   in_=self.wB()[:, i_ * 8192:(i_ + 1) * 8192])
                mk.sd.inc(i, 16)
            wait(sync, mk.sV, t_a2)
            wait(sync, mk.sd)
            i = sync.dma_start(out=mk.a1s[:, :], in_=mk.al1_sb[:, :])
            mk.sd.inc(i, 16)
            wait(sync, mk.sd)
            i = sync.dma_start(out=mk.a2s[:, :], in_=mk.al2_sb[:, :])
            mk.sd.inc(i, 16)

    def p5a(self, block):
        mk = self
        XG = self.vw(self.aX, 0, 128, 4096, F32)
        XG3 = XG.rearrange("q (c p) -> q c p", c=8)
        XNG = self.vw(self.aX, 16384, 128, 4096, BF16)
        XNG3 = XNG.rearrange("q (c p) -> q c p", c=8)
        WSM = self.vw(self.aX, 24576, 128, 8192, BF16)
        WSM3 = WSM.rearrange("q (c f) -> q c f", c=8)
        QTG = self.vw(self.aX, 40960, 128, 4096, BF16)
        QTG3 = QTG.rearrange("q (f p) -> q f p", f=8)
        x2v = mk.x2s.rearrange("(c p) s -> p c s", p=128)
        xn1v = mk.xn1s.rearrange("(c p) s -> p c s", p=128)
        kT3 = self.wA().rearrange("q (f s) -> q f s", f=8)
        vb3 = self.wB().rearrange("q (t f) -> q t f", t=32)

        t_w = self.load_w(block, WSM, mk.wqb, 2)
        xg_rd_v = 0
        xg_rd_s = 0
        xng_mm = 0
        xng_sp = 0
        qtg_sp = 0
        bankA = [0, 0]
        bankAe = [None, None]
        for grp in range(8):
            @block.sync
            def _(sync):
                wait(sync, mk.sV, xg_rd_v)
                wait(sync, mk.sS, xg_rd_s)
                wait(sync, mk.sd)
                i = sync.dma_start(out=XG3, in_=x2v[:, :, grp * 512:(grp + 1) * 512])
                mk.sd.inc(i, 16)
            t_xg = mk.sd.n

            for tau in range(4):
                def extra_wait(eng, _t=t_xg):
                    wait(eng, mk.sd, _t)

                def out_wait(eng, _a=xng_mm, _b=xng_sp):
                    wait(eng, mk.sT, _a)
                    wait(eng, mk.sd, _b)

                t_n = self.emit_ln(block, XG3[:, :, tau * 128:(tau + 1) * 128],
                                   XNG3[:, :, tau * 128:(tau + 1) * 128],
                                   extra_wait, out_wait)
            t_norm = t_n
            xg_rd_v = t_norm
            xg_rd_s = mk.sS.n

            @block.sync
            def _(sync):
                wait(sync, mk.sV, t_norm)
                wait(sync, mk.sd)
                i = sync.dma_start(out=xn1v[:, :, grp * 512:(grp + 1) * 512], in_=XNG3)
                mk.sd.inc(i, 16)
            xng_sp = mk.sd.n

            for fc in range(8):
                b = fc % 2

                @block.tensor
                def _(tensor):
                    wait(tensor, mk.sV, t_norm)
                    if grp == 0 and fc == 0:
                        wait(tensor, mk.sg, t_w)
                    if bankAe[b] is not None:
                        wait(tensor, bankAe[b], bankA[b])
                    for c in range(8):
                        i = tensor.matmul(mk.psA[b][:, 0:512], WSM3[:, c, fc * 128:(fc + 1) * 128],
                                          XNG3[:, c, :], start=(c == 0), stop=(c == 7))
                    mk.sT.inc(i)
                t_p = mk.sT.n

                if fc % 2 == 0:
                    @block.vector
                    def _(vector):
                        wait(vector, mk.sV)
                        wait(vector, mk.sT, t_p)
                        wait(vector, mk.sd, qtg_sp)
                        i = vector.tensor_copy(out=QTG[:, fc * 512:(fc + 1) * 512],
                                               in_=mk.psA[b][:, 0:512])
                        mk.sV.inc(i)
                    bankA[b], bankAe[b] = mk.sV.n, mk.sV
                else:
                    @block.vector
                    def _(vector):
                        wait(vector, mk.sV)
                        wait(vector, mk.sT, t_p)
                        wait(vector, mk.sd, qtg_sp)
                        i = vector.tensor_copy(out=QTG[:, fc * 512:(fc + 1) * 512],
                                        in_=mk.psA[b][:, 0:512])
                        mk.sV.inc(i)
                    bankA[b], bankAe[b] = mk.sV.n, mk.sV
            xng_mm = mk.sT.n

            @block.sync
            def _(sync):
                wait(sync, mk.sV, mk.sV.n)
                wait(sync, mk.sS, mk.sS.n)
                wait(sync, mk.sd)
                i = sync.dma_start(out=mk.qts.rearrange("r (f s) -> r f s", f=8)[:, :, grp * 512:(grp + 1) * 512],
                                   in_=QTG3)
                mk.sd.inc(i, 16)
            qtg_sp = mk.sd.n
        self.barrier(block)

        t_w = self.load_w(block, WSM, mk.wkb, 2)
        xng_free = 0
        xng_ld = 0
        bankA = [0, 0, 0, 0]
        bankAe = [None, None, None, None]
        for grp in range(8):
            @block.sync
            def _(sync):
                wait(sync, mk.sT, xng_free)
                wait(sync, mk.sd)
                i = sync.dma_start(out=XNG3, in_=xn1v[:, :, grp * 512:(grp + 1) * 512])
                mk.sd.inc(i, 16)
            xng_ld = mk.sd.n

            for fc in range(8):
                b = fc % 2

                @block.tensor
                def _(tensor):
                    wait(tensor, mk.sd, xng_ld)
                    if grp == 0 and fc == 0:
                        wait(tensor, mk.sg, t_w)
                    if bankAe[b] is not None:
                        wait(tensor, bankAe[b], bankA[b])
                    for c in range(8):
                        i = tensor.matmul(mk.psA[b][:, 0:512], WSM3[:, c, fc * 128:(fc + 1) * 128],
                                          XNG3[:, c, :], start=(c == 0), stop=(c == 7))
                    mk.sT.inc(i)
                t_p = mk.sT.n

                if fc % 2 == 0:
                    @block.vector
                    def _(vector):
                        wait(vector, mk.sV)
                        wait(vector, mk.sT, t_p)
                        i = vector.tensor_copy(out=kT3[:, fc, grp * 512:(grp + 1) * 512],
                                               in_=mk.psA[b][:, 0:512])
                        mk.sV.inc(i)
                    bankA[b], bankAe[b] = mk.sV.n, mk.sV
                else:
                    @block.vector
                    def _(vector):
                        wait(vector, mk.sV)
                        wait(vector, mk.sT, t_p)
                        i = vector.tensor_copy(out=kT3[:, fc, grp * 512:(grp + 1) * 512],
                                        in_=mk.psA[b][:, 0:512])
                        mk.sV.inc(i)
                    bankA[b], bankAe[b] = mk.sV.n, mk.sV
            xng_free = mk.sT.n
        self.barrier(block)

        t_w = self.load_w(block, WSM, mk.wvb, 2)
        bankP = [0, 0]
        bankPe = [None, None]
        for grp in range(8):
            @block.sync
            def _(sync):
                wait(sync, mk.sT, xng_free)
                wait(sync, mk.sd)
                i = sync.dma_start(out=XNG3, in_=xn1v[:, :, grp * 512:(grp + 1) * 512])
                mk.sd.inc(i, 16)
            xng_ld = mk.sd.n

            for tau in range(4):
                tt = grp * 4 + tau
                for half in range(2):
                    b = half

                    @block.tensor
                    def _(tensor):
                        wait(tensor, mk.sd, xng_ld)
                        if grp == 0 and tau == 0 and half == 0:
                            wait(tensor, mk.sg, t_w)
                        if bankPe[b] is not None:
                            wait(tensor, bankPe[b], bankP[b])
                        for c in range(8):
                            i = tensor.matmul(mk.psP[b][:, 0:512],
                                              XNG3[:, c, tau * 128:(tau + 1) * 128],
                                              WSM3[:, c, half * 512:(half + 1) * 512],
                                              start=(c == 0), stop=(c == 7))
                        mk.sT.inc(i)
                    t_p = mk.sT.n

                    if half == 0:
                        @block.vector
                        def _(vector):
                            wait(vector, mk.sV)
                            wait(vector, mk.sT, t_p)
                            i = vector.tensor_copy(out=vb3[:, tt, 0:512], in_=mk.psP[b][:, 0:512])
                            mk.sV.inc(i)
                        bankP[b], bankPe[b] = mk.sV.n, mk.sV
                    else:
                        @block.vector
                        def _(vector):
                            wait(vector, mk.sV)
                            wait(vector, mk.sT, t_p)
                            i = vector.tensor_copy(out=vb3[:, tt, 512:1024], in_=mk.psP[b][:, 0:512])
                            mk.sV.inc(i)
                        bankP[b], bankPe[b] = mk.sV.n, mk.sV
            xng_free = mk.sT.n

    def p5b(self, block):
        mk = self
        WOB = self.vw(self.aX, 0, 128, 8192, BF16)
        WOB3 = WOB.rearrange("q (c f) -> q c f", c=8)
        QT = [self.vw(self.aX, 16384, 128, 1024, BF16),
              self.vw(self.aX, 18432, 128, 1024, BF16)]
        SCO = self.vw(self.aX, 20480, 128, 4096, BF16)
        E = self.vw(self.aX, 28672, 128, 4096, BF16)
        OUTB = self.vw(self.aX, 36864, 128, 1024, BF16)
        OUTTG = self.vw(self.aX, 38912, 128, 4096, BF16)
        XTG = self.vw(self.aX, 47104, 128, 4096, F32)
        XTG4 = XTG.rearrange("q (t c p) -> q t c p", t=4, c=8)
        kT3 = self.wA().rearrange("q (f s) -> q f s", f=8)
        vb3 = self.wB().rearrange("q (t f) -> q t f", t=32)
        x2v = mk.x2s.rearrange("(c p) s -> p c s", p=128)
        x3v = mk.x3s.rearrange("(c p) s -> p c s", p=128)
        qtsv = mk.qts.rearrange("r (f s) -> r f s", f=8)

        t_w = self.load_w(block, WOB, mk.wob, 2)
        qt_free = [0, 0]
        xtg_sp = 0
        xtg_res = 0
        bankA = [[0, 0], [0, 0]]
        sco_free = 0
        e_free = 0
        ent_mm = 0
        psT_free = [0, 0]
        psT_eng = [None, None]
        psy_evac = 0
        outb_tp = 0
        outtg_mm = 0
        bankP = [0, 0]
        bankPe = [None, None]
        for qg in range(8):
            @block.sync
            def _(sync):
                wait(sync, mk.sd, xtg_sp)
                wait(sync, mk.sV, xtg_res)
                for tau4 in range(4):
                    wait(sync, mk.sd)
                    i = sync.dma_start(out=XTG4[:, tau4],
                                       in_=x2v[:, :, (qg * 4 + tau4) * 128:(qg * 4 + tau4 + 1) * 128])
                    mk.sd.inc(i, 16)
            t_xtg = mk.sd.n

            for tau in range(4):
                qi = qg * 4 + tau
                pi = tau % 2
                kp = (qi + 1) * 128
                nkc = (kp + 511) // 512

                @block.sync
                def _(sync):
                    wait(sync, mk.sT, qt_free[pi])
                    wait(sync, mk.sd)
                    i = sync.dma_start(out=QT[pi].rearrange("q (f p) -> q f p", f=8),
                                       in_=qtsv[:, :, qi * 128:(qi + 1) * 128])
                    mk.sd.inc(i, 16)
                t_qt = mk.sd.n

                QT3 = QT[pi].rearrange("q (f p) -> q f p", f=8)
                for kc in range(nkc):
                    b = kc % 2
                    klen = min(512, kp - kc * 512)

                    @block.tensor
                    def _(tensor):
                        wait(tensor, mk.sd, t_qt)
                        if qg == 0 and tau == 0 and kc == 0:
                            wait(tensor, mk.sg, t_w)
                        wait(tensor, mk.sV, bankA[b][0])
                        wait(tensor, mk.sS, bankA[b][1])
                        for fc in range(8):
                            i = tensor.matmul(mk.psA[b][:, 0:klen], QT3[:, fc, :],
                                              kT3[:, fc, kc * 512:kc * 512 + klen],
                                              start=(fc == 0), stop=(fc == 7))
                        mk.sT.inc(i)
                    t_sc = mk.sT.n

                    @block.vector
                    def _(vector):
                        wait(vector, mk.sV)
                        wait(vector, mk.sT, t_sc)
                        wait(vector, mk.sT, ent_mm)
                        i = vector.tensor_copy(out=SCO[:, kc * 512:kc * 512 + klen],
                                        in_=mk.psA[b][:, 0:klen])
                        mk.sV.inc(i)
                    t_cp = mk.sS.n

                    @block.vector
                    def _(vector):
                        wait(vector, mk.sV)
                        wait(vector, mk.sT, t_sc)
                        i = vector.reduce_max(out=mk.negm[:, kc:kc + 1], in_=mk.psA[b][:, 0:klen],
                                              axis=AX.X, negate=True)
                        mk.sV.inc(i)
                    bankA[b] = [mk.sV.n, t_cp]

                @block.vector
                def _(vector):
                    wait(vector, mk.sV)
                    wait(vector, mk.sS, mk.sS.n)
                    wait(vector, mk.sV)
                    mk.sV.inc(vector.tensor_tensor(out=SCO[:, qi * 128:qi * 128 + 128],
                                                   in0=SCO[:, qi * 128:qi * 128 + 128],
                                                   in1=mk.cmaskb[:, :], op=ALU.add))
                    mk.sV.inc(vector.tensor_reduce(out=mk.nm1[:, 0:1], in_=mk.negm[:, 0:nkc],
                                                   axis=AX.X, op=ALU.min))
                    wait(vector, mk.sV)
                    i = vector.tensor_mul(out=mk.ebias[:, 0:1], in0=mk.nm1[:, 0:1],
                                          in1=mk.c32[:, 0:1])
                    mk.sV.inc(i)
                t_msk = mk.sV.n

                @block.scalar
                def _(scalar):
                    wait(scalar, mk.sS)
                    wait(scalar, mk.sV, t_msk)
                    wait(scalar, mk.sT, e_free)
                    i = scalar.activation(out=E[:, 0:kp], in_=SCO[:, 0:kp], func=AF.Exp,
                                          scale=mk.c32[:, 0:1], bias=mk.ebias[:, 0:1],
                                          accum_out=mk.edenom[:, 0:1])
                    mk.sS.inc(i)
                t_exp = mk.sS.n
                sco_free = t_exp

                @block.vector
                def _(vector):
                    wait(vector, mk.sV)
                    wait(vector, mk.sS, t_exp)
                    mk.sV.inc(vector.reciprocal(out=mk.erd[:, 0:1], in_=mk.edenom[:, 0:1]))
                    wait(vector, mk.sV)
                    i = vector.tensor_scalar(out=E[:, 0:kp], in0=E[:, 0:kp],
                                             scalar1=mk.erd[:, 0:1], scalar2=None, op0=ALU.mult)
                    mk.sV.inc(i)
                t_en = mk.sV.n

                nkb = qi + 1
                for w0 in range(0, nkb, 8):
                    ps_ = (w0 // 8) % 2
                    nw = min(8, nkb - w0)

                    @block.tensor
                    def _(tensor):
                        wait(tensor, mk.sV, t_en)
                        if psT_eng[ps_] is not None:
                            wait(tensor, psT_eng[ps_], psT_free[ps_])
                        for u in range(nw):
                            kb = w0 + u
                            i = tensor.transpose(mk.psT[ps_][:, u * 128:(u + 1) * 128],
                                                 E[:, kb * 128:(kb + 1) * 128],
                                                 mk.identb[:, :])
                        mk.sT.inc(i)
                    t_tw = mk.sT.n

                    if ps_ == 0:
                        @block.vector
                        def _(vector):
                            wait(vector, mk.sV)
                            wait(vector, mk.sT, t_tw)
                            wait(vector, mk.sS, sco_free)
                            wait(vector, mk.sT, ent_mm)
                            i = vector.tensor_copy(out=SCO[:, w0 * 128:(w0 + nw) * 128],
                                                   in_=mk.psT[ps_][:, 0:nw * 128])
                            mk.sV.inc(i)
                        psT_free[ps_], psT_eng[ps_] = mk.sV.n, mk.sV
                    else:
                        @block.vector
                        def _(vector):
                            wait(vector, mk.sV)
                            wait(vector, mk.sT, t_tw)
                            wait(vector, mk.sT, ent_mm)
                            i = vector.tensor_copy(out=SCO[:, w0 * 128:(w0 + nw) * 128],
                                            in_=mk.psT[ps_][:, 0:nw * 128])
                            mk.sV.inc(i)
                        psT_free[ps_], psT_eng[ps_] = mk.sV.n, mk.sV
                e_free = mk.sT.n
                t_ent_v, t_ent_s = mk.sV.n, mk.sS.n

                @block.tensor
                def _(tensor):
                    wait(tensor, mk.sV, t_ent_v)
                    wait(tensor, mk.sS, t_ent_s)
                    wait(tensor, mk.sV, psy_evac)
                    for kb in range(nkb):
                        for half in range(2):
                            i = tensor.matmul(mk.psY[:, half * 512:(half + 1) * 512],
                                              SCO[:, kb * 128:(kb + 1) * 128],
                                              vb3[:, kb, half * 512:(half + 1) * 512],
                                              start=(kb == 0), stop=(kb == nkb - 1))
                    mk.sT.inc(i)
                ent_mm = mk.sT.n

                @block.vector
                def _(vector):
                    wait(vector, mk.sV)
                    wait(vector, mk.sT, ent_mm)
                    wait(vector, mk.sT, outb_tp)
                    i = vector.tensor_copy(out=OUTB[:, :], in_=mk.psY[:, 0:1024])
                    mk.sV.inc(i)
                psy_evac = mk.sV.n
                t_outb = mk.sV.n

                for vc in range(8):
                    ps_ = vc % 2

                    @block.tensor
                    def _(tensor):
                        wait(tensor, mk.sV, t_outb)
                        if psT_eng[ps_] is not None:
                            wait(tensor, psT_eng[ps_], psT_free[ps_])
                        i = tensor.transpose(mk.psT[ps_][:, 0:128],
                                             OUTB[:, vc * 128:(vc + 1) * 128],
                                             mk.identb[:, :])
                        mk.sT.inc(i)
                    t_tp = mk.sT.n

                    if ps_ == 0:
                        @block.vector
                        def _(vector):
                            wait(vector, mk.sV)
                            wait(vector, mk.sT, t_tp)
                            wait(vector, mk.sT, outtg_mm)
                            i = vector.tensor_copy(
                                out=OUTTG[:, vc * 512 + tau * 128: vc * 512 + (tau + 1) * 128],
                                in_=mk.psT[ps_][:, 0:128])
                            mk.sV.inc(i)
                        psT_free[ps_], psT_eng[ps_] = mk.sV.n, mk.sV
                    else:
                        @block.vector
                        def _(vector):
                            wait(vector, mk.sV)
                            wait(vector, mk.sT, t_tp)
                            wait(vector, mk.sT, outtg_mm)
                            i = vector.tensor_copy(
                                out=OUTTG[:, vc * 512 + tau * 128: vc * 512 + (tau + 1) * 128],
                                in_=mk.psT[ps_][:, 0:128])
                            mk.sV.inc(i)
                        psT_free[ps_], psT_eng[ps_] = mk.sV.n, mk.sV
                outb_tp = mk.sT.n
                qt_free[pi] = mk.sT.n
            t_outtg_v, t_outtg_s = mk.sV.n, mk.sS.n

            for mc in range(8):
                b = mc % 2

                @block.tensor
                def _(tensor):
                    wait(tensor, mk.sV, t_outtg_v)
                    wait(tensor, mk.sS, t_outtg_s)
                    if bankPe[b] is not None:
                        wait(tensor, bankPe[b], bankP[b])
                    for vc in range(8):
                        i = tensor.matmul(mk.psP[b][:, 0:512], WOB3[:, vc, mc * 128:(mc + 1) * 128],
                                          OUTTG[:, vc * 512:(vc + 1) * 512],
                                          start=(vc == 0), stop=(vc == 7))
                    mk.sT.inc(i)
                t_y = mk.sT.n

                @block.vector
                def _(vector):
                    wait(vector, mk.sV)
                    wait(vector, mk.sT, t_y)
                    wait(vector, mk.sd, t_xtg)
                    ps4 = mk.psP[b][:, 0:512].rearrange("q (t p) -> q t p", t=4)
                    i = vector.tensor_tensor(out=XTG4[:, :, mc, :], in0=ps4,
                                             in1=XTG4[:, :, mc, :], op=ALU.add)
                    mk.sV.inc(i)
                bankP[b], bankPe[b] = mk.sV.n, mk.sV
            outtg_mm = mk.sT.n
            xtg_res = mk.sV.n

            @block.sync
            def _(sync):
                wait(sync, mk.sV, xtg_res)
                for tau4 in range(4):
                    wait(sync, mk.sd)
                    i = sync.dma_start(out=x3v[:, :, (qg * 4 + tau4) * 128:(qg * 4 + tau4 + 1) * 128],
                                       in_=XTG4[:, tau4])
                    mk.sd.inc(i, 16)
            xtg_sp = mk.sd.n


def build_nc(stop_after=None, dbg=None):
    from contextlib import ExitStack
    nc = bass.Bass("TRN2", target_bir_lowering=False)
    es = ExitStack()
    m = MK(nc, es, stop_after)
    m.declare()
    dbg_out = None
    if dbg is not None:
        dbg_out = nc.dram_tensor("dbg", [128, 4096], F32, kind="ExternalOutput").ap()
    with nc.Block() as block:
        m.p0(block)
        phases = []
        phases.append(("p1", lambda: m.p1(block)))
        phases.append(("p2", lambda: m.p2(block)))
        phases.append(("p3a", lambda: m.p3a(block)))
        phases.append(("p3b", lambda: m.p3b(block)))
        phases.append(("p4prep", lambda: m.p4_prep(block)))
        phases.append(("p4", lambda: m.p4(block, m.x1s, m.x2s, prep=False)))
        phases.append(("p5a", lambda: m.p5a(block)))
        phases.append(("p5b", lambda: m.p5b(block)))
        phases.append(("p6", lambda: m.p4(block, m.x3s, m.out, prep=False)))
        for name, fn in phases:
            fn()
            m.barrier(block)
            if stop_after == name:
                break

        if dbg is not None:
            src_map = {
                "xn1": (m.xn1s, BF16), "sq": (m.sqs, BF16), "sv": (m.svs, BF16),
                "x1": (m.x1s, F32), "x2": (m.x2s, F32), "x3": (m.x3s, F32),
                "qts": (m.qts, BF16), "out": (m.out, F32),
                "ksum": (None, None), "w1ss": (m.w1ss, BF16), "a1": (None, None),
            }
            if dbg == "xng2":
                XNGd = m.vw(m.aX, 16384, 128, 4096, BF16)
                HBd = m.vw(m.aX, 24576, 128, 16384, BF16)

                @block.gpsimd
                def _(gpsimd):
                    wait(gpsimd, m.sg)
                    i = gpsimd.dma_start(out=dbg_out[:, 0:2048], in_=XNGd[:, 2048:4096])
                    m.sg.inc(i, 16)
                    wait(gpsimd, m.sg)
                    i = gpsimd.dma_start(out=dbg_out[:, 2048:4096], in_=HBd[:, 2048:4096])
                    m.sg.inc(i, 16)
            elif dbg == "w1sig":
                @block.gpsimd
                def _(gpsimd):
                    wait(gpsimd, m.sg)
                    i = gpsimd.dma_start(out=dbg_out[:, 0:4096], in_=m.wA()[:, 0:4096])
                    m.sg.inc(i, 16)
            elif dbg == "xng_hb":
                XNGd = m.vw(m.aX, 16384, 128, 4096, BF16)
                HBd = m.vw(m.aX, 24576, 128, 16384, BF16)

                @block.gpsimd
                def _(gpsimd):
                    wait(gpsimd, m.sg)
                    i = gpsimd.dma_start(out=dbg_out[:, 0:2048], in_=XNGd[:, 0:2048])
                    m.sg.inc(i, 16)
                    wait(gpsimd, m.sg)
                    i = gpsimd.dma_start(out=dbg_out[:, 2048:4096], in_=HBd[:, 0:2048])
                    m.sg.inc(i, 16)
            elif dbg == "ksum":
                ks = m.vw(m.aX, 4096, 128, NT * 64, F32)

                @block.sync
                def _(sync):
                    wait(sync, m.sd)
                    i = sync.dma_start(out=dbg_out[:, 0:2048], in_=ks[:, :])
                    m.sd.inc(i, 16)
            elif dbg == "a1":
                AROWd = m.vw(m.aX, 53248, 1, 4096, F32)
                ABSBd = m.vw(m.aX, 36864, 128, 4096, BF16)

                @block.sync
                def _(sync):
                    wait(sync, m.sd)
                    i = sync.dma_start(out=dbg_out[:, 0:32], in_=m.al1_sb[:, :])
                    m.sd.inc(i, 16)
                    wait(sync, m.sd)
                    i = sync.dma_start(out=dbg_out[:, 32:40], in_=m.al2_sb[:, :])
                    m.sd.inc(i, 16)
                    wait(sync, m.sd)
                    i = sync.dma_start(out=dbg_out[0:1, 1024:2048], in_=AROWd[0:1, 0:1024])
                    m.sd.inc(i, 16)

                @block.gpsimd
                def _(gpsimd):
                    wait(gpsimd, m.sd)
                    wait(gpsimd, m.sg)
                    i = gpsimd.dma_start(out=dbg_out[:, 2048:2560], in_=ABSBd[:, 0:512])
                    m.sg.inc(i, 16)
            else:
                src, sdt = src_map[dbg]
                rows, cols = src.shape
                bounce = m.vw(m.aX, 0, 128, 4096, sdt)
                n_el = rows * cols
                srcf = src.rearrange("a b -> (a b)").rearrange("(n p f) -> n p f", p=128, f=4096)
                nch = n_el // (128 * 4096)
                nch = min(nch, 8)
                dbf = dbg_out.rearrange("p (n f) -> n p f", n=8) if nch > 1 else None

                @block.gpsimd
                def _(gpsimd):
                    for ci in range(nch):
                        wait(gpsimd, m.sg, m.sg.n)
                        wait(gpsimd, m.sd, m.sd.n)
                        wait(gpsimd, m.sg)
                        i = gpsimd.dma_start(out=bounce[:, :], in_=srcf[ci, :, :])
                        m.sg.inc(i, 16)
                        wait(gpsimd, m.sg, m.sg.n)
                        if nch > 1:
                            wait(gpsimd, m.sg)
                            i = gpsimd.dma_start(out=dbf[ci, :, 0:512], in_=bounce[:, 0:512])
                        else:
                            wait(gpsimd, m.sg)
                            i = gpsimd.dma_start(out=dbg_out[:, :], in_=bounce[:, :])
                        m.sg.inc(i, 16)

        @block.sync
        def _(sync):
            wait(sync, m.sd, m.sd.n)
            wait(sync, m.sg, m.sg.n)
    es.close()
    return nc


DPERM = np.arange(4096).reshape(64, 64).T.reshape(-1)


def prep_inputs(inputs):
    f = np.ascontiguousarray
    w = {}
    w["wqa"] = f(inputs["L0_attn_a_Wq"].reshape(8, 128, 4096).transpose(1, 0, 2).reshape(128, 32768))
    wk = inputs["L0_attn_a_Wk"][:, DPERM]
    w["wka"] = f(wk.reshape(8, 128, 4096).transpose(1, 0, 2).reshape(128, 32768))
    w["wva"] = f(inputs["L0_attn_a_Wv"].reshape(8, 128, 4096).transpose(1, 0, 2).reshape(128, 32768))
    w["woa"] = f(inputs["L0_attn_a_Wo"].reshape(32, 128, 1024).transpose(1, 0, 2).reshape(128, 32768))
    w["boa"] = f(inputs["L0_attn_a_bo"].reshape(8, 128).T)
    w["wqb"] = f(inputs["L0_attn_b_Wq"].reshape(8, 128, 1024).transpose(1, 0, 2).reshape(128, 8192))
    w["wkb"] = f(inputs["L0_attn_b_Wk"].reshape(8, 128, 1024).transpose(1, 0, 2).reshape(128, 8192))
    w["wvb"] = f(inputs["L0_attn_b_Wv"].reshape(8, 128, 1024).transpose(1, 0, 2).reshape(128, 8192))
    w["wob"] = f(inputs["L0_attn_b_Wo"].reshape(8, 128, 1024).transpose(1, 0, 2).reshape(128, 8192))
    w["w1t"] = f(inputs["L0_ffn_w1"].T.reshape(8, 128, 4096).transpose(1, 0, 2).reshape(128, 32768))
    w["w2t"] = f(inputs["L0_ffn_w2"].T.reshape(32, 128, 1024).transpose(1, 0, 2).reshape(128, 32768))
    w["b1"] = f(inputs["L0_ffn_b1"].reshape(32, 128).T)
    w["b2"] = f(inputs["L0_ffn_b2"].reshape(8, 128).T)
    w["ident"] = np.eye(128, dtype=np.float32)
    cm = np.triu(np.ones((128, 128), dtype=np.float32), 1) * (-1e30)
    w["cmask"] = cm
    return w


def kernel(**inputs):
    w = prep_inputs(inputs)
    x = inputs["L0_attn_a_x"]
    nc = build_nc()
    in_maps = []
    for c in range(8):
        m = dict(w)
        m["x0"] = np.ascontiguousarray(x[c].T)
        in_maps.append(m)
    res = run_bass_kernel_spmd(nc, in_maps, core_ids=list(range(8)))
    outs = [np.ascontiguousarray(r["out"].T) for r in res.results]
    return np.stack(outs, 0)


# version 05
# speedup vs baseline: 1.1728x; 1.0471x over previous
import numpy as np
import ml_dtypes
import concourse.bass as bass
import concourse.mybir as mybir
from concourse.bass_utils import run_bass_kernel_spmd

F32 = mybir.dt.float32
BF16 = mybir.dt.bfloat16
AF = mybir.ActivationFunctionType
ALU = mybir.AluOpType
AX = mybir.AxisListType

NT = 32
LN_LVL = 6
S = 4096
D = 1024
HID = 4096


class Sem:
    def __init__(self, nc, es, name):
        self.h = es.enter_context(nc.semaphore(name))
        self.n = 0

    def inc(self, instr, k=1):
        instr.then_inc(self.h, k)
        self.n += k
        return self.n


SELF_EDGES = True
QUEUE_SERIAL = False


class QSem:

    def __init__(self, nc, es, name, n=4):
        self.sems = [Sem(nc, es, f"{name}{i}") for i in range(n)]
        self.n = n
        self.k = 0

    def pre(self, eng):
        s = self.sems[self.k % self.n]
        wait(eng, s)
        return s

    def post(self, instr):
        s = self.sems[self.k % self.n]
        s.inc(instr, 16)
        self.k += 1
        self.last = (s, s.n)
        return self.last

    def drain(self, eng):
        for s in self.sems:
            wait(eng, s)


def waith(eng, h):
    if not h:
        return
    if isinstance(h, list):
        waiths(eng, h)
    else:
        eng.wait_ge(h[0].h, h[1])


def waiths(eng, hs):
    seen = {}
    for h in hs:
        if h:
            seen[id(h[0])] = (h[0], max(h[1], seen.get(id(h[0]), (None, -1))[1]))
    for s, v in seen.values():
        eng.wait_ge(s.h, v)


def wait(eng, sem, tok=None):
    if tok is None:
        tok = sem.n
    if tok > 0:
        eng.wait_ge(sem.h, tok)


def selfwait(eng, sem):
    if SELF_EDGES:
        wait(eng, sem)


def qwait(eng, sem):
    if QUEUE_SERIAL:
        wait(eng, sem)


class MK:
    def __init__(self, nc, es, stop_after=None):
        self.nc = nc
        self.es = es
        self.stop_after = stop_after

    def declare(self):
        nc, es = self.nc, self.es
        dram = lambda n, s, d, k="ExternalInput": nc.dram_tensor(n, s, d, kind=k).ap()
        self.x0 = dram("x0", [D, S], F32)
        self.wqa = dram("wqa", [128, 8 * HID], F32)
        self.wka = dram("wka", [128, 8 * HID], F32)
        self.wva = dram("wva", [128, 8 * HID], F32)
        self.woa = dram("woa", [128, 32 * D], F32)
        self.boa = dram("boa", [128, 8], F32)
        self.wqb = dram("wqb", [128, 8 * D], F32)
        self.wkb = dram("wkb", [128, 8 * D], F32)
        self.wvb = dram("wvb", [128, 8 * D], F32)
        self.wob = dram("wob", [128, 8 * D], F32)
        self.w1t = dram("w1t", [128, 8 * HID], F32)
        self.w2t = dram("w2t", [128, 32 * D], F32)
        self.b1 = dram("b1", [128, 32], F32)
        self.b2 = dram("b2", [128, 8], F32)
        self.ident = dram("ident", [128, 128], F32)
        self.cmask = dram("cmask", [128, 128], F32)
        self.out = dram("out", [D, S], F32, "ExternalOutput")

        scr = lambda n, s, d: nc.dram_tensor(n, s, d, kind="Internal").ap()
        self.xn1s = scr("xn1s", [D, S], BF16)
        self.sqs = scr("sqs", [64, NT * 8192], BF16)
        self.svs = scr("svs", [64, NT * 8192], BF16)
        self.x1s = scr("x1s", [D, S], F32)
        self.x2s = scr("x2s", [D, S], F32)
        self.x3s = scr("x3s", [D, S], F32)
        self.qts = scr("qts", [128, 8 * S], BF16)
        self.w1ss = scr("w1ss", [128, 8 * HID], BF16)
        self.w2ss = scr("w2ss", [128, 32 * D], BF16)
        self.ar1s = scr("ar1s", [1, 4096], F32)
        self.ar2s = scr("ar2s", [1, 1024], F32)
        self.a1s = scr("a1s", [128, 32], F32)
        self.a2s = scr("a2s", [128, 8], F32)

        sb = lambda n, s, d: es.enter_context(nc.sbuf_tensor(n, s, d))
        self.identb = sb("identb", [128, 128], BF16)
        self.identf = sb("identf", [128, 128], F32)
        self.cmaskb = sb("cmaskb", [128, 128], BF16)
        self.onesb = sb("onesb", [128, 1], BF16)
        self.onesf = sb("onesf", [1, 128], F32)
        self.c1024 = sb("c1024", [1, 1], F32)
        self.cneg = sb("cneg", [1, 1], F32)
        self.epsln = sb("epsln", [1, 1], F32)
        self.eps6 = sb("eps6", [128, 1], F32)
        self.zero128 = sb("zero128", [128, 1], F32)
        self.c32 = sb("c32", [128, 1], F32)
        self.c1024c = sb("c1024c", [128, 1], F32)
        self.c4096c = sb("c4096c", [128, 1], F32)
        self.c0125 = sb("c0125", [128, 1], F32)
        self.cm05 = sb("cm05", [1, 1], F32)
        self.bo_sb = sb("bo_sb", [128, 8], F32)
        self.b1_sb = sb("b1_sb", [128, 32], F32)
        self.b2_sb = sb("b2_sb", [128, 8], F32)
        self.al1_sb = sb("al1_sb", [128, 32], F32)
        self.al2_sb = sb("al2_sb", [128, 8], F32)
        self.qn = sb("qn", [128, 64], F32)
        self.qnr = sb("qnr", [128, 64], F32)
        self.negm = sb("negm", [128, 8], F32)
        self.nm1 = sb("nm1", [128, 1], F32)
        self.ebias = sb("ebias", [128, 1], F32)
        self.edenom = sb("edenom", [128, 1], F32)
        self.erd = sb("erd", [128, 1], F32)
        self.statrow = sb("statrow", [1, 256], F32)
        self.r_m = sb("r_m", [1, 128], F32)
        self.r_ms = sb("r_ms", [1, 128], F32)
        self.r_var = sb("r_var", [1, 128], F32)
        self.r_std = sb("r_std", [1, 128], F32)
        self.r_a = sb("r_a", [1, 128], F32)
        self.r_t = sb("r_t", [1, 128], F32)
        self.r_b = sb("r_b", [1, 128], F32)

        self.aW = sb("aW", [128, 65536], BF16)
        self.aX = sb("aX", [128, 34816], BF16)

        ps = lambda n, s, d: es.enter_context(nc.psum_tensor(n, s, d))
        self.psA = [ps("psA0", [128, 512], F32), ps("psA1", [128, 512], F32)]
        self.psT = [ps("psT0", [128, 1024], BF16), ps("psT1", [128, 1024], BF16)]
        self.psP = [ps("psP0", [128, 512], F32), ps("psP1", [128, 512], F32)]
        self.psY = ps("psY", [128, 1024], F32)

        sm = lambda n: Sem(self.nc, es, n)
        self.sd = sm("sd")
        self.sg = sm("sg")
        self.qd = QSem(self.nc, es, "qd", 4)
        self.qg = QSem(self.nc, es, "qg", 2)
        self.sT = sm("sT")
        self.sV = sm("sV")
        self.sS = sm("sS")
        self.sy = sm("sy")
        self.sp = sm("sp")

    def vw(self, arena, off, parts, cols, dtype):
        assert off % 4 == 0
        nb = cols * (2 if dtype == BF16 else 4) // 2
        a = arena[0:parts, off // 2: off // 2 + nb]
        if dtype != BF16:
            a = a.bitcast(dtype)
        return a

    def wA(self, cols=32768, dtype=BF16, parts=128, off=0):
        return self.vw(self.aW, off, parts, cols, dtype)

    def wB(self, cols=32768, dtype=BF16, parts=128, off=0):
        return self.vw(self.aW, 65536, parts, cols, dtype) if off == 0 else \
            self.vw(self.aW, 65536 + off, parts, cols, dtype)

    def barrier(self, block):
        mk = self

        @block.tensor
        def _(tensor):
            mk.sT.inc(tensor.drain())

        @block.vector
        def _(vector):
            mk.sV.inc(vector.drain())

        @block.scalar
        def _(scalar):
            mk.sS.inc(scalar.drain())

        @block.sync
        def _(sync):
            mk.sy.inc(sync.nop())

        @block.gpsimd
        def _(gpsimd):
            mk.sp.inc(gpsimd.drain())
        t = dict(T=mk.sT.n, V=mk.sV.n, S=mk.sS.n, y=mk.sy.n, p=mk.sp.n)

        @block.tensor
        def _(tensor):
            wait(tensor, mk.sV, t['V']); wait(tensor, mk.sS, t['S'])
            wait(tensor, mk.sy, t['y']); wait(tensor, mk.sp, t['p'])
            mk.qd.drain(tensor); mk.qg.drain(tensor)

        @block.vector
        def _(vector):
            selfwait(vector, mk.sV)
            wait(vector, mk.sT, t['T']); wait(vector, mk.sS, t['S'])
            wait(vector, mk.sy, t['y']); wait(vector, mk.sp, t['p'])
            mk.qd.drain(vector); mk.qg.drain(vector)

        @block.scalar
        def _(scalar):
            selfwait(scalar, mk.sS)
            wait(scalar, mk.sT, t['T']); wait(scalar, mk.sV, t['V'])
            wait(scalar, mk.sy, t['y']); wait(scalar, mk.sp, t['p'])
            mk.qd.drain(scalar); mk.qg.drain(scalar)

        @block.sync
        def _(sync):
            wait(sync, mk.sT, t['T']); wait(sync, mk.sV, t['V'])
            wait(sync, mk.sS, t['S']); wait(sync, mk.sp, t['p'])
            mk.qd.drain(sync); mk.qg.drain(sync)

        @block.gpsimd
        def _(gpsimd):
            wait(gpsimd, mk.sT, t['T']); wait(gpsimd, mk.sV, t['V'])
            wait(gpsimd, mk.sS, t['S']); wait(gpsimd, mk.sy, t['y'])
            mk.qd.drain(gpsimd); mk.qg.drain(gpsimd)

    def p0(self, block):
        mk = self

        @block.sync
        def _(sync):
            for dst, src in ((mk.identf, mk.ident), (mk.bo_sb, mk.boa),
                             (mk.b1_sb, mk.b1), (mk.b2_sb, mk.b2)):
                _qs = mk.qd.pre(sync)
                i = sync.dma_start(out=dst[:, :], in_=src[:, :])
                _h = mk.qd.post(i)

        @block.gpsimd
        def _(gpsimd):
            _qs = mk.qg.pre(gpsimd)
            i = gpsimd.dma_start(out=mk.identb[:], in_=mk.ident[:, :])
            _h = mk.qg.post(i)
            _qs = mk.qg.pre(gpsimd)
            i = gpsimd.dma_start(out=mk.cmaskb[:], in_=mk.cmask[:, :])
            _h = mk.qg.post(i)

        @block.vector
        def _(vector):
            selfwait(vector, mk.sV)
            vector.memset(mk.onesb[:, 0:1], 1.0)
            vector.memset(mk.onesf[0:1, :], 1.0)
            vector.memset(mk.c1024[0:1, 0:1], 1.0 / 1024.0)
            vector.memset(mk.cneg[0:1, 0:1], -1.0)
            vector.memset(mk.epsln[0:1, 0:1], 1e-5)
            vector.memset(mk.eps6[:, 0:1], 1e-6)
            vector.memset(mk.zero128[:, 0:1], 0.0)
            vector.memset(mk.c32[:, 0:1], float(D) ** -0.5)
            vector.memset(mk.c1024c[:, 0:1], 1.0 / 1024.0)
            vector.memset(mk.c4096c[:, 0:1], 1.0 / 4096.0)
            vector.memset(mk.cm05[0:1, 0:1], -0.5)
            i = vector.memset(mk.c0125[:, 0:1], 0.125)
            mk.sV.inc(i)

    def load_w(self, block, dst, src, n_parts=4):
        mk = self
        cols = src.shape[1]
        step = cols // n_parts

        hs = []

        @block.gpsimd
        def _(gpsimd):
            for i_ in range(n_parts):
                _qs = mk.qg.pre(gpsimd)
                ins = gpsimd.dma_start(out=dst[:, i_ * step:(i_ + 1) * step],
                                       in_=src[:, i_ * step:(i_ + 1) * step])
                hs.append(mk.qg.post(ins))
        return hs

    def emit_ln(self, block, x3, xnb_out, extra_wait, out_wait, lnps=None):
        mk = self
        lvl = LN_LVL
        lnps = lnps if lnps is not None else mk.psY
        xbb = self.vw(self.aX, 65536, 128, 1024, BF16)
        xsqb = self.vw(self.aX, 67584, 128, 1024, BF16)
        xbb3 = xbb.rearrange("q (c p) -> q c p", c=8)
        xsq3 = xsqb.rearrange("q (c p) -> q c p", c=8)

        @block.vector
        def _(vector):
            selfwait(vector, mk.sV)
            extra_wait(vector)
            i = vector.tensor_copy(out=xbb3, in_=x3)
            mk.sV.inc(i)
        t_xbb = mk.sV.n

        @block.gpsimd
        def _(gpsimd):
            selfwait(gpsimd, mk.sp)
            wait(gpsimd, mk.sV, t_xbb)
            i = gpsimd.tensor_mul(out=xsqb[:, :], in0=xbb[:, :], in1=xbb[:, :])
            mk.sp.inc(i)
        t_xsq = mk.sp.n
        if lvl < 2:
            return mk.sV.n

        @block.tensor
        def _(tensor):
            wait(tensor, mk.sV, t_xbb)
            wait(tensor, mk.sp, t_xsq)
            xb3 = xbb.rearrange("q (c p) -> q c p", c=8)
            xs3 = xsqb.rearrange("q (c p) -> q c p", c=8)
            for c in range(8):
                tensor.matmul(lnps[0:1, 0:128], mk.onesb[:, 0:1], xb3[:, c, :],
                              start=(c == 0), stop=(c == 7))
            for c in range(8):
                i = tensor.matmul(lnps[0:1, 128:256], mk.onesb[:, 0:1], xs3[:, c, :],
                                  start=(c == 0), stop=(c == 7))
            mk.sT.inc(i)
        t_stats = mk.sT.n

        @block.vector
        def _(vector):
            selfwait(vector, mk.sV)
            wait(vector, mk.sT, t_stats)
            mk.sV.inc(vector.tensor_scalar(out=mk.r_m[0:1, :], in0=mk.psY[0:1, 0:128],
                                           scalar1=mk.c1024[0:1, 0:1], scalar2=None, op0=ALU.mult))
            selfwait(vector, mk.sV)
            mk.sV.inc(vector.tensor_mul(out=mk.r_ms[0:1, :], in0=mk.r_m[0:1, :], in1=mk.r_m[0:1, :]))
            selfwait(vector, mk.sV)
            i = vector.scalar_tensor_tensor(out=mk.r_var[0:1, :], in0=mk.psY[0:1, 128:256],
                                            scalar=mk.c1024[0:1, 0:1], in1=mk.r_ms[0:1, :],
                                            op0=ALU.mult, op1=ALU.subtract)
            mk.sV.inc(i)
        t_var = mk.sV.n
        if lvl < 4:
            return mk.sV.n

        @block.scalar
        def _(scalar):
            selfwait(scalar, mk.sS)
            wait(scalar, mk.sV, t_var)
            mk.sS.inc(scalar.activation(out=mk.r_std[0:1, :], in_=mk.r_var[0:1, :], func=AF.Ln,
                                        bias=mk.epsln[0:1, 0:1]))
            selfwait(scalar, mk.sS)
            i = scalar.activation(out=mk.r_a[0:1, :], in_=mk.r_std[0:1, :], func=AF.Exp,
                                  scale=mk.cm05[0:1, 0:1])
            mk.sS.inc(i)
        t_std = mk.sS.n

        @block.vector
        def _(vector):
            selfwait(vector, mk.sV)
            i = vector.tensor_scalar(out=mk.r_b[0:1, :], in0=mk.r_m[0:1, :],
                                     scalar1=mk.cneg[0:1, 0:1], scalar2=None, op0=ALU.mult)
            mk.sV.inc(i)
        t_ab = mk.sV.n

        @block.tensor
        def _(tensor):
            wait(tensor, mk.sV, t_ab)
            wait(tensor, mk.sS, t_std)
            tensor.matmul(lnps[:, 256:384], mk.onesf[0:1, :], mk.r_a[0:1, :],
                          start=True, stop=True)
            i = tensor.matmul(lnps[:, 384:512], mk.onesf[0:1, :], mk.r_b[0:1, :],
                              start=True, stop=True)
            mk.sT.inc(i)
        t_rep = mk.sT.n
        if lvl < 6:
            return mk.sV.n

        @block.vector
        def _(vector):
            selfwait(vector, mk.sV)
            wait(vector, mk.sT, t_rep)
            out_wait(vector)
            arep = lnps[:, 256:384].unsqueeze(1).broadcast_to([128, 8, 128])
            brep = lnps[:, 384:512].unsqueeze(1).broadcast_to([128, 8, 128])
            mk.sV.inc(vector.tensor_tensor(out=xnb_out, in0=x3, in1=brep, op=ALU.add))
            selfwait(vector, mk.sV)
            i = vector.tensor_tensor(out=xnb_out, in0=xnb_out, in1=arep, op=ALU.mult)
            mk.sV.inc(i)
        return mk.sV.n

    def p1(self, block):
        mk = self
        t_wk = self.load_w(block, self.wA(), mk.wka)
        t_wq = self.load_w(block, self.wB(), mk.wqa)
        self.t_wq = t_wq
        wk3 = self.wA().rearrange("q (c f) -> q c f", c=8)
        ksum = self.vw(self.aX, 4096, 128, NT * 64, F32)
        xts = [self.vw(self.aX, 12288, 128, 1024, F32),
               self.vw(self.aX, 16384, 128, 1024, F32)]
        xnbs = [self.vw(self.aX, 20480, 128, 1024, BF16),
                self.vw(self.aX, 22528, 128, 1024, BF16)]
        kst = self.vw(self.aX, 24576, 128, 512, F32)

        xt_free = [0, 0]
        xnb_free = [0, 0]
        xnb_spill = [0, 0]
        bankV = [0, 0, 0, 0]
        pb4 = [mk.psA[0], mk.psA[1], mk.psP[0], mk.psP[1]]
        x0v = mk.x0.rearrange("(c p) s -> p c s", p=128)
        xn1v = mk.xn1s.rearrange("(c p) s -> p c s", p=128)
        for t in range(NT):
            pi = t % 2

            @block.sync
            def _(sync):
                wait(sync, mk.sV, xt_free[pi])
                _qs = mk.qd.pre(sync)
                i = sync.dma_start(out=xts[pi].rearrange("q (c p) -> q c p", c=8),
                                   in_=x0v[:, :, t * 128:(t + 1) * 128])
                _h = mk.qd.post(i)
            t_x = mk.qd.last

            xn_dst = xnbs[pi].rearrange("q (c p) -> q c p", c=8)

            def extra_wait(eng, _t=t_x):
                waith(eng, _t)

            def out_wait(eng, _a=xnb_free[pi], _b=xnb_spill[pi]):
                wait(eng, mk.sT, _a)
                waith(eng, _b)

            t_norm = self.emit_ln(block, xts[pi].rearrange("q (c p) -> q c p", c=8), xn_dst, extra_wait, out_wait)
            xt_free[pi] = t_norm

            @block.sync
            def _(sync):
                wait(sync, mk.sV, t_norm)
                _qs = mk.qd.pre(sync)
                i = sync.dma_start(out=xn1v[:, :, t * 128:(t + 1) * 128], in_=xn_dst)
                _h = mk.qd.post(i)
            xnb_spill[pi] = mk.qd.last

            xnb3 = xnbs[pi].rearrange("q (c p) -> q c p", c=8)
            for nck in range(8):
                b = nck % 2

                @block.tensor
                def _(tensor):
                    if t == 0 and nck == 0:
                        waith(tensor, t_wk)
                    wait(tensor, mk.sV, t_norm)
                    wait(tensor, mk.sV, bankV[b])
                    for c in range(8):
                        i = tensor.matmul(mk.psA[b][:, 0:512], xnb3[:, c, :],
                                          wk3[:, c, nck * 512:(nck + 1) * 512],
                                          start=(c == 0), stop=(c == 7))
                    mk.sT.inc(i)
                t_proj = mk.sT.n

                @block.vector
                def _(vector):
                    selfwait(vector, mk.sV)
                    wait(vector, mk.sT, t_proj)
                    mk.sV.inc(vector.tensor_scalar(out=kst[:, 0:512], in0=mk.psA[b][:, 0:512],
                                                   scalar1=mk.zero128[:, 0:1], scalar2=None, op0=ALU.max))
                    selfwait(vector, mk.sV)
                    i = vector.reduce_sum(out=ksum[:, t * 64 + nck * 8: t * 64 + nck * 8 + 8],
                                          in_=kst.rearrange("q (a b) -> q a b", b=64),
                                          axis=AX.X)
                    mk.sV.inc(i)
                bankV[b] = mk.sV.n
            xnb_free[pi] = mk.sT.n

        @block.vector
        def _(vector):
            selfwait(vector, mk.sV)
            selfwait(vector, mk.sV)
            i = vector.tensor_scalar(out=ksum[:, :], in0=ksum[:, :],
                                     scalar1=mk.c0125[:, 0:1], scalar2=None, op0=ALU.mult)
            mk.sV.inc(i)
        self.t_ksum = mk.sV.n

    def p2(self, block):
        mk = self
        self.t_wv = self.load_w(block, self.wA(), mk.wva)
        wq3 = self.wB().rearrange("q (c f) -> q c f", c=8)
        ksum = self.vw(self.aX, 4096, 128, NT * 64, F32)
        xnbs = [self.vw(self.aX, 0, 128, 1024, BF16),
                self.vw(self.aX, 2048, 128, 1024, BF16)]
        qp = self.vw(self.aX, 12288, 128, 4096, F32)
        qb = self.vw(self.aX, 28672, 128, 4096, BF16)
        sqt = [self.vw(self.aX, 36864, 64, 8192, BF16),
               self.vw(self.aX, 53248, 64, 8192, BF16)]
        xn1v = mk.xn1s.rearrange("(c p) s -> p c s", p=128)

        xnb_ld = [0, 0]
        xnb_free = [0, 0]
        bankV = [0, 0, 0, 0]
        pb4 = [mk.psA[0], mk.psA[1], mk.psP[0], mk.psP[1]]
        t_qb_prev = 0
        qb_mm = 0
        psT_free = [0, 0]
        psT_eng = [None, None]
        sq_spill = [0, 0]
        for t in range(NT + 1):
            pi = t % 2
            if t < NT:
                @block.sync
                def _(sync):
                    wait(sync, mk.sT, xnb_free[pi])
                    _qs = mk.qd.pre(sync)
                    i = sync.dma_start(out=xnbs[pi].rearrange("q (c p) -> q c p", c=8),
                                       in_=xn1v[:, :, t * 128:(t + 1) * 128])
                    _h = mk.qd.post(i)
                xnb_ld[pi] = mk.qd.last

                xnb3 = xnbs[pi].rearrange("q (c p) -> q c p", c=8)
                for nck in range(8):
                    b = nck % 4

                    @block.tensor
                    def _(tensor):
                        if nck == 0:
                            waith(tensor, xnb_ld[pi])
                        if t == 0 and nck == 0:
                            waith(tensor, self.t_wq)
                            wait(tensor, mk.sV, self.t_ksum)
                        wait(tensor, mk.sV, bankV[b])
                        for c in range(8):
                            i = tensor.matmul(pb4[b][:, 0:512], xnb3[:, c, :],
                                              wq3[:, c, nck * 512:(nck + 1) * 512],
                                              start=(c == 0), stop=(c == 7))
                        mk.sT.inc(i)
                    t_proj = mk.sT.n

                    @block.vector
                    def _(vector):
                        selfwait(vector, mk.sV)
                        wait(vector, mk.sT, t_proj)
                        ks = ksum[:, t * 64:(t + 1) * 64].unsqueeze(1).broadcast_to([128, 8, 64])
                        mk.sV.inc(vector.scalar_tensor_tensor(
                            out=qp[:, nck * 512:(nck + 1) * 512].rearrange("q (a b) -> q a b", b=64),
                            in0=pb4[b][:, 0:512].rearrange("q (a b) -> q a b", b=64),
                            scalar=mk.zero128[:, 0:1], in1=ks, op0=ALU.max, op1=ALU.mult))
                        selfwait(vector, mk.sV)
                        i = vector.reduce_sum(out=mk.qn[:, nck * 8:(nck + 1) * 8],
                                              in_=qp[:, nck * 512:(nck + 1) * 512]
                                              .rearrange("q (a b) -> q a b", b=64),
                                              axis=AX.X)
                        mk.sV.inc(i)
                    bankV[b] = mk.sV.n
                xnb_free[pi] = mk.sT.n


            if t > 0:
                tp = t - 1
                pp = tp % 2
                for w in range(8):
                    ps_ = w % 2

                    @block.tensor
                    def _(tensor):
                        if w == 0:
                            wait(tensor, mk.sV, t_qb_prev)
                        if psT_eng[ps_] is not None:
                            wait(tensor, psT_eng[ps_], psT_free[ps_])
                        for u in range(8):
                            i = tensor.transpose(mk.psT[ps_][0:64, u * 128:(u + 1) * 128],
                                                 qb[:, (w * 8 + u) * 64:(w * 8 + u + 1) * 64],
                                                 mk.identb[:, :])
                        mk.sT.inc(i)
                    t_w = mk.sT.n

                    @block.scalar
                    def _(scalar):
                        selfwait(scalar, mk.sS)
                        wait(scalar, mk.sT, t_w)
                        waith(scalar, sq_spill[pp])
                        i = scalar.copy(out=sqt[pp][:, w * 1024:(w + 1) * 1024],
                                        in_=mk.psT[ps_][0:64, :])
                        mk.sS.inc(i)
                    psT_free[ps_], psT_eng[ps_] = mk.sS.n, mk.sS
                qb_mm = mk.sT.n

                @block.sync
                def _(sync):
                    wait(sync, mk.sS, mk.sS.n)
                    _qs = mk.qd.pre(sync)
                    i = sync.dma_start(out=mk.sqs[:, tp * 8192:(tp + 1) * 8192],
                                       in_=sqt[pp][:, :])
                    _h = mk.qd.post(i)
                sq_spill[pp] = mk.qd.last

            if t < NT:
                @block.vector
                def _(vector):
                    selfwait(vector, mk.sV)
                    mk.sV.inc(vector.tensor_scalar(out=mk.qnr[:, :], in0=mk.qn[:, :],
                                                   scalar1=mk.eps6[:, 0:1], scalar2=None, op0=ALU.add))
                    selfwait(vector, mk.sV)
                    mk.sV.inc(vector.reciprocal(out=mk.qnr[:, :], in_=mk.qnr[:, :]))
                    selfwait(vector, mk.sV)
                    wait(vector, mk.sT, qb_mm)
                    qr = mk.qnr[:, :].unsqueeze(2).broadcast_to([128, 64, 64])
                    i = vector.tensor_tensor(out=qb.rearrange("q (a b) -> q a b", b=64),
                                             in0=qp.rearrange("q (a b) -> q a b", b=64),
                                             in1=qr, op=ALU.mult)
                    mk.sV.inc(i)
                t_qb_prev = mk.sV.n

    def p3a(self, block):
        mk = self
        self.t_wo = self.load_w(block, self.wB(), mk.woa)
        wv3 = self.wA().rearrange("q (c f) -> q c f", c=8)
        entry_sV, entry_sS = mk.sV.n, mk.sS.n
        xnbs = [self.vw(self.aX, 0, 128, 1024, BF16),
                self.vw(self.aX, 2048, 128, 1024, BF16)]
        vb = self.vw(self.aX, 28672, 128, 4096, BF16)
        svt = [self.vw(self.aX, 36864, 64, 8192, BF16),
               self.vw(self.aX, 53248, 64, 8192, BF16)]
        xn1v = mk.xn1s.rearrange("(c p) s -> p c s", p=128)

        xnb_ld = [0, 0]
        xnb_free = [0, 0]
        bankE = [0, 0, 0, 0]
        bankEng = [None, None, None, None]
        pb4 = [mk.psA[0], mk.psA[1], mk.psP[0], mk.psP[1]]
        vb_done_v = 0
        vb_done_s = 0
        vb_mm = 0
        psT_free = [0, 0]
        psT_eng = [None, None]
        sv_spill = [0, 0]
        for t in range(NT + 1):
            pi = t % 2

            if t > 0:
                tp = t - 1
                pp = tp % 2
                tv, ts_ = vb_done_v, vb_done_s
                for w in range(8):
                    ps_ = w % 2

                    @block.tensor
                    def _(tensor):
                        if w == 0:
                            wait(tensor, mk.sV, tv)
                            wait(tensor, mk.sS, ts_)
                        if psT_eng[ps_] is not None:
                            wait(tensor, psT_eng[ps_], psT_free[ps_])
                        for u in range(8):
                            i = tensor.transpose(mk.psT[ps_][0:64, u * 128:(u + 1) * 128],
                                                 vb[:, (w * 8 + u) * 64:(w * 8 + u + 1) * 64],
                                                 mk.identb[:, :])
                        mk.sT.inc(i)
                    t_w = mk.sT.n

                    @block.scalar
                    def _(scalar):
                        selfwait(scalar, mk.sS)
                        wait(scalar, mk.sT, t_w)
                        waith(scalar, sv_spill[pp])
                        i = scalar.copy(out=svt[pp][:, w * 1024:(w + 1) * 1024],
                                        in_=mk.psT[ps_][0:64, :])
                        mk.sS.inc(i)
                    psT_free[ps_], psT_eng[ps_] = mk.sS.n, mk.sS
                vb_mm = mk.sT.n

                @block.sync
                def _(sync):
                    wait(sync, mk.sS, mk.sS.n)
                    _qs = mk.qd.pre(sync)
                    i = sync.dma_start(out=mk.svs[:, tp * 8192:(tp + 1) * 8192],
                                       in_=svt[pp][:, :])
                    _h = mk.qd.post(i)
                sv_spill[pp] = mk.qd.last

            if t < NT:
                @block.sync
                def _(sync):
                    wait(sync, mk.sT, xnb_free[pi])
                    _qs = mk.qd.pre(sync)
                    i = sync.dma_start(out=xnbs[pi].rearrange("q (c p) -> q c p", c=8),
                                       in_=xn1v[:, :, t * 128:(t + 1) * 128])
                    _h = mk.qd.post(i)
                xnb_ld[pi] = mk.qd.last

                xnb3 = xnbs[pi].rearrange("q (c p) -> q c p", c=8)
                for nck in range(8):
                    b = nck % 4

                    @block.tensor
                    def _(tensor):
                        if nck == 0:
                            waith(tensor, xnb_ld[pi])
                        if t == 0 and nck == 0:
                            waith(tensor, self.t_wv)
                            wait(tensor, mk.sV, entry_sV)
                            wait(tensor, mk.sS, entry_sS)
                        if bankEng[b] is not None:
                            wait(tensor, bankEng[b], bankE[b])
                        for c in range(8):
                            i = tensor.matmul(pb4[b][:, 0:512], xnb3[:, c, :],
                                              wv3[:, c, nck * 512:(nck + 1) * 512],
                                              start=(c == 0), stop=(c == 7))
                        mk.sT.inc(i)
                    t_proj = mk.sT.n

                    if nck % 2 == 0:
                        @block.vector
                        def _(vector):
                            selfwait(vector, mk.sV)
                            wait(vector, mk.sT, t_proj)
                            if nck == 0:
                                wait(vector, mk.sT, vb_mm)
                            i = vector.tensor_copy(out=vb[:, nck * 512:(nck + 1) * 512],
                                                   in_=pb4[b][:, 0:512])
                            mk.sV.inc(i)
                        bankE[b], bankEng[b] = mk.sV.n, mk.sV
                    else:
                        @block.scalar
                        def _(scalar):
                            selfwait(scalar, mk.sS)
                            wait(scalar, mk.sT, t_proj)
                            if nck == 1:
                                wait(scalar, mk.sT, vb_mm)
                            i = scalar.copy(out=vb[:, nck * 512:(nck + 1) * 512],
                                            in_=pb4[b][:, 0:512])
                            mk.sS.inc(i)
                        bankE[b], bankEng[b] = mk.sS.n, mk.sS
                xnb_free[pi] = mk.sT.n
                vb_done_v, vb_done_s = mk.sV.n, mk.sS.n

    def p3b(self, block):
        mk = self
        entry_sV, entry_sS = mk.sV.n, mk.sS.n
        wo3 = self.wB().rearrange("q (g m) -> q g m", g=32)
        sqt = [self.wA(8192, BF16, 64, 0), self.wA(8192, BF16, 64, 16384)]
        svt = [self.wA(8192, BF16, 64, 32768), self.wA(8192, BF16, 64, 49152)]
        s2d = [self.vw(self.aX, 0, 128, 8192, BF16),
               self.vw(self.aX, 16384, 128, 8192, BF16)]
        xts = [self.vw(self.aX, 32768, 128, 1024, F32),
               self.vw(self.aX, 36864, 128, 1024, F32)]
        xos = [self.vw(self.aX, 40960, 128, 1024, F32),
               self.vw(self.aX, 45056, 128, 1024, F32)]
        yb = self.vw(self.aX, 49152, 128, 1024, F32)
        x0v = mk.x0.rearrange("(c p) s -> p c s", p=128)
        x1v = mk.x1s.rearrange("(c p) s -> p c s", p=128)

        sq_free = [0, 0]
        x_free = [0, 0]
        s2_mm = [0, 0]
        xo_spill = [0, 0]
        bankP = [0, 0]
        bankPe = [None, None]
        psy_sc = 0
        psy_tt = 0
        psa_free = [0, 0]
        psa_eng = [None, None]
        for t in range(NT):
            pi = t % 2

            hs_ld = []

            @block.sync
            def _(sync):
                wait(sync, mk.sT, sq_free[pi])
                _qs = mk.qd.pre(sync)
                i = sync.dma_start(out=sqt[pi][:, :], in_=mk.sqs[:, t * 8192:(t + 1) * 8192])
                hs_ld.append(mk.qd.post(i))
                _qs = mk.qd.pre(sync)
                i = sync.dma_start(out=svt[pi][:, :], in_=mk.svs[:, t * 8192:(t + 1) * 8192])
                hs_ld.append(mk.qd.post(i))
                wait(sync, mk.sV, x_free[pi])
                _qs = mk.qd.pre(sync)
                i = sync.dma_start(out=xts[pi].rearrange("q (c p) -> q c p", c=8),
                                   in_=x0v[:, :, t * 128:(t + 1) * 128])
                hs_ld.append(mk.qd.post(i))
            t_loads = hs_ld

            sv3 = svt[pi].rearrange("d (j p) -> d j p", p=128)
            sq3 = sqt[pi].rearrange("d (i p) -> d i p", p=128)
            for pg in range(16):
                b = pg % 2

                @block.tensor
                def _(tensor):
                    waith(tensor, t_loads)
                    if t == 0 and pg < 2:
                        wait(tensor, mk.sV, entry_sV)
                        wait(tensor, mk.sS, entry_sS)
                        waith(tensor, self.t_wo)
                    if bankPe[b] is not None:
                        wait(tensor, bankPe[b], bankP[b])
                    for u in range(8):
                        p = pg * 8 + u
                        lhsT = sv3[:, :, p]
                        rhs = sq3[:, :, p]
                        tensor.matmul(mk.psP[b][0:64, u * 64:(u + 1) * 64], lhsT, rhs,
                                      start=True, stop=True)
                        i = tensor.matmul(mk.psP[b][64:128, u * 64:(u + 1) * 64], lhsT, rhs,
                                          start=True, stop=True, tile_position=(0, 64))
                    mk.sT.inc(i)
                t_pg = mk.sT.n

                if pg % 2 == 0:
                    @block.vector
                    def _(vector):
                        selfwait(vector, mk.sV)
                        wait(vector, mk.sT, t_pg)
                        wait(vector, mk.sT, s2_mm[pi])
                        i = vector.tensor_copy(out=s2d[pi][:, pg * 512:(pg + 1) * 512],
                                               in_=mk.psP[b][:, :])
                        mk.sV.inc(i)
                    bankP[b], bankPe[b] = mk.sV.n, mk.sV
                else:
                    @block.vector
                    def _(vector):
                        selfwait(vector, mk.sV)
                        wait(vector, mk.sT, t_pg)
                        wait(vector, mk.sT, s2_mm[pi])
                        i = vector.tensor_copy(out=s2d[pi][:, pg * 512:(pg + 1) * 512],
                                        in_=mk.psP[b][:, :])
                        mk.sV.inc(i)
                    bankP[b], bankPe[b] = mk.sV.n, mk.sV
            sq_free[pi] = mk.sT.n
            t_s2_v, t_s2_s = mk.sV.n, mk.sS.n

            s2e = s2d[pi][0:64, :].rearrange("j (p i) -> j p i", i=64)
            s2o = s2d[pi][64:128, :].rearrange("j (p i) -> j p i", i=64)
            for h in range(2):
                @block.tensor
                def _(tensor):
                    wait(tensor, mk.sV, t_s2_v)
                    wait(tensor, mk.sS, t_s2_s)
                    wait(tensor, mk.sV, psy_sc)
                    wait(tensor, mk.sV, psy_tt)
                    for ip in range(32):
                        i0, i1 = 2 * ip, 2 * ip + 1
                        tensor.matmul(mk.psY[:, 0:512], s2e[:, :, i0],
                                      wo3[0:64, i0 // 2, h * 512:(h + 1) * 512],
                                      start=(ip == 0), stop=(ip == 31))
                        i = tensor.matmul(mk.psY[:, 512:1024], s2o[:, :, i1],
                                          wo3[64:128, i1 // 2, h * 512:(h + 1) * 512],
                                          start=(ip == 0), stop=(ip == 31))
                    mk.sT.inc(i)
                t_wo_mm = mk.sT.n

                @block.vector
                def _(vector):
                    selfwait(vector, mk.sV)
                    wait(vector, mk.sT, t_wo_mm)
                    i = vector.tensor_copy(out=yb[:, h * 512:(h + 1) * 512], in_=mk.psY[:, 0:512])
                    mk.sV.inc(i)
                psy_sc = mk.sS.n

                @block.vector
                def _(vector):
                    selfwait(vector, mk.sV)
                    i = vector.tensor_tensor(out=yb[:, h * 512:(h + 1) * 512],
                                             in0=yb[:, h * 512:(h + 1) * 512],
                                             in1=mk.psY[:, 512:1024], op=ALU.add)
                    mk.sV.inc(i)
                psy_tt = mk.sV.n
            s2_mm[pi] = mk.sT.n
            t_yb = mk.sV.n

            for c in range(8):
                a = c % 2
                slot = c // 2

                @block.tensor
                def _(tensor):
                    wait(tensor, mk.sV, t_yb)
                    if psa_eng[a] is not None:
                        wait(tensor, psa_eng[a], psa_free[a])
                    i = tensor.matmul(mk.psA[a][:, slot * 128:(slot + 1) * 128],
                                      yb[:, c * 128:(c + 1) * 128], mk.identf[:, :],
                                      start=True, stop=True)
                    mk.sT.inc(i)
                t_tp = mk.sT.n

                @block.vector
                def _(vector):
                    selfwait(vector, mk.sV)
                    wait(vector, mk.sT, t_tp)
                    waith(vector, t_loads)
                    waith(vector, xo_spill[pi])
                    i = vector.scalar_tensor_tensor(
                        out=xos[pi][:, c * 128:(c + 1) * 128],
                        in0=mk.psA[a][:, slot * 128:(slot + 1) * 128],
                        scalar=mk.bo_sb[:, c:c + 1],
                        in1=xts[pi][:, c * 128:(c + 1) * 128],
                        op0=ALU.add, op1=ALU.add)
                    mk.sV.inc(i)
                psa_free[a], psa_eng[a] = mk.sV.n, mk.sV
            t_res = mk.sV.n
            x_free[pi] = t_res

            @block.sync
            def _(sync):
                wait(sync, mk.sV, t_res)
                _qs = mk.qd.pre(sync)
                i = sync.dma_start(out=x1v[:, :, t * 128:(t + 1) * 128],
                                   in_=xos[pi].rearrange("q (c p) -> q c p", c=8))
                _h = mk.qd.post(i)
            xo_spill[pi] = mk.qd.last

    def p4(self, block, xin, xout, prep, ngroups=8, depth=3, no_reload=False):
        mk = self
        w1s3 = self.wA().rearrange("q (c f) -> q c f", c=8)
        w2s3 = self.wB().rearrange("q (c f) -> q c f", c=32)
        XG = self.vw(self.aX, 0, 128, 4096, F32)
        XG3 = XG.rearrange("q (c p) -> q c p", c=8)
        XNG = self.vw(self.aX, 16384, 128, 4096, BF16)
        XNG3 = XNG.rearrange("q (c p) -> q c p", c=8)
        HB = self.vw(self.aX, 24576, 128, 16384, BF16)
        TMP = self.vw(self.aX, 57344, 128, 512, F32)
        xinv = xin.rearrange("(c p) s -> p c s", p=128)
        xoutv = xout.rearrange("(c p) s -> p c s", p=128)

        if prep:
            self.p4_prep(block)
        elif no_reload:
            self._p4_wtok = (0, 0)
        else:
            hs_w = []

            @block.gpsimd
            def _(gpsimd):
                for i_ in range(4):
                    _qs = mk.qg.pre(gpsimd)
                    ins = gpsimd.dma_start(out=self.wA()[:, i_ * 8192:(i_ + 1) * 8192],
                                           in_=mk.w1ss[:, i_ * 8192:(i_ + 1) * 8192])
                    hs_w.append(mk.qg.post(ins))
                for i_ in range(4):
                    _qs = mk.qg.pre(gpsimd)
                    ins = gpsimd.dma_start(out=self.wB()[:, i_ * 8192:(i_ + 1) * 8192],
                                           in_=mk.w2ss[:, i_ * 8192:(i_ + 1) * 8192])
                    hs_w.append(mk.qg.post(ins))
            t_w = hs_w

            hs_al = []

            @block.sync
            def _(sync):
                _qs = mk.qd.pre(sync)
                i = sync.dma_start(out=mk.al1_sb[:, :], in_=mk.a1s[:, :])
                hs_al.append(mk.qd.post(i))
                _qs = mk.qd.pre(sync)
                i = sync.dma_start(out=mk.al2_sb[:, :], in_=mk.a2s[:, :])
                hs_al.append(mk.qd.post(i))
            t_al = hs_al
            self._p4_wtok = (t_w, t_al)

        xg_free = 0
        hb_free = 0
        bank4 = [0, 0, 0, 0]
        bank4e = [None, None, None, None]
        pb4 = [mk.psA[0], mk.psA[1], mk.psP[0], mk.psP[1]]
        for grp in range(ngroups):
            @block.sync
            def _(sync):
                waith(sync, xg_free)
                _qs = mk.qd.pre(sync)
                i = sync.dma_start(out=XG3, in_=xinv[:, :, grp * 512:(grp + 1) * 512])
                _h = mk.qd.post(i)
            t_xg = mk.qd.last

            t_norms = [mk.sV.n]
            for tau in range(4 if depth >= 1 else 0):
                def extra_wait(eng, _t=t_xg):
                    waith(eng, _t)

                def out_wait(eng, _t=hb_free):
                    wait(eng, mk.sT, _t)

                t_n = self.emit_ln(block, XG3[:, :, tau * 128:(tau + 1) * 128],
                                   XNG3[:, :, tau * 128:(tau + 1) * 128],
                                   extra_wait, out_wait)
                t_norms.append(t_n)
            t_norm = t_norms[-1]

            for hfc in range(32 if depth >= 2 else 0):
                b = hfc % 4

                @block.tensor
                def _(tensor):
                    if hfc == 0:
                        wait(tensor, mk.sV, t_norm)
                    if grp == 0 and hfc == 0 and not prep:
                        waith(tensor, self._p4_wtok[0])
                    if bank4e[b] is not None:
                        wait(tensor, bank4e[b], bank4[b])
                    for c in range(8):
                        i = tensor.matmul(pb4[b][:, 0:512], w1s3[:, c, hfc * 128:(hfc + 1) * 128],
                                          XNG3[:, c, :], start=(c == 0), stop=(c == 7))
                    mk.sT.inc(i)
                t_h = mk.sT.n

                @block.scalar
                def _(scalar):
                    selfwait(scalar, mk.sS)
                    wait(scalar, mk.sT, t_h)
                    if grp == 0 and hfc == 0 and not prep:
                        waith(scalar, self._p4_wtok[1])
                    if hfc == 0:
                        wait(scalar, mk.sT, hb_free)
                    i = scalar.activation(out=HB[:, hfc * 512:(hfc + 1) * 512],
                                          in_=pb4[b][:, 0:512], func=AF.Gelu,
                                          scale=mk.al1_sb[:, hfc:hfc + 1],
                                          bias=mk.b1_sb[:, hfc:hfc + 1])
                    mk.sS.inc(i)
                bank4[b], bank4e[b] = mk.sS.n, mk.sS
            t_hb = mk.sS.n

            for dc in range(8 if depth >= 3 else 0):
                b = dc % 4

                @block.tensor
                def _(tensor):
                    if dc == 0:
                        wait(tensor, mk.sS, t_hb)
                    if bank4e[b] is not None:
                        wait(tensor, bank4e[b], bank4[b])
                    for hc in range(32):
                        i = tensor.matmul(pb4[b][:, 0:512], w2s3[:, hc, dc * 128:(dc + 1) * 128],
                                          HB[:, hc * 512:(hc + 1) * 512],
                                          start=(hc == 0), stop=(hc == 31))
                    mk.sT.inc(i)
                t_o = mk.sT.n

                @block.vector
                def _(vector):
                    selfwait(vector, mk.sV)
                    wait(vector, mk.sT, t_o)
                    mk.sV.inc(vector.scalar_tensor_tensor(out=TMP[:, :], in0=pb4[b][:, 0:512],
                                                          scalar=mk.al2_sb[:, dc:dc + 1],
                                                          in1=XG3[:, dc, :],
                                                          op0=ALU.mult, op1=ALU.add))
                    selfwait(vector, mk.sV)
                    i = vector.tensor_scalar(out=XG3[:, dc, :], in0=TMP[:, :],
                                             scalar1=mk.b2_sb[:, dc:dc + 1], scalar2=None,
                                             op0=ALU.add)
                    mk.sV.inc(i)
                bank4[b], bank4e[b] = mk.sV.n, mk.sV
            hb_free = mk.sT.n
            t_x2 = mk.sV.n

            @block.sync
            def _(sync):
                wait(sync, mk.sV, t_x2)
                _qs = mk.qd.pre(sync)
                i = sync.dma_start(out=xoutv[:, :, grp * 512:(grp + 1) * 512], in_=XG3)
                _h = mk.qd.post(i)
            xg_free = mk.qd.last

    def p4_prep(self, block):
        mk = self
        STG = self.vw(self.aX, 20480, 128, 4096, F32)
        ABSB = self.vw(self.aX, 36864, 128, 4096, BF16)
        AROW = self.vw(self.aX, 53248, 1, 4096, F32)
        w1s3 = self.wA().rearrange("q (c f) -> q c f", c=8)
        w2s3 = self.wB().rearrange("q (c f) -> q c f", c=32)

        @block.vector
        def _(vector):
            selfwait(vector, mk.sV)
            i = vector.memset(AROW[0:1, :], 0.0)
            mk.sV.inc(i)
        t_z = mk.sV.n

        stg_free = 0
        absb_free = 0
        psy_free = [0, 0]
        for c in range(8):
            @block.sync
            def _(sync):
                wait(sync, mk.sS, stg_free)
                _qs = mk.qd.pre(sync)
                i = sync.dma_start(out=STG[:, :], in_=mk.w1t[:, c * 4096:(c + 1) * 4096])
                _h = mk.qd.post(i)
            t_stg = mk.qd.last

            @block.scalar
            def _(scalar):
                selfwait(scalar, mk.sS)
                waith(scalar, t_stg)
                scalar.activation(out=w1s3[:, c, :], in_=STG[:, :], func=AF.Sign)
                wait(scalar, mk.sT, absb_free)
                i = scalar.activation(out=ABSB[:, :], in_=STG[:, :], func=AF.Abs)
                mk.sS.inc(i)
            stg_free = mk.sS.n
            t_abs = mk.sS.n

            for sub in range(8):
                b = sub % 2

                @block.tensor
                def _(tensor):
                    wait(tensor, mk.sS, t_abs)
                    wait(tensor, mk.sV, psy_free[b])
                    i = tensor.matmul(mk.psY[0:1, b * 512:(b + 1) * 512], mk.onesb[:, 0:1],
                                      ABSB[:, sub * 512:(sub + 1) * 512], start=True, stop=True)
                    mk.sT.inc(i)
                t_mm = mk.sT.n

                @block.vector
                def _(vector):
                    selfwait(vector, mk.sV)
                    wait(vector, mk.sT, t_mm)
                    if c == 0 and sub < 2:
                        wait(vector, mk.sV, t_z)
                    selfwait(vector, mk.sV)
                    i = vector.tensor_tensor(out=AROW[0:1, sub * 512:(sub + 1) * 512],
                                             in0=AROW[0:1, sub * 512:(sub + 1) * 512],
                                             in1=mk.psY[0:1, b * 512:(b + 1) * 512], op=ALU.add)
                    mk.sV.inc(i)
                psy_free[b] = mk.sV.n
            absb_free = mk.sT.n
        t_arow = mk.sV.n

        @block.sync
        def _(sync):
            wait(sync, mk.sV, t_arow)
            _qs = mk.qd.pre(sync)
            i = sync.dma_start(out=mk.ar1s[0:1, :], in_=AROW[0:1, :])
            h_sp = mk.qd.post(i)
            _qs = mk.qd.pre(sync)
            waith(sync, h_sp)
            i = sync.dma_start(out=mk.al1_sb[:, :],
                               in_=mk.ar1s.rearrange("o (s p) -> (o p) s", p=128),
                               allow_slow_non_contiguous=True)
            _h = mk.qd.post(i)
        t_a1d = mk.qd.last

        @block.vector
        def _(vector):
            selfwait(vector, mk.sV)
            waith(vector, t_a1d)
            mk.sV.inc(vector.tensor_scalar(out=mk.al1_sb[:, :], in0=mk.al1_sb[:, :],
                                           scalar1=mk.c1024c[:, 0:1], scalar2=None, op0=ALU.mult))
            i = vector.memset(AROW[0:1, 0:1024], 0.0)
            mk.sV.inc(i)
        t_a1 = mk.sV.n

        stg_free = mk.sS.n
        absb_free = mk.sT.n
        psy_free = [0, 0]
        for c in range(32):
            @block.sync
            def _(sync):
                wait(sync, mk.sS, stg_free)
                _qs = mk.qd.pre(sync)
                i = sync.dma_start(out=STG[:, 0:1024], in_=mk.w2t[:, c * 1024:(c + 1) * 1024])
                _h = mk.qd.post(i)
            t_stg = mk.qd.last

            @block.scalar
            def _(scalar):
                selfwait(scalar, mk.sS)
                waith(scalar, t_stg)
                scalar.activation(out=w2s3[:, c, :], in_=STG[:, 0:1024], func=AF.Sign)
                wait(scalar, mk.sT, absb_free)
                i = scalar.activation(out=ABSB[:, 0:1024], in_=STG[:, 0:1024], func=AF.Abs)
                mk.sS.inc(i)
            stg_free = mk.sS.n
            t_abs = mk.sS.n

            for sub in range(2):
                b = sub % 2

                @block.tensor
                def _(tensor):
                    wait(tensor, mk.sS, t_abs)
                    wait(tensor, mk.sV, psy_free[b])
                    i = tensor.matmul(mk.psY[0:1, b * 512:(b + 1) * 512], mk.onesb[:, 0:1],
                                      ABSB[:, sub * 512:(sub + 1) * 512], start=True, stop=True)
                    mk.sT.inc(i)
                t_mm = mk.sT.n

                @block.vector
                def _(vector):
                    selfwait(vector, mk.sV)
                    wait(vector, mk.sT, t_mm)
                    if c == 0 and sub < 2:
                        wait(vector, mk.sV, t_a1)
                    selfwait(vector, mk.sV)
                    i = vector.tensor_tensor(out=AROW[0:1, sub * 512:(sub + 1) * 512],
                                             in0=AROW[0:1, sub * 512:(sub + 1) * 512],
                                             in1=mk.psY[0:1, b * 512:(b + 1) * 512], op=ALU.add)
                    mk.sV.inc(i)
                psy_free[b] = mk.sV.n
            absb_free = mk.sT.n
        t_arow2 = mk.sV.n

        @block.sync
        def _(sync):
            wait(sync, mk.sV, t_arow2)
            _qs = mk.qd.pre(sync)
            i = sync.dma_start(out=mk.ar2s[0:1, :], in_=AROW[0:1, 0:1024])
            h_sp = mk.qd.post(i)
            _qs = mk.qd.pre(sync)
            waith(sync, h_sp)
            i = sync.dma_start(out=mk.al2_sb[:, :],
                               in_=mk.ar2s.rearrange("o (s p) -> (o p) s", p=128),
                               allow_slow_non_contiguous=True)
            _h = mk.qd.post(i)
        t_a2d = mk.qd.last

        @block.vector
        def _(vector):
            selfwait(vector, mk.sV)
            waith(vector, t_a2d)
            i = vector.tensor_scalar(out=mk.al2_sb[:, :], in0=mk.al2_sb[:, :],
                                     scalar1=mk.c4096c[:, 0:1], scalar2=None, op0=ALU.mult)
            mk.sV.inc(i)
        t_a2 = mk.sV.n

        @block.sync
        def _(sync):
            wait(sync, mk.sS, mk.sS.n)
            for i_ in range(4):
                _qs = mk.qd.pre(sync)
                i = sync.dma_start(out=mk.w1ss[:, i_ * 8192:(i_ + 1) * 8192],
                                   in_=self.wA()[:, i_ * 8192:(i_ + 1) * 8192])
                _h = mk.qd.post(i)
            for i_ in range(4):
                _qs = mk.qd.pre(sync)
                i = sync.dma_start(out=mk.w2ss[:, i_ * 8192:(i_ + 1) * 8192],
                                   in_=self.wB()[:, i_ * 8192:(i_ + 1) * 8192])
                _h = mk.qd.post(i)
            wait(sync, mk.sV, t_a2)
            _qs = mk.qd.pre(sync)
            i = sync.dma_start(out=mk.a1s[:, :], in_=mk.al1_sb[:, :])
            _h = mk.qd.post(i)
            _qs = mk.qd.pre(sync)
            i = sync.dma_start(out=mk.a2s[:, :], in_=mk.al2_sb[:, :])
            _h = mk.qd.post(i)

    def p5a(self, block):
        mk = self
        XG = self.vw(self.aX, 0, 128, 4096, F32)
        XG3 = XG.rearrange("q (c p) -> q c p", c=8)
        XNG = self.vw(self.aX, 16384, 128, 4096, BF16)
        XNG3 = XNG.rearrange("q (c p) -> q c p", c=8)
        WSM = self.vw(self.aX, 24576, 128, 8192, BF16)
        WSM3 = WSM.rearrange("q (c f) -> q c f", c=8)
        QTG = self.vw(self.aX, 40960, 128, 4096, BF16)
        QTG3 = QTG.rearrange("q (f p) -> q f p", f=8)
        x2v = mk.x2s.rearrange("(c p) s -> p c s", p=128)
        xn1v = mk.xn1s.rearrange("(c p) s -> p c s", p=128)
        kT3 = self.wA().rearrange("q (f s) -> q f s", f=8)
        vb3 = self.wB().rearrange("q (t f) -> q t f", t=32)

        t_w = self.load_w(block, WSM, mk.wqb, 2)
        xg_rd_v = 0
        xg_rd_s = 0
        xng_mm = 0
        xng_sp = 0
        qtg_sp = 0
        bankA = [0, 0]
        bankAe = [None, None]
        for grp in range(8):
            @block.sync
            def _(sync):
                wait(sync, mk.sV, xg_rd_v)
                wait(sync, mk.sS, xg_rd_s)
                _qs = mk.qd.pre(sync)
                i = sync.dma_start(out=XG3, in_=x2v[:, :, grp * 512:(grp + 1) * 512])
                _h = mk.qd.post(i)
            t_xg = mk.qd.last

            for tau in range(4):
                def extra_wait(eng, _t=t_xg):
                    waith(eng, _t)

                def out_wait(eng, _a=xng_mm, _b=xng_sp):
                    wait(eng, mk.sT, _a)
                    waith(eng, _b)

                t_n = self.emit_ln(block, XG3[:, :, tau * 128:(tau + 1) * 128],
                                   XNG3[:, :, tau * 128:(tau + 1) * 128],
                                   extra_wait, out_wait)
            t_norm = t_n
            xg_rd_v = t_norm
            xg_rd_s = mk.sS.n

            @block.sync
            def _(sync):
                wait(sync, mk.sV, t_norm)
                _qs = mk.qd.pre(sync)
                i = sync.dma_start(out=xn1v[:, :, grp * 512:(grp + 1) * 512], in_=XNG3)
                _h = mk.qd.post(i)
            xng_sp = mk.qd.last

            for fc in range(8):
                b = fc % 2

                @block.tensor
                def _(tensor):
                    wait(tensor, mk.sV, t_norm)
                    if grp == 0 and fc == 0:
                        waith(tensor, t_w)
                    if bankAe[b] is not None:
                        wait(tensor, bankAe[b], bankA[b])
                    for c in range(8):
                        i = tensor.matmul(mk.psA[b][:, 0:512], WSM3[:, c, fc * 128:(fc + 1) * 128],
                                          XNG3[:, c, :], start=(c == 0), stop=(c == 7))
                    mk.sT.inc(i)
                t_p = mk.sT.n

                if fc % 2 == 0:
                    @block.vector
                    def _(vector):
                        selfwait(vector, mk.sV)
                        wait(vector, mk.sT, t_p)
                        waith(vector, qtg_sp)
                        i = vector.tensor_copy(out=QTG[:, fc * 512:(fc + 1) * 512],
                                               in_=mk.psA[b][:, 0:512])
                        mk.sV.inc(i)
                    bankA[b], bankAe[b] = mk.sV.n, mk.sV
                else:
                    @block.vector
                    def _(vector):
                        selfwait(vector, mk.sV)
                        wait(vector, mk.sT, t_p)
                        waith(vector, qtg_sp)
                        i = vector.tensor_copy(out=QTG[:, fc * 512:(fc + 1) * 512],
                                        in_=mk.psA[b][:, 0:512])
                        mk.sV.inc(i)
                    bankA[b], bankAe[b] = mk.sV.n, mk.sV
            xng_mm = mk.sT.n

            @block.sync
            def _(sync):
                wait(sync, mk.sV, mk.sV.n)
                wait(sync, mk.sS, mk.sS.n)
                _qs = mk.qd.pre(sync)
                i = sync.dma_start(out=mk.qts.rearrange("r (f s) -> r f s", f=8)[:, :, grp * 512:(grp + 1) * 512],
                                   in_=QTG3)
                _h = mk.qd.post(i)
            qtg_sp = mk.qd.last
        self.barrier(block)

        t_w = self.load_w(block, WSM, mk.wkb, 2)
        xng_free = 0
        xng_ld = 0
        bankA = [0, 0, 0, 0]
        bankAe = [None, None, None, None]
        for grp in range(8):
            @block.sync
            def _(sync):
                wait(sync, mk.sT, xng_free)
                _qs = mk.qd.pre(sync)
                i = sync.dma_start(out=XNG3, in_=xn1v[:, :, grp * 512:(grp + 1) * 512])
                _h = mk.qd.post(i)
            xng_ld = mk.qd.last

            for fc in range(8):
                b = fc % 2

                @block.tensor
                def _(tensor):
                    waith(tensor, xng_ld)
                    if grp == 0 and fc == 0:
                        waith(tensor, t_w)
                    if bankAe[b] is not None:
                        wait(tensor, bankAe[b], bankA[b])
                    for c in range(8):
                        i = tensor.matmul(mk.psA[b][:, 0:512], WSM3[:, c, fc * 128:(fc + 1) * 128],
                                          XNG3[:, c, :], start=(c == 0), stop=(c == 7))
                    mk.sT.inc(i)
                t_p = mk.sT.n

                if fc % 2 == 0:
                    @block.vector
                    def _(vector):
                        selfwait(vector, mk.sV)
                        wait(vector, mk.sT, t_p)
                        i = vector.tensor_copy(out=kT3[:, fc, grp * 512:(grp + 1) * 512],
                                               in_=mk.psA[b][:, 0:512])
                        mk.sV.inc(i)
                    bankA[b], bankAe[b] = mk.sV.n, mk.sV
                else:
                    @block.vector
                    def _(vector):
                        selfwait(vector, mk.sV)
                        wait(vector, mk.sT, t_p)
                        i = vector.tensor_copy(out=kT3[:, fc, grp * 512:(grp + 1) * 512],
                                        in_=mk.psA[b][:, 0:512])
                        mk.sV.inc(i)
                    bankA[b], bankAe[b] = mk.sV.n, mk.sV
            xng_free = mk.sT.n
        self.barrier(block)

        t_w = self.load_w(block, WSM, mk.wvb, 2)
        bankP = [0, 0]
        bankPe = [None, None]
        for grp in range(8):
            @block.sync
            def _(sync):
                wait(sync, mk.sT, xng_free)
                _qs = mk.qd.pre(sync)
                i = sync.dma_start(out=XNG3, in_=xn1v[:, :, grp * 512:(grp + 1) * 512])
                _h = mk.qd.post(i)
            xng_ld = mk.qd.last

            for tau in range(4):
                tt = grp * 4 + tau
                for half in range(2):
                    b = half

                    @block.tensor
                    def _(tensor):
                        waith(tensor, xng_ld)
                        if grp == 0 and tau == 0 and half == 0:
                            waith(tensor, t_w)
                        if bankPe[b] is not None:
                            wait(tensor, bankPe[b], bankP[b])
                        for c in range(8):
                            i = tensor.matmul(mk.psP[b][:, 0:512],
                                              XNG3[:, c, tau * 128:(tau + 1) * 128],
                                              WSM3[:, c, half * 512:(half + 1) * 512],
                                              start=(c == 0), stop=(c == 7))
                        mk.sT.inc(i)
                    t_p = mk.sT.n

                    if half == 0:
                        @block.vector
                        def _(vector):
                            selfwait(vector, mk.sV)
                            wait(vector, mk.sT, t_p)
                            i = vector.tensor_copy(out=vb3[:, tt, 0:512], in_=mk.psP[b][:, 0:512])
                            mk.sV.inc(i)
                        bankP[b], bankPe[b] = mk.sV.n, mk.sV
                    else:
                        @block.vector
                        def _(vector):
                            selfwait(vector, mk.sV)
                            wait(vector, mk.sT, t_p)
                            i = vector.tensor_copy(out=vb3[:, tt, 512:1024], in_=mk.psP[b][:, 0:512])
                            mk.sV.inc(i)
                        bankP[b], bankPe[b] = mk.sV.n, mk.sV
            xng_free = mk.sT.n

    def p5b(self, block):
        mk = self
        WOB = self.vw(self.aX, 0, 128, 8192, BF16)
        WOB3 = WOB.rearrange("q (c f) -> q c f", c=8)
        QT = [self.vw(self.aX, 16384, 128, 1024, BF16),
              self.vw(self.aX, 18432, 128, 1024, BF16)]
        SCO = self.vw(self.aX, 20480, 128, 4096, BF16)
        E = self.vw(self.aX, 28672, 128, 4096, BF16)
        OUTB = self.vw(self.aX, 36864, 128, 1024, BF16)
        OUTTG = self.vw(self.aX, 38912, 128, 4096, BF16)
        XTG = self.vw(self.aX, 47104, 128, 4096, F32)
        XTG4 = XTG.rearrange("q (t c p) -> q t c p", t=4, c=8)
        kT3 = self.wA().rearrange("q (f s) -> q f s", f=8)
        vb3 = self.wB().rearrange("q (t f) -> q t f", t=32)
        x2v = mk.x2s.rearrange("(c p) s -> p c s", p=128)
        x3v = mk.x3s.rearrange("(c p) s -> p c s", p=128)
        qtsv = mk.qts.rearrange("r (f s) -> r f s", f=8)

        t_w = self.load_w(block, WOB, mk.wob, 2)
        qt_free = [0, 0]
        xtg_sp = 0
        xtg_res = 0
        bankA = [[0, 0], [0, 0]]
        sco_free = 0
        e_free = 0
        ent_mm = 0
        psT_free = [0, 0]
        psT_eng = [None, None]
        psy_evac = 0
        outb_tp = 0
        outtg_mm = 0
        bankP = [0, 0]
        bankPe = [None, None]
        for qg in range(8):
            hs_xtg = []

            @block.sync
            def _(sync):
                waith(sync, xtg_sp)
                wait(sync, mk.sV, xtg_res)
                for tau4 in range(4):
                    _qs = mk.qd.pre(sync)
                    i = sync.dma_start(out=XTG4[:, tau4],
                                       in_=x2v[:, :, (qg * 4 + tau4) * 128:(qg * 4 + tau4 + 1) * 128])
                    hs_xtg.append(mk.qd.post(i))
            t_xtg = hs_xtg

            for tau in range(4):
                qi = qg * 4 + tau
                pi = tau % 2
                kp = (qi + 1) * 128
                nkc = (kp + 511) // 512

                @block.sync
                def _(sync):
                    wait(sync, mk.sT, qt_free[pi])
                    _qs = mk.qd.pre(sync)
                    i = sync.dma_start(out=QT[pi].rearrange("q (f p) -> q f p", f=8),
                                       in_=qtsv[:, :, qi * 128:(qi + 1) * 128])
                    _h = mk.qd.post(i)
                t_qt = mk.qd.last

                QT3 = QT[pi].rearrange("q (f p) -> q f p", f=8)
                for kc in range(nkc):
                    b = kc % 2
                    klen = min(512, kp - kc * 512)

                    @block.tensor
                    def _(tensor):
                        waith(tensor, t_qt)
                        if qg == 0 and tau == 0 and kc == 0:
                            waith(tensor, t_w)
                        wait(tensor, mk.sV, bankA[b][0])
                        wait(tensor, mk.sS, bankA[b][1])
                        for fc in range(8):
                            i = tensor.matmul(mk.psA[b][:, 0:klen], QT3[:, fc, :],
                                              kT3[:, fc, kc * 512:kc * 512 + klen],
                                              start=(fc == 0), stop=(fc == 7))
                        mk.sT.inc(i)
                    t_sc = mk.sT.n

                    @block.vector
                    def _(vector):
                        selfwait(vector, mk.sV)
                        wait(vector, mk.sT, t_sc)
                        wait(vector, mk.sT, ent_mm)
                        i = vector.tensor_copy(out=SCO[:, kc * 512:kc * 512 + klen],
                                        in_=mk.psA[b][:, 0:klen])
                        mk.sV.inc(i)
                    t_cp = mk.sS.n

                    @block.vector
                    def _(vector):
                        selfwait(vector, mk.sV)
                        wait(vector, mk.sT, t_sc)
                        i = vector.reduce_max(out=mk.negm[:, kc:kc + 1], in_=mk.psA[b][:, 0:klen],
                                              axis=AX.X, negate=True)
                        mk.sV.inc(i)
                    bankA[b] = [mk.sV.n, t_cp]

                @block.vector
                def _(vector):
                    selfwait(vector, mk.sV)
                    wait(vector, mk.sS, mk.sS.n)
                    selfwait(vector, mk.sV)
                    mk.sV.inc(vector.tensor_tensor(out=SCO[:, qi * 128:qi * 128 + 128],
                                                   in0=SCO[:, qi * 128:qi * 128 + 128],
                                                   in1=mk.cmaskb[:, :], op=ALU.add))
                    mk.sV.inc(vector.tensor_reduce(out=mk.nm1[:, 0:1], in_=mk.negm[:, 0:nkc],
                                                   axis=AX.X, op=ALU.min))
                    selfwait(vector, mk.sV)
                    i = vector.tensor_mul(out=mk.ebias[:, 0:1], in0=mk.nm1[:, 0:1],
                                          in1=mk.c32[:, 0:1])
                    mk.sV.inc(i)
                t_msk = mk.sV.n

                @block.scalar
                def _(scalar):
                    selfwait(scalar, mk.sS)
                    wait(scalar, mk.sV, t_msk)
                    wait(scalar, mk.sT, e_free)
                    i = scalar.activation(out=E[:, 0:kp], in_=SCO[:, 0:kp], func=AF.Exp,
                                          scale=mk.c32[:, 0:1], bias=mk.ebias[:, 0:1],
                                          accum_out=mk.edenom[:, 0:1])
                    mk.sS.inc(i)
                t_exp = mk.sS.n
                sco_free = t_exp

                @block.vector
                def _(vector):
                    selfwait(vector, mk.sV)
                    wait(vector, mk.sS, t_exp)
                    mk.sV.inc(vector.reciprocal(out=mk.erd[:, 0:1], in_=mk.edenom[:, 0:1]))
                    selfwait(vector, mk.sV)
                    i = vector.tensor_scalar(out=E[:, 0:kp], in0=E[:, 0:kp],
                                             scalar1=mk.erd[:, 0:1], scalar2=None, op0=ALU.mult)
                    mk.sV.inc(i)
                t_en = mk.sV.n

                nkb = qi + 1
                for w0 in range(0, nkb, 8):
                    ps_ = (w0 // 8) % 2
                    nw = min(8, nkb - w0)

                    @block.tensor
                    def _(tensor):
                        wait(tensor, mk.sV, t_en)
                        if psT_eng[ps_] is not None:
                            wait(tensor, psT_eng[ps_], psT_free[ps_])
                        for u in range(nw):
                            kb = w0 + u
                            i = tensor.transpose(mk.psT[ps_][:, u * 128:(u + 1) * 128],
                                                 E[:, kb * 128:(kb + 1) * 128],
                                                 mk.identb[:, :])
                        mk.sT.inc(i)
                    t_tw = mk.sT.n

                    if ps_ == 0:
                        @block.vector
                        def _(vector):
                            selfwait(vector, mk.sV)
                            wait(vector, mk.sT, t_tw)
                            wait(vector, mk.sS, sco_free)
                            wait(vector, mk.sT, ent_mm)
                            i = vector.tensor_copy(out=SCO[:, w0 * 128:(w0 + nw) * 128],
                                                   in_=mk.psT[ps_][:, 0:nw * 128])
                            mk.sV.inc(i)
                        psT_free[ps_], psT_eng[ps_] = mk.sV.n, mk.sV
                    else:
                        @block.vector
                        def _(vector):
                            selfwait(vector, mk.sV)
                            wait(vector, mk.sT, t_tw)
                            wait(vector, mk.sT, ent_mm)
                            i = vector.tensor_copy(out=SCO[:, w0 * 128:(w0 + nw) * 128],
                                            in_=mk.psT[ps_][:, 0:nw * 128])
                            mk.sV.inc(i)
                        psT_free[ps_], psT_eng[ps_] = mk.sV.n, mk.sV
                e_free = mk.sT.n
                t_ent_v, t_ent_s = mk.sV.n, mk.sS.n

                @block.tensor
                def _(tensor):
                    wait(tensor, mk.sV, t_ent_v)
                    wait(tensor, mk.sS, t_ent_s)
                    wait(tensor, mk.sV, psy_evac)
                    for kb in range(nkb):
                        for half in range(2):
                            i = tensor.matmul(mk.psY[:, half * 512:(half + 1) * 512],
                                              SCO[:, kb * 128:(kb + 1) * 128],
                                              vb3[:, kb, half * 512:(half + 1) * 512],
                                              start=(kb == 0), stop=(kb == nkb - 1))
                    mk.sT.inc(i)
                ent_mm = mk.sT.n

                @block.vector
                def _(vector):
                    selfwait(vector, mk.sV)
                    wait(vector, mk.sT, ent_mm)
                    wait(vector, mk.sT, outb_tp)
                    i = vector.tensor_copy(out=OUTB[:, :], in_=mk.psY[:, 0:1024])
                    mk.sV.inc(i)
                psy_evac = mk.sV.n
                t_outb = mk.sV.n

                for vc in range(8):
                    ps_ = vc % 2

                    @block.tensor
                    def _(tensor):
                        wait(tensor, mk.sV, t_outb)
                        if psT_eng[ps_] is not None:
                            wait(tensor, psT_eng[ps_], psT_free[ps_])
                        i = tensor.transpose(mk.psT[ps_][:, 0:128],
                                             OUTB[:, vc * 128:(vc + 1) * 128],
                                             mk.identb[:, :])
                        mk.sT.inc(i)
                    t_tp = mk.sT.n

                    if ps_ == 0:
                        @block.vector
                        def _(vector):
                            selfwait(vector, mk.sV)
                            wait(vector, mk.sT, t_tp)
                            wait(vector, mk.sT, outtg_mm)
                            i = vector.tensor_copy(
                                out=OUTTG[:, vc * 512 + tau * 128: vc * 512 + (tau + 1) * 128],
                                in_=mk.psT[ps_][:, 0:128])
                            mk.sV.inc(i)
                        psT_free[ps_], psT_eng[ps_] = mk.sV.n, mk.sV
                    else:
                        @block.vector
                        def _(vector):
                            selfwait(vector, mk.sV)
                            wait(vector, mk.sT, t_tp)
                            wait(vector, mk.sT, outtg_mm)
                            i = vector.tensor_copy(
                                out=OUTTG[:, vc * 512 + tau * 128: vc * 512 + (tau + 1) * 128],
                                in_=mk.psT[ps_][:, 0:128])
                            mk.sV.inc(i)
                        psT_free[ps_], psT_eng[ps_] = mk.sV.n, mk.sV
                outb_tp = mk.sT.n
                qt_free[pi] = mk.sT.n
            t_outtg_v, t_outtg_s = mk.sV.n, mk.sS.n

            for mc in range(8):
                b = mc % 2

                @block.tensor
                def _(tensor):
                    wait(tensor, mk.sV, t_outtg_v)
                    wait(tensor, mk.sS, t_outtg_s)
                    if bankPe[b] is not None:
                        wait(tensor, bankPe[b], bankP[b])
                    for vc in range(8):
                        i = tensor.matmul(mk.psP[b][:, 0:512], WOB3[:, vc, mc * 128:(mc + 1) * 128],
                                          OUTTG[:, vc * 512:(vc + 1) * 512],
                                          start=(vc == 0), stop=(vc == 7))
                    mk.sT.inc(i)
                t_y = mk.sT.n

                @block.vector
                def _(vector):
                    selfwait(vector, mk.sV)
                    wait(vector, mk.sT, t_y)
                    waith(vector, t_xtg)
                    ps4 = mk.psP[b][:, 0:512].rearrange("q (t p) -> q t p", t=4)
                    i = vector.tensor_tensor(out=XTG4[:, :, mc, :], in0=ps4,
                                             in1=XTG4[:, :, mc, :], op=ALU.add)
                    mk.sV.inc(i)
                bankP[b], bankPe[b] = mk.sV.n, mk.sV
            outtg_mm = mk.sT.n
            xtg_res = mk.sV.n

            hs_sp = []

            @block.sync
            def _(sync):
                wait(sync, mk.sV, xtg_res)
                for tau4 in range(4):
                    _qs = mk.qd.pre(sync)
                    i = sync.dma_start(out=x3v[:, :, (qg * 4 + tau4) * 128:(qg * 4 + tau4 + 1) * 128],
                                       in_=XTG4[:, tau4])
                    hs_sp.append(mk.qd.post(i))
            xtg_sp = hs_sp


def build_nc(stop_after=None, dbg=None):
    from contextlib import ExitStack
    nc = bass.Bass("TRN2", target_bir_lowering=False)
    es = ExitStack()
    m = MK(nc, es, stop_after)
    m.declare()
    dbg_out = None
    if dbg is not None:
        dbg_out = nc.dram_tensor("dbg", [128, 4096], F32, kind="ExternalOutput").ap()
    with nc.Block() as block:
        m.p0(block)
        phases = []
        phases.append(("p1", lambda: m.p1(block)))
        phases.append(("p2", lambda: m.p2(block)))
        phases.append(("p3a", lambda: m.p3a(block)))
        phases.append(("p3b", lambda: m.p3b(block)))
        phases.append(("p4prep", lambda: m.p4_prep(block)))
        phases.append(("p4", lambda: m.p4(block, m.x1s, m.x2s, prep=False)))
        phases.append(("p5a", lambda: m.p5a(block)))
        phases.append(("p5b", lambda: m.p5b(block)))
        phases.append(("p6", lambda: m.p4(block, m.x3s, m.out, prep=False)))
        for name, fn in phases:
            fn()
            m.barrier(block)
            if stop_after == name:
                break

        if dbg is not None:
            src_map = {
                "xn1": (m.xn1s, BF16), "sq": (m.sqs, BF16), "sv": (m.svs, BF16),
                "x1": (m.x1s, F32), "x2": (m.x2s, F32), "x3": (m.x3s, F32),
                "qts": (m.qts, BF16), "out": (m.out, F32),
                "ksum": (None, None), "w1ss": (m.w1ss, BF16), "a1": (None, None),
            }
            if dbg == "xng2":
                XNGd = m.vw(m.aX, 16384, 128, 4096, BF16)
                HBd = m.vw(m.aX, 24576, 128, 16384, BF16)

                @block.gpsimd
                def _(gpsimd):
                    m.qg.drain(gpsimd)
                    i = gpsimd.dma_start(out=dbg_out[:, 0:2048], in_=XNGd[:, 2048:4096])
                    _h = m.qg.post(i)
                    m.qg.drain(gpsimd)
                    i = gpsimd.dma_start(out=dbg_out[:, 2048:4096], in_=HBd[:, 2048:4096])
                    _h = m.qg.post(i)
            elif dbg == "w1sig":
                @block.gpsimd
                def _(gpsimd):
                    m.qg.drain(gpsimd)
                    i = gpsimd.dma_start(out=dbg_out[:, 0:4096], in_=m.wA()[:, 0:4096])
                    _h = m.qg.post(i)
            elif dbg == "xng_hb":
                XNGd = m.vw(m.aX, 16384, 128, 4096, BF16)
                HBd = m.vw(m.aX, 24576, 128, 16384, BF16)

                @block.gpsimd
                def _(gpsimd):
                    m.qg.drain(gpsimd)
                    i = gpsimd.dma_start(out=dbg_out[:, 0:2048], in_=XNGd[:, 0:2048])
                    _h = m.qg.post(i)
                    m.qg.drain(gpsimd)
                    i = gpsimd.dma_start(out=dbg_out[:, 2048:4096], in_=HBd[:, 0:2048])
                    _h = m.qg.post(i)
            elif dbg == "ksum":
                ks = m.vw(m.aX, 4096, 128, NT * 64, F32)

                @block.sync
                def _(sync):
                    m.qd.drain(sync)
                    i = sync.dma_start(out=dbg_out[:, 0:2048], in_=ks[:, :])
                    _h = m.qd.post(i)
            elif dbg == "a1":
                AROWd = m.vw(m.aX, 53248, 1, 4096, F32)
                ABSBd = m.vw(m.aX, 36864, 128, 4096, BF16)

                @block.sync
                def _(sync):
                    m.qd.drain(sync)
                    i = sync.dma_start(out=dbg_out[:, 0:32], in_=m.al1_sb[:, :])
                    _h = m.qd.post(i)
                    m.qd.drain(sync)
                    i = sync.dma_start(out=dbg_out[:, 32:40], in_=m.al2_sb[:, :])
                    _h = m.qd.post(i)
                    m.qd.drain(sync)
                    i = sync.dma_start(out=dbg_out[0:1, 1024:2048], in_=AROWd[0:1, 0:1024])
                    _h = m.qd.post(i)

                @block.gpsimd
                def _(gpsimd):
                    m.qd.drain(gpsimd)
                    m.qg.drain(gpsimd)
                    i = gpsimd.dma_start(out=dbg_out[:, 2048:2560], in_=ABSBd[:, 0:512])
                    _h = m.qg.post(i)
            else:
                src, sdt = src_map[dbg]
                rows, cols = src.shape
                bounce = m.vw(m.aX, 0, 128, 4096, sdt)
                n_el = rows * cols
                srcf = src.rearrange("a b -> (a b)").rearrange("(n p f) -> n p f", p=128, f=4096)
                nch = n_el // (128 * 4096)
                nch = min(nch, 8)
                dbf = dbg_out.rearrange("p (n f) -> n p f", n=8) if nch > 1 else None

                @block.gpsimd
                def _(gpsimd):
                    for ci in range(nch):
                        wait(gpsimd, m.sg, m.sg.n)
                        wait(gpsimd, m.sd, m.sd.n)
                        m.qg.drain(gpsimd)
                        i = gpsimd.dma_start(out=bounce[:, :], in_=srcf[ci, :, :])
                        _h = m.qg.post(i)
                        wait(gpsimd, m.sg, m.sg.n)
                        if nch > 1:
                            m.qg.drain(gpsimd)
                            i = gpsimd.dma_start(out=dbf[ci, :, 0:512], in_=bounce[:, 0:512])
                        else:
                            m.qg.drain(gpsimd)
                            i = gpsimd.dma_start(out=dbg_out[:, :], in_=bounce[:, :])
                        _h = m.qg.post(i)

        @block.sync
        def _(sync):
            m.qd.drain(sync)
            m.qg.drain(sync)
    es.close()
    return nc


DPERM = np.arange(4096).reshape(64, 64).T.reshape(-1)


def prep_inputs(inputs):
    f = np.ascontiguousarray
    w = {}
    w["wqa"] = f(inputs["L0_attn_a_Wq"].reshape(8, 128, 4096).transpose(1, 0, 2).reshape(128, 32768))
    wk = inputs["L0_attn_a_Wk"][:, DPERM]
    w["wka"] = f(wk.reshape(8, 128, 4096).transpose(1, 0, 2).reshape(128, 32768))
    w["wva"] = f(inputs["L0_attn_a_Wv"].reshape(8, 128, 4096).transpose(1, 0, 2).reshape(128, 32768))
    w["woa"] = f(inputs["L0_attn_a_Wo"].reshape(32, 128, 1024).transpose(1, 0, 2).reshape(128, 32768))
    w["boa"] = f(inputs["L0_attn_a_bo"].reshape(8, 128).T)
    w["wqb"] = f(inputs["L0_attn_b_Wq"].reshape(8, 128, 1024).transpose(1, 0, 2).reshape(128, 8192))
    w["wkb"] = f(inputs["L0_attn_b_Wk"].reshape(8, 128, 1024).transpose(1, 0, 2).reshape(128, 8192))
    w["wvb"] = f(inputs["L0_attn_b_Wv"].reshape(8, 128, 1024).transpose(1, 0, 2).reshape(128, 8192))
    w["wob"] = f(inputs["L0_attn_b_Wo"].reshape(8, 128, 1024).transpose(1, 0, 2).reshape(128, 8192))
    w["w1t"] = f(inputs["L0_ffn_w1"].T.reshape(8, 128, 4096).transpose(1, 0, 2).reshape(128, 32768))
    w["w2t"] = f(inputs["L0_ffn_w2"].T.reshape(32, 128, 1024).transpose(1, 0, 2).reshape(128, 32768))
    w["b1"] = f(inputs["L0_ffn_b1"].reshape(32, 128).T)
    w["b2"] = f(inputs["L0_ffn_b2"].reshape(8, 128).T)
    w["ident"] = np.eye(128, dtype=np.float32)
    cm = np.triu(np.ones((128, 128), dtype=np.float32), 1) * (-1e30)
    w["cmask"] = cm
    return w


def kernel(**inputs):
    w = prep_inputs(inputs)
    x = inputs["L0_attn_a_x"]
    nc = build_nc()
    in_maps = []
    for c in range(8):
        m = dict(w)
        m["x0"] = np.ascontiguousarray(x[c].T)
        in_maps.append(m)
    res = run_bass_kernel_spmd(nc, in_maps, core_ids=list(range(8)))
    outs = [np.ascontiguousarray(r["out"].T) for r in res.results]
    return np.stack(outs, 0)


# version 06
# speedup vs baseline: 1.3164x; 1.1225x over previous
import numpy as np
import ml_dtypes
import concourse.bass as bass
import concourse.mybir as mybir
from concourse.bass_utils import run_bass_kernel_spmd

F32 = mybir.dt.float32
BF16 = mybir.dt.bfloat16
AF = mybir.ActivationFunctionType
ALU = mybir.AluOpType
AX = mybir.AxisListType

NT = 32
LN_LVL = 6
S = 4096
D = 1024
HID = 4096


class Sem:
    def __init__(self, nc, es, name):
        self.h = es.enter_context(nc.semaphore(name))
        self.n = 0

    def inc(self, instr, k=1):
        instr.then_inc(self.h, k)
        self.n += k
        return self.n


SELF_EDGES = True
QUEUE_SERIAL = False


class QSem:

    def __init__(self, nc, es, name, n=4):
        self.sems = [Sem(nc, es, f"{name}{i}") for i in range(n)]
        self.n = n
        self.k = 0

    def pre(self, eng):
        s = self.sems[self.k % self.n]
        wait(eng, s)
        return s

    def post(self, instr):
        s = self.sems[self.k % self.n]
        s.inc(instr, 16)
        self.k += 1
        self.last = (s, s.n)
        return self.last

    def drain(self, eng):
        for s in self.sems:
            wait(eng, s)


def waith(eng, h):
    if not h:
        return
    if isinstance(h, list):
        waiths(eng, h)
    else:
        eng.wait_ge(h[0].h, h[1])


def waiths(eng, hs):
    seen = {}
    for h in hs:
        if h:
            seen[id(h[0])] = (h[0], max(h[1], seen.get(id(h[0]), (None, -1))[1]))
    for s, v in seen.values():
        eng.wait_ge(s.h, v)


def wait(eng, sem, tok=None):
    if tok is None:
        tok = sem.n
    if tok > 0:
        eng.wait_ge(sem.h, tok)


def selfwait(eng, sem):
    if SELF_EDGES:
        wait(eng, sem)


def qwait(eng, sem):
    if QUEUE_SERIAL:
        wait(eng, sem)


class MK:
    def __init__(self, nc, es, stop_after=None):
        self.nc = nc
        self.es = es
        self.stop_after = stop_after

    def declare(self):
        nc, es = self.nc, self.es
        dram = lambda n, s, d, k="ExternalInput": nc.dram_tensor(n, s, d, kind=k).ap()
        self.x0 = dram("x0", [D, S], F32)
        self.wqa = dram("wqa", [128, 8 * HID], F32)
        self.wka = dram("wka", [128, 8 * HID], F32)
        self.wva = dram("wva", [128, 8 * HID], F32)
        self.woa = dram("woa", [128, 32 * D], F32)
        self.boa = dram("boa", [128, 8], F32)
        self.wqb = dram("wqb", [128, 8 * D], F32)
        self.wkb = dram("wkb", [128, 8 * D], F32)
        self.wvb = dram("wvb", [128, 8 * D], F32)
        self.wob = dram("wob", [128, 8 * D], F32)
        self.w1t = dram("w1t", [128, 8 * HID], F32)
        self.w2t = dram("w2t", [128, 32 * D], F32)
        self.b1 = dram("b1", [128, 32], F32)
        self.b2 = dram("b2", [128, 8], F32)
        self.ident = dram("ident", [128, 128], F32)
        self.cmask = dram("cmask", [128, 128], F32)
        self.out = dram("out", [D, S], F32, "ExternalOutput")

        scr = lambda n, s, d: nc.dram_tensor(n, s, d, kind="Internal").ap()
        self.xn1s = scr("xn1s", [D, S], BF16)
        self.sqs = scr("sqs", [64, NT * 8192], BF16)
        self.svs = scr("svs", [64, NT * 8192], BF16)
        self.x1s = scr("x1s", [D, S], F32)
        self.x2s = scr("x2s", [D, S], F32)
        self.x3s = scr("x3s", [D, S], F32)
        self.qts = scr("qts", [128, 8 * S], BF16)
        self.w1ss = scr("w1ss", [128, 8 * HID], BF16)
        self.w2ss = scr("w2ss", [128, 32 * D], BF16)
        self.ar1s = scr("ar1s", [1, 4096], F32)
        self.ar2s = scr("ar2s", [1, 1024], F32)
        self.a1s = scr("a1s", [128, 32], F32)
        self.a2s = scr("a2s", [128, 8], F32)

        sb = lambda n, s, d: es.enter_context(nc.sbuf_tensor(n, s, d))
        self.identb = sb("identb", [128, 128], BF16)
        self.identf = sb("identf", [128, 128], F32)
        self.cmaskb = sb("cmaskb", [128, 128], BF16)
        self.onesb = sb("onesb", [128, 1], BF16)
        self.onesf = sb("onesf", [1, 128], F32)
        self.c1024 = sb("c1024", [1, 1], F32)
        self.cneg = sb("cneg", [1, 1], F32)
        self.epsln = sb("epsln", [1, 1], F32)
        self.eps6 = sb("eps6", [128, 1], F32)
        self.zero128 = sb("zero128", [128, 1], F32)
        self.c32 = sb("c32", [128, 1], F32)
        self.c1024c = sb("c1024c", [128, 1], F32)
        self.c4096c = sb("c4096c", [128, 1], F32)
        self.c0125 = sb("c0125", [128, 1], F32)
        self.cm05 = sb("cm05", [1, 1], F32)
        self.bo_sb = sb("bo_sb", [128, 8], F32)
        self.b1_sb = sb("b1_sb", [128, 32], F32)
        self.b2_sb = sb("b2_sb", [128, 8], F32)
        self.al1_sb = sb("al1_sb", [128, 32], F32)
        self.al2_sb = sb("al2_sb", [128, 8], F32)
        self.qn = sb("qn", [128, 64], F32)
        self.qnr = sb("qnr", [128, 64], F32)
        self.negm = sb("negm", [128, 8], F32)
        self.nm1 = sb("nm1", [128, 1], F32)
        self.ebias = sb("ebias", [128, 1], F32)
        self.edenom = sb("edenom", [128, 1], F32)
        self.erd = sb("erd", [128, 1], F32)
        self.statrow = sb("statrow", [1, 256], F32)
        self.r_m = sb("r_m", [1, 128], F32)
        self.r_ms = sb("r_ms", [1, 128], F32)
        self.r_var = sb("r_var", [1, 128], F32)
        self.r_std = sb("r_std", [1, 128], F32)
        self.r_a = sb("r_a", [1, 128], F32)
        self.r_t = sb("r_t", [1, 128], F32)
        self.r_b = sb("r_b", [1, 128], F32)

        self.aW = sb("aW", [128, 65536], BF16)
        self.aX = sb("aX", [128, 34816], BF16)

        ps = lambda n, s, d: es.enter_context(nc.psum_tensor(n, s, d))
        self.psA = [ps("psA0", [128, 512], F32), ps("psA1", [128, 512], F32)]
        self.psT = [ps("psT0", [128, 1024], BF16), ps("psT1", [128, 1024], BF16)]
        self.psP = [ps("psP0", [128, 512], F32), ps("psP1", [128, 512], F32)]
        self.psY = ps("psY", [128, 1024], F32)

        sm = lambda n: Sem(self.nc, es, n)
        self.sd = sm("sd")
        self.sg = sm("sg")
        self.qd = QSem(self.nc, es, "qd", 4)
        self.qg = QSem(self.nc, es, "qg", 2)
        self.sT = sm("sT")
        self.sV = sm("sV")
        self.sS = sm("sS")
        self.sy = sm("sy")
        self.sp = sm("sp")

    def vw(self, arena, off, parts, cols, dtype):
        assert off % 4 == 0
        nb = cols * (2 if dtype == BF16 else 4) // 2
        a = arena[0:parts, off // 2: off // 2 + nb]
        if dtype != BF16:
            a = a.bitcast(dtype)
        return a

    def wA(self, cols=32768, dtype=BF16, parts=128, off=0):
        return self.vw(self.aW, off, parts, cols, dtype)

    def wB(self, cols=32768, dtype=BF16, parts=128, off=0):
        return self.vw(self.aW, 65536, parts, cols, dtype) if off == 0 else \
            self.vw(self.aW, 65536 + off, parts, cols, dtype)

    def barrier(self, block):
        mk = self

        @block.tensor
        def _(tensor):
            mk.sT.inc(tensor.drain())

        @block.vector
        def _(vector):
            mk.sV.inc(vector.drain())

        @block.scalar
        def _(scalar):
            mk.sS.inc(scalar.drain())

        @block.sync
        def _(sync):
            mk.sy.inc(sync.nop())

        @block.gpsimd
        def _(gpsimd):
            mk.sp.inc(gpsimd.drain())
        t = dict(T=mk.sT.n, V=mk.sV.n, S=mk.sS.n, y=mk.sy.n, p=mk.sp.n)

        @block.tensor
        def _(tensor):
            wait(tensor, mk.sV, t['V']); wait(tensor, mk.sS, t['S'])
            wait(tensor, mk.sy, t['y']); wait(tensor, mk.sp, t['p'])
            mk.qd.drain(tensor); mk.qg.drain(tensor)

        @block.vector
        def _(vector):
            selfwait(vector, mk.sV)
            wait(vector, mk.sT, t['T']); wait(vector, mk.sS, t['S'])
            wait(vector, mk.sy, t['y']); wait(vector, mk.sp, t['p'])
            mk.qd.drain(vector); mk.qg.drain(vector)

        @block.scalar
        def _(scalar):
            selfwait(scalar, mk.sS)
            wait(scalar, mk.sT, t['T']); wait(scalar, mk.sV, t['V'])
            wait(scalar, mk.sy, t['y']); wait(scalar, mk.sp, t['p'])
            mk.qd.drain(scalar); mk.qg.drain(scalar)

        @block.sync
        def _(sync):
            wait(sync, mk.sT, t['T']); wait(sync, mk.sV, t['V'])
            wait(sync, mk.sS, t['S']); wait(sync, mk.sp, t['p'])
            mk.qd.drain(sync); mk.qg.drain(sync)

        @block.gpsimd
        def _(gpsimd):
            wait(gpsimd, mk.sT, t['T']); wait(gpsimd, mk.sV, t['V'])
            wait(gpsimd, mk.sS, t['S']); wait(gpsimd, mk.sy, t['y'])
            mk.qd.drain(gpsimd); mk.qg.drain(gpsimd)

    def p0(self, block):
        mk = self

        @block.sync
        def _(sync):
            for dst, src in ((mk.identf, mk.ident), (mk.bo_sb, mk.boa),
                             (mk.b1_sb, mk.b1), (mk.b2_sb, mk.b2)):
                _qs = mk.qd.pre(sync)
                i = sync.dma_start(out=dst[:, :], in_=src[:, :])
                _h = mk.qd.post(i)

        @block.gpsimd
        def _(gpsimd):
            _qs = mk.qg.pre(gpsimd)
            i = gpsimd.dma_start(out=mk.identb[:], in_=mk.ident[:, :])
            _h = mk.qg.post(i)
            _qs = mk.qg.pre(gpsimd)
            i = gpsimd.dma_start(out=mk.cmaskb[:], in_=mk.cmask[:, :])
            _h = mk.qg.post(i)

        @block.vector
        def _(vector):
            selfwait(vector, mk.sV)
            vector.memset(mk.onesb[:, 0:1], 1.0)
            vector.memset(mk.onesf[0:1, :], 1.0)
            vector.memset(mk.c1024[0:1, 0:1], 1.0 / 1024.0)
            vector.memset(mk.cneg[0:1, 0:1], -1.0)
            vector.memset(mk.epsln[0:1, 0:1], 1e-5)
            vector.memset(mk.eps6[:, 0:1], 1e-6)
            vector.memset(mk.zero128[:, 0:1], 0.0)
            vector.memset(mk.c32[:, 0:1], float(D) ** -0.5)
            vector.memset(mk.c1024c[:, 0:1], 1.0 / 1024.0)
            vector.memset(mk.c4096c[:, 0:1], 1.0 / 4096.0)
            vector.memset(mk.cm05[0:1, 0:1], -0.5)
            i = vector.memset(mk.c0125[:, 0:1], 0.125)
            mk.sV.inc(i)

    def load_w(self, block, dst, src, n_parts=4):
        mk = self
        cols = src.shape[1]
        step = cols // n_parts

        hs = []

        @block.gpsimd
        def _(gpsimd):
            for i_ in range(n_parts):
                _qs = mk.qg.pre(gpsimd)
                ins = gpsimd.dma_start(out=dst[:, i_ * step:(i_ + 1) * step],
                                       in_=src[:, i_ * step:(i_ + 1) * step])
                hs.append(mk.qg.post(ins))
        return hs

    def emit_ln(self, block, x3, xnb_out, extra_wait, out_wait, lnps=None):
        mk = self
        lvl = LN_LVL
        lnps = lnps if lnps is not None else mk.psY
        xbb = self.vw(self.aX, 65536, 128, 1024, BF16)
        xsqb = self.vw(self.aX, 67584, 128, 1024, BF16)
        xbb3 = xbb.rearrange("q (c p) -> q c p", c=8)
        xsq3 = xsqb.rearrange("q (c p) -> q c p", c=8)

        @block.vector
        def _(vector):
            selfwait(vector, mk.sV)
            extra_wait(vector)
            i = vector.tensor_copy(out=xbb3, in_=x3)
            mk.sV.inc(i)
        t_xbb = mk.sV.n

        @block.gpsimd
        def _(gpsimd):
            selfwait(gpsimd, mk.sp)
            wait(gpsimd, mk.sV, t_xbb)
            i = gpsimd.tensor_mul(out=xsqb[:, :], in0=xbb[:, :], in1=xbb[:, :])
            mk.sp.inc(i)
        t_xsq = mk.sp.n
        if lvl < 2:
            return mk.sV.n

        @block.tensor
        def _(tensor):
            wait(tensor, mk.sV, t_xbb)
            wait(tensor, mk.sp, t_xsq)
            xb3 = xbb.rearrange("q (c p) -> q c p", c=8)
            xs3 = xsqb.rearrange("q (c p) -> q c p", c=8)
            for c in range(8):
                tensor.matmul(lnps[0:1, 0:128], mk.onesb[:, 0:1], xb3[:, c, :],
                              start=(c == 0), stop=(c == 7))
            for c in range(8):
                i = tensor.matmul(lnps[0:1, 128:256], mk.onesb[:, 0:1], xs3[:, c, :],
                                  start=(c == 0), stop=(c == 7))
            mk.sT.inc(i)
        t_stats = mk.sT.n

        @block.vector
        def _(vector):
            selfwait(vector, mk.sV)
            wait(vector, mk.sT, t_stats)
            mk.sV.inc(vector.tensor_scalar(out=mk.r_m[0:1, :], in0=mk.psY[0:1, 0:128],
                                           scalar1=mk.c1024[0:1, 0:1], scalar2=None, op0=ALU.mult))
            selfwait(vector, mk.sV)
            mk.sV.inc(vector.tensor_mul(out=mk.r_ms[0:1, :], in0=mk.r_m[0:1, :], in1=mk.r_m[0:1, :]))
            selfwait(vector, mk.sV)
            i = vector.scalar_tensor_tensor(out=mk.r_var[0:1, :], in0=mk.psY[0:1, 128:256],
                                            scalar=mk.c1024[0:1, 0:1], in1=mk.r_ms[0:1, :],
                                            op0=ALU.mult, op1=ALU.subtract)
            mk.sV.inc(i)
        t_var = mk.sV.n
        if lvl < 4:
            return mk.sV.n

        @block.scalar
        def _(scalar):
            selfwait(scalar, mk.sS)
            wait(scalar, mk.sV, t_var)
            mk.sS.inc(scalar.activation(out=mk.r_std[0:1, :], in_=mk.r_var[0:1, :], func=AF.Ln,
                                        bias=mk.epsln[0:1, 0:1]))
            selfwait(scalar, mk.sS)
            i = scalar.activation(out=mk.r_a[0:1, :], in_=mk.r_std[0:1, :], func=AF.Exp,
                                  scale=mk.cm05[0:1, 0:1])
            mk.sS.inc(i)
        t_std = mk.sS.n

        @block.vector
        def _(vector):
            selfwait(vector, mk.sV)
            i = vector.tensor_scalar(out=mk.r_b[0:1, :], in0=mk.r_m[0:1, :],
                                     scalar1=mk.cneg[0:1, 0:1], scalar2=None, op0=ALU.mult)
            mk.sV.inc(i)
        t_ab = mk.sV.n

        @block.tensor
        def _(tensor):
            wait(tensor, mk.sV, t_ab)
            wait(tensor, mk.sS, t_std)
            tensor.matmul(lnps[:, 256:384], mk.onesf[0:1, :], mk.r_a[0:1, :],
                          start=True, stop=True)
            i = tensor.matmul(lnps[:, 384:512], mk.onesf[0:1, :], mk.r_b[0:1, :],
                              start=True, stop=True)
            mk.sT.inc(i)
        t_rep = mk.sT.n
        if lvl < 6:
            return mk.sV.n

        @block.vector
        def _(vector):
            selfwait(vector, mk.sV)
            wait(vector, mk.sT, t_rep)
            out_wait(vector)
            arep = lnps[:, 256:384].unsqueeze(1).broadcast_to([128, 8, 128])
            brep = lnps[:, 384:512].unsqueeze(1).broadcast_to([128, 8, 128])
            mk.sV.inc(vector.tensor_tensor(out=xnb_out, in0=x3, in1=brep, op=ALU.add))
            selfwait(vector, mk.sV)
            i = vector.tensor_tensor(out=xnb_out, in0=xnb_out, in1=arep, op=ALU.mult)
            mk.sV.inc(i)
        return mk.sV.n

    def ln_a(self, block, x3, xw, lnps, prev_norm=0):
        mk = self
        xbb = self.vw(self.aX, 65536, 128, 1024, BF16)
        xsqb = self.vw(self.aX, 67584, 128, 1024, BF16)
        xbb3 = xbb.rearrange("q (c p) -> q c p", c=8)

        @block.vector
        def _(vector):
            selfwait(vector, mk.sV)
            for w in xw:
                if isinstance(w, tuple) or isinstance(w, list):
                    waith(vector, w)
            i = vector.tensor_copy(out=xbb3, in_=x3)
            mk.sV.inc(i)
        t_xbb = mk.sV.n

        @block.gpsimd
        def _(gpsimd):
            selfwait(gpsimd, mk.sp)
            wait(gpsimd, mk.sV, t_xbb)
            i = gpsimd.tensor_mul(out=xsqb[:, :], in0=xbb[:, :], in1=xbb[:, :])
            mk.sp.inc(i)
        t_xsq = mk.sp.n

        @block.tensor
        def _(tensor):
            wait(tensor, mk.sV, t_xbb)
            wait(tensor, mk.sp, t_xsq)
            wait(tensor, mk.sV, prev_norm)
            xb3 = xbb.rearrange("q (c p) -> q c p", c=8)
            xs3 = xsqb.rearrange("q (c p) -> q c p", c=8)
            for c in range(8):
                tensor.matmul(lnps[0:1, 0:128], mk.onesb[:, 0:1], xb3[:, c, :],
                              start=(c == 0), stop=(c == 7))
            for c in range(8):
                i = tensor.matmul(lnps[0:1, 128:256], mk.onesb[:, 0:1], xs3[:, c, :],
                                  start=(c == 0), stop=(c == 7))
            mk.sT.inc(i)
        t_stats = mk.sT.n

        @block.vector
        def _(vector):
            selfwait(vector, mk.sV)
            wait(vector, mk.sT, t_stats)
            mk.sV.inc(vector.tensor_scalar(out=mk.r_m[0:1, :], in0=lnps[0:1, 0:128],
                                           scalar1=mk.c1024[0:1, 0:1], scalar2=None, op0=ALU.mult))
            selfwait(vector, mk.sV)
            mk.sV.inc(vector.tensor_mul(out=mk.r_ms[0:1, :], in0=mk.r_m[0:1, :], in1=mk.r_m[0:1, :]))
            selfwait(vector, mk.sV)
            mk.sV.inc(vector.scalar_tensor_tensor(out=mk.r_var[0:1, :], in0=lnps[0:1, 128:256],
                                                  scalar=mk.c1024[0:1, 0:1], in1=mk.r_ms[0:1, :],
                                                  op0=ALU.mult, op1=ALU.subtract))
            selfwait(vector, mk.sV)
            i = vector.tensor_scalar(out=mk.r_b[0:1, :], in0=mk.r_m[0:1, :],
                                     scalar1=mk.cneg[0:1, 0:1], scalar2=None, op0=ALU.mult)
            mk.sV.inc(i)
        t_var = mk.sV.n

        @block.scalar
        def _(scalar):
            selfwait(scalar, mk.sS)
            wait(scalar, mk.sV, t_var)
            mk.sS.inc(scalar.activation(out=mk.r_std[0:1, :], in_=mk.r_var[0:1, :], func=AF.Ln,
                                        bias=mk.epsln[0:1, 0:1]))
            selfwait(scalar, mk.sS)
            i = scalar.activation(out=mk.r_a[0:1, :], in_=mk.r_std[0:1, :], func=AF.Exp,
                                  scale=mk.cm05[0:1, 0:1])
            mk.sS.inc(i)
        t_std = mk.sS.n
        return dict(t_xbb=t_xbb, t_xsq=t_xsq, t_ab=t_var, t_std=t_std)

    def ln_rep(self, block, toks, lnps, prev_norm):
        mk = self

        @block.tensor
        def _(tensor):
            wait(tensor, mk.sV, toks["t_ab"])
            wait(tensor, mk.sS, toks["t_std"])
            wait(tensor, mk.sV, prev_norm)
            tensor.matmul(lnps[:, 256:384], mk.onesf[0:1, :], mk.r_a[0:1, :],
                          start=True, stop=True)
            i = tensor.matmul(lnps[:, 384:512], mk.onesf[0:1, :], mk.r_b[0:1, :],
                              start=True, stop=True)
            mk.sT.inc(i)
        return mk.sT.n

    def ln_norm(self, block, x3, xnb_out, lnps, t_rep, out_waits):
        mk = self

        @block.vector
        def _(vector):
            selfwait(vector, mk.sV)
            wait(vector, mk.sT, t_rep)
            for w in out_waits:
                w(vector)
            arep = lnps[:, 256:384].unsqueeze(1).broadcast_to([128, 8, 128])
            brep = lnps[:, 384:512].unsqueeze(1).broadcast_to([128, 8, 128])
            mk.sV.inc(vector.tensor_tensor(out=xnb_out, in0=x3, in1=brep, op=ALU.add))
            selfwait(vector, mk.sV)
            i = vector.tensor_tensor(out=xnb_out, in0=xnb_out, in1=arep, op=ALU.mult)
            mk.sV.inc(i)
        return mk.sV.n

    def p1(self, block):
        mk = self
        t_wk = self.load_w(block, self.wA(), mk.wka)
        self.t_wq = self.load_w(block, self.wB(), mk.wqa)
        wk3 = self.wA().rearrange("q (c f) -> q c f", c=8)
        ksum = self.vw(self.aX, 4096, 128, NT * 64, F32)
        xts = [self.vw(self.aX, 12288, 128, 1024, F32),
               self.vw(self.aX, 16384, 128, 1024, F32)]
        xnbs = [self.vw(self.aX, 20480, 128, 1024, BF16),
                self.vw(self.aX, 22528, 128, 1024, BF16)]
        kst = self.vw(self.aX, 24576, 128, 512, F32)
        x0v = mk.x0.rearrange("(c p) s -> p c s", p=128)
        xn1v = mk.xn1s.rearrange("(c p) s -> p c s", p=128)
        lnps_par = [mk.psY[:, 0:512], mk.psY[:, 512:1024]]

        xt_free = [0, 0]
        xt_freep = [0, 0]
        x_ld = [None, None]
        xnb_free = [0, 0]
        xnb_spill = [None, None]
        t_norm = [0, 0]
        bankV = [0, 0, 0, 0]
        pb4 = [mk.psA[0], mk.psA[1], mk.psP[0], mk.psP[1]]

        def issue_x_load(tt):
            pi2 = tt % 2

            @block.sync
            def _(sync):
                wait(sync, mk.sV, xt_free[pi2])
                wait(sync, mk.sp, xt_freep[pi2])
                _qs = mk.qd.pre(sync)
                i = sync.dma_start(out=xts[pi2].rearrange("q (c p) -> q c p", c=8),
                                   in_=x0v[:, :, tt * 128:(tt + 1) * 128])
                x_ld[pi2] = mk.qd.post(i)

        def emit_lnA(tt):
            pi2 = tt % 2
            x3 = xts[pi2].rearrange("q (c p) -> q c p", c=8)
            toks = self.ln_a(block, x3, [x_ld[pi2]], lnps_par[pi2], prev_norm=t_norm[pi2])
            xt_free[pi2] = toks["t_xbb"]
            xt_freep[pi2] = toks["t_xsq"]
            return toks

        def emit_lnB(tt, toks):
            pi2 = tt % 2
            t_rep = self.ln_rep(block, toks, lnps_par[pi2], t_norm[pi2])
            x3 = xts[pi2].rearrange("q (c p) -> q c p", c=8)
            xn_dst = xnbs[pi2].rearrange("q (c p) -> q c p", c=8)

            def w1(eng, _a=xnb_free[pi2]):
                wait(eng, mk.sT, _a)

            def w2(eng, _b=xnb_spill[pi2]):
                waith(eng, _b)

            t_n = self.ln_norm(block, x3, xn_dst, lnps_par[pi2], t_rep, [w1, w2])
            t_norm[pi2] = t_n
            xt_free[pi2] = max(xt_free[pi2], t_n)
            return t_n

        issue_x_load(0)
        issue_x_load(1)
        toks0 = self.ln_a(block, xts[0].rearrange("q (c p) -> q c p", c=8), [x_ld[0]], lnps_par[0])
        xt_free[0] = toks0["t_xbb"]
        xt_freep[0] = toks0["t_xsq"]
        t_norm[0] = emit_lnB(0, toks0)
        toks_next = None
        for t in range(NT):
            pi = t % 2
            if t + 2 < NT:
                issue_x_load(t + 2)

            xnb3 = xnbs[pi].rearrange("q (c p) -> q c p", c=8)
            for nck in range(8):
                b = nck % 4

                if nck == 3 and t + 1 < NT:
                    toks_next = emit_lnA(t + 1)
                if nck == 6 and t + 1 < NT:
                    t_norm[1 - pi] = emit_lnB(t + 1, toks_next)

                @block.tensor
                def _(tensor):
                    if t == 0 and nck == 0:
                        waith(tensor, t_wk)
                    if nck == 0:
                        wait(tensor, mk.sV, t_norm[pi])
                    wait(tensor, mk.sV, bankV[b])
                    for c in range(8):
                        i = tensor.matmul(pb4[b][:, 0:512], xnb3[:, c, :],
                                          wk3[:, c, nck * 512:(nck + 1) * 512],
                                          start=(c == 0), stop=(c == 7))
                    mk.sT.inc(i)
                t_proj = mk.sT.n

                @block.vector
                def _(vector):
                    selfwait(vector, mk.sV)
                    wait(vector, mk.sT, t_proj)
                    mk.sV.inc(vector.tensor_scalar(out=kst[:, 0:512], in0=pb4[b][:, 0:512],
                                                   scalar1=mk.zero128[:, 0:1], scalar2=None, op0=ALU.max))
                    selfwait(vector, mk.sV)
                    i = vector.reduce_sum(out=ksum[:, t * 64 + nck * 8: t * 64 + nck * 8 + 8],
                                          in_=kst.rearrange("q (a b) -> q a b", b=64),
                                          axis=AX.X)
                    mk.sV.inc(i)
                bankV[b] = mk.sV.n
            xnb_free[pi] = mk.sT.n

            @block.sync
            def _(sync):
                wait(sync, mk.sV, t_norm[pi])
                _qs = mk.qd.pre(sync)
                i = sync.dma_start(out=xn1v[:, :, t * 128:(t + 1) * 128],
                                   in_=xnbs[pi].rearrange("q (c p) -> q c p", c=8))
                xnb_spill[pi] = mk.qd.post(i)

        @block.vector
        def _(vector):
            selfwait(vector, mk.sV)
            i = vector.tensor_scalar(out=ksum[:, :], in0=ksum[:, :],
                                     scalar1=mk.c0125[:, 0:1], scalar2=None, op0=ALU.mult)
            mk.sV.inc(i)
        self.t_ksum = mk.sV.n

    def p2(self, block):
        mk = self
        self.t_wv = self.load_w(block, self.wA(), mk.wva)
        wq3 = self.wB().rearrange("q (c f) -> q c f", c=8)
        ksum = self.vw(self.aX, 4096, 128, NT * 64, F32)
        xnbs = [self.vw(self.aX, 0, 128, 1024, BF16),
                self.vw(self.aX, 2048, 128, 1024, BF16)]
        qp = self.vw(self.aX, 12288, 128, 4096, F32)
        qb = self.vw(self.aX, 28672, 128, 4096, BF16)
        sqt = [self.vw(self.aX, 36864, 64, 8192, BF16),
               self.vw(self.aX, 53248, 64, 8192, BF16)]
        xn1v = mk.xn1s.rearrange("(c p) s -> p c s", p=128)

        xnb_ld = [0, 0]
        xnb_free = [0, 0]
        bankV = [0, 0, 0, 0]
        pb4 = [mk.psA[0], mk.psA[1], mk.psP[0], mk.psP[1]]
        t_qb_prev = 0
        qb_mm = 0
        psT_free = [0, 0]
        psT_eng = [None, None]
        sq_spill = [0, 0]
        def issue_xn_load(tt):
            pi2 = tt % 2

            @block.sync
            def _(sync):
                wait(sync, mk.sT, xnb_free[pi2])
                _qs = mk.qd.pre(sync)
                i = sync.dma_start(out=xnbs[pi2].rearrange("q (c p) -> q c p", c=8),
                                   in_=xn1v[:, :, tt * 128:(tt + 1) * 128])
                xnb_ld[pi2] = mk.qd.post(i)

        issue_xn_load(0)
        for t in range(NT + 1):
            pi = t % 2
            if t < NT:
                if t + 1 < NT:
                    issue_xn_load(t + 1)

                xnb3 = xnbs[pi].rearrange("q (c p) -> q c p", c=8)
                for nck in range(8):
                    b = nck % 4

                    @block.tensor
                    def _(tensor):
                        if nck == 0:
                            waith(tensor, xnb_ld[pi])
                        if t == 0 and nck == 0:
                            waith(tensor, self.t_wq)
                            wait(tensor, mk.sV, self.t_ksum)
                        wait(tensor, mk.sV, bankV[b])
                        for c in range(8):
                            i = tensor.matmul(pb4[b][:, 0:512], xnb3[:, c, :],
                                              wq3[:, c, nck * 512:(nck + 1) * 512],
                                              start=(c == 0), stop=(c == 7))
                        mk.sT.inc(i)
                    t_proj = mk.sT.n

                    @block.vector
                    def _(vector):
                        selfwait(vector, mk.sV)
                        wait(vector, mk.sT, t_proj)
                        ks = ksum[:, t * 64:(t + 1) * 64].unsqueeze(1).broadcast_to([128, 8, 64])
                        mk.sV.inc(vector.scalar_tensor_tensor(
                            out=qp[:, nck * 512:(nck + 1) * 512].rearrange("q (a b) -> q a b", b=64),
                            in0=pb4[b][:, 0:512].rearrange("q (a b) -> q a b", b=64),
                            scalar=mk.zero128[:, 0:1], in1=ks, op0=ALU.max, op1=ALU.mult))
                        selfwait(vector, mk.sV)
                        i = vector.reduce_sum(out=mk.qn[:, nck * 8:(nck + 1) * 8],
                                              in_=qp[:, nck * 512:(nck + 1) * 512]
                                              .rearrange("q (a b) -> q a b", b=64),
                                              axis=AX.X)
                        mk.sV.inc(i)
                    bankV[b] = mk.sV.n
                xnb_free[pi] = mk.sT.n


            if t > 0:
                tp = t - 1
                pp = tp % 2
                for w in range(8):
                    ps_ = w % 2

                    @block.tensor
                    def _(tensor):
                        if w == 0:
                            wait(tensor, mk.sV, t_qb_prev)
                        if psT_eng[ps_] is not None:
                            wait(tensor, psT_eng[ps_], psT_free[ps_])
                        for u in range(8):
                            i = tensor.transpose(mk.psT[ps_][0:64, u * 128:(u + 1) * 128],
                                                 qb[:, (w * 8 + u) * 64:(w * 8 + u + 1) * 64],
                                                 mk.identb[:, :])
                        mk.sT.inc(i)
                    t_w = mk.sT.n

                    @block.scalar
                    def _(scalar):
                        selfwait(scalar, mk.sS)
                        wait(scalar, mk.sT, t_w)
                        waith(scalar, sq_spill[pp])
                        i = scalar.copy(out=sqt[pp][:, w * 1024:(w + 1) * 1024],
                                        in_=mk.psT[ps_][0:64, :])
                        mk.sS.inc(i)
                    psT_free[ps_], psT_eng[ps_] = mk.sS.n, mk.sS
                qb_mm = mk.sT.n

                @block.sync
                def _(sync):
                    wait(sync, mk.sS, mk.sS.n)
                    _qs = mk.qd.pre(sync)
                    i = sync.dma_start(out=mk.sqs[:, tp * 8192:(tp + 1) * 8192],
                                       in_=sqt[pp][:, :])
                    _h = mk.qd.post(i)
                sq_spill[pp] = mk.qd.last

            if t < NT:
                @block.vector
                def _(vector):
                    selfwait(vector, mk.sV)
                    mk.sV.inc(vector.tensor_scalar(out=mk.qnr[:, :], in0=mk.qn[:, :],
                                                   scalar1=mk.eps6[:, 0:1], scalar2=None, op0=ALU.add))
                    selfwait(vector, mk.sV)
                    mk.sV.inc(vector.reciprocal(out=mk.qnr[:, :], in_=mk.qnr[:, :]))
                    selfwait(vector, mk.sV)
                    wait(vector, mk.sT, qb_mm)
                    qr = mk.qnr[:, :].unsqueeze(2).broadcast_to([128, 64, 64])
                    i = vector.tensor_tensor(out=qb.rearrange("q (a b) -> q a b", b=64),
                                             in0=qp.rearrange("q (a b) -> q a b", b=64),
                                             in1=qr, op=ALU.mult)
                    mk.sV.inc(i)
                t_qb_prev = mk.sV.n

    def p3a(self, block):
        mk = self
        self.t_wo = self.load_w(block, self.wB(), mk.woa)
        wv3 = self.wA().rearrange("q (c f) -> q c f", c=8)
        entry_sV, entry_sS = mk.sV.n, mk.sS.n
        xnbs = [self.vw(self.aX, 0, 128, 1024, BF16),
                self.vw(self.aX, 2048, 128, 1024, BF16)]
        vb = self.vw(self.aX, 28672, 128, 4096, BF16)
        svt = [self.vw(self.aX, 36864, 64, 8192, BF16),
               self.vw(self.aX, 53248, 64, 8192, BF16)]
        xn1v = mk.xn1s.rearrange("(c p) s -> p c s", p=128)

        xnb_ld = [0, 0]
        xnb_free = [0, 0]
        bankE = [0, 0, 0, 0]
        bankEng = [None, None, None, None]
        pb4 = [mk.psA[0], mk.psA[1], mk.psP[0], mk.psP[1]]
        vb_done_v = 0
        vb_done_s = 0
        vb_mm = 0
        psT_free = [0, 0]
        psT_eng = [None, None]
        sv_spill = [0, 0]
        xnb_ld_d = {}

        def issue_v_load(tt):
            pi2 = tt % 2

            @block.sync
            def _(sync):
                wait(sync, mk.sT, xnb_free[pi2])
                _qs = mk.qd.pre(sync)
                i = sync.dma_start(out=xnbs[pi2].rearrange("q (c p) -> q c p", c=8),
                                   in_=xn1v[:, :, tt * 128:(tt + 1) * 128])
                xnb_ld[pi2] = mk.qd.post(i)

        for t in range(NT + 1):
            pi = t % 2

            if t > 0:
                tp = t - 1
                pp = tp % 2
                tv, ts_ = vb_done_v, vb_done_s
                for w in range(8):
                    ps_ = w % 2

                    @block.tensor
                    def _(tensor):
                        if w == 0:
                            wait(tensor, mk.sV, tv)
                            wait(tensor, mk.sS, ts_)
                        if psT_eng[ps_] is not None:
                            wait(tensor, psT_eng[ps_], psT_free[ps_])
                        for u in range(8):
                            i = tensor.transpose(mk.psT[ps_][0:64, u * 128:(u + 1) * 128],
                                                 vb[:, (w * 8 + u) * 64:(w * 8 + u + 1) * 64],
                                                 mk.identb[:, :])
                        mk.sT.inc(i)
                    t_w = mk.sT.n

                    @block.scalar
                    def _(scalar):
                        selfwait(scalar, mk.sS)
                        wait(scalar, mk.sT, t_w)
                        waith(scalar, sv_spill[pp])
                        i = scalar.copy(out=svt[pp][:, w * 1024:(w + 1) * 1024],
                                        in_=mk.psT[ps_][0:64, :])
                        mk.sS.inc(i)
                    psT_free[ps_], psT_eng[ps_] = mk.sS.n, mk.sS
                vb_mm = mk.sT.n

                @block.sync
                def _(sync):
                    wait(sync, mk.sS, mk.sS.n)
                    _qs = mk.qd.pre(sync)
                    i = sync.dma_start(out=mk.svs[:, tp * 8192:(tp + 1) * 8192],
                                       in_=svt[pp][:, :])
                    _h = mk.qd.post(i)
                sv_spill[pp] = mk.qd.last

            if t == 0:
                issue_v_load(0)
            if t + 1 < NT:
                issue_v_load(t + 1)
            if t < NT:

                xnb3 = xnbs[pi].rearrange("q (c p) -> q c p", c=8)
                for nck in range(8):
                    b = nck % 4

                    @block.tensor
                    def _(tensor):
                        if nck == 0:
                            waith(tensor, xnb_ld[pi])
                        if t == 0 and nck == 0:
                            waith(tensor, self.t_wv)
                            wait(tensor, mk.sV, entry_sV)
                            wait(tensor, mk.sS, entry_sS)
                        if bankEng[b] is not None:
                            wait(tensor, bankEng[b], bankE[b])
                        for c in range(8):
                            i = tensor.matmul(pb4[b][:, 0:512], xnb3[:, c, :],
                                              wv3[:, c, nck * 512:(nck + 1) * 512],
                                              start=(c == 0), stop=(c == 7))
                        mk.sT.inc(i)
                    t_proj = mk.sT.n

                    if nck % 2 == 0:
                        @block.vector
                        def _(vector):
                            selfwait(vector, mk.sV)
                            wait(vector, mk.sT, t_proj)
                            if nck == 0:
                                wait(vector, mk.sT, vb_mm)
                            i = vector.tensor_copy(out=vb[:, nck * 512:(nck + 1) * 512],
                                                   in_=pb4[b][:, 0:512])
                            mk.sV.inc(i)
                        bankE[b], bankEng[b] = mk.sV.n, mk.sV
                    else:
                        @block.scalar
                        def _(scalar):
                            selfwait(scalar, mk.sS)
                            wait(scalar, mk.sT, t_proj)
                            if nck == 1:
                                wait(scalar, mk.sT, vb_mm)
                            i = scalar.copy(out=vb[:, nck * 512:(nck + 1) * 512],
                                            in_=pb4[b][:, 0:512])
                            mk.sS.inc(i)
                        bankE[b], bankEng[b] = mk.sS.n, mk.sS
                xnb_free[pi] = mk.sT.n
                vb_done_v, vb_done_s = mk.sV.n, mk.sS.n

    def p3b(self, block):
        mk = self
        entry_sV, entry_sS = mk.sV.n, mk.sS.n
        wo3 = self.wB().rearrange("q (g m) -> q g m", g=32)
        sqt = [self.wA(8192, BF16, 64, 0), self.wA(8192, BF16, 64, 16384)]
        svt = [self.wA(8192, BF16, 64, 32768), self.wA(8192, BF16, 64, 49152)]
        s2d = [self.vw(self.aX, 0, 128, 8192, BF16),
               self.vw(self.aX, 16384, 128, 8192, BF16)]
        xts = [self.vw(self.aX, 32768, 128, 1024, F32),
               self.vw(self.aX, 36864, 128, 1024, F32)]
        xos = [self.vw(self.aX, 40960, 128, 1024, F32),
               self.vw(self.aX, 45056, 128, 1024, F32)]
        yb = self.vw(self.aX, 49152, 128, 1024, F32)
        x0v = mk.x0.rearrange("(c p) s -> p c s", p=128)
        x1v = mk.x1s.rearrange("(c p) s -> p c s", p=128)

        sq_free = [0, 0]
        x_free = [0, 0]
        s2_mm = [0, 0]
        xo_spill = [0, 0]
        bankP = [0, 0]
        bankPe = [None, None]
        psy_sc = 0
        psy_tt = 0
        psa_free = [0, 0]
        psa_eng = [None, None]
        ld_h = [None, None]

        def issue_p3b_loads(tt):
            pi2 = tt % 2
            hs_ld = []

            @block.sync
            def _(sync):
                wait(sync, mk.sT, sq_free[pi2])
                _qs = mk.qd.pre(sync)
                i = sync.dma_start(out=sqt[pi2][:, :], in_=mk.sqs[:, tt * 8192:(tt + 1) * 8192])
                hs_ld.append(mk.qd.post(i))
                _qs = mk.qd.pre(sync)
                i = sync.dma_start(out=svt[pi2][:, :], in_=mk.svs[:, tt * 8192:(tt + 1) * 8192])
                hs_ld.append(mk.qd.post(i))
                wait(sync, mk.sV, x_free[pi2])
                _qs = mk.qd.pre(sync)
                i = sync.dma_start(out=xts[pi2].rearrange("q (c p) -> q c p", c=8),
                                   in_=x0v[:, :, tt * 128:(tt + 1) * 128])
                hs_ld.append(mk.qd.post(i))
            ld_h[pi2] = hs_ld

        for t in range(NT):
            pi = t % 2

            if t == 0:
                issue_p3b_loads(0)
            if t + 1 < NT:
                issue_p3b_loads(t + 1)
            t_loads = ld_h[pi]

            sv3 = svt[pi].rearrange("d (j p) -> d j p", p=128)
            sq3 = sqt[pi].rearrange("d (i p) -> d i p", p=128)
            for pg in range(16):
                b = pg % 2

                @block.tensor
                def _(tensor):
                    waith(tensor, t_loads)
                    if t == 0 and pg < 2:
                        wait(tensor, mk.sV, entry_sV)
                        wait(tensor, mk.sS, entry_sS)
                        waith(tensor, self.t_wo)
                    if bankPe[b] is not None:
                        wait(tensor, bankPe[b], bankP[b])
                    for u in range(8):
                        p = pg * 8 + u
                        lhsT = sv3[:, :, p]
                        rhs = sq3[:, :, p]
                        tensor.matmul(mk.psP[b][0:64, u * 64:(u + 1) * 64], lhsT, rhs,
                                      start=True, stop=True)
                        i = tensor.matmul(mk.psP[b][64:128, u * 64:(u + 1) * 64], lhsT, rhs,
                                          start=True, stop=True, tile_position=(0, 64))
                    mk.sT.inc(i)
                t_pg = mk.sT.n

                if pg % 2 == 0:
                    @block.vector
                    def _(vector):
                        selfwait(vector, mk.sV)
                        wait(vector, mk.sT, t_pg)
                        wait(vector, mk.sT, s2_mm[pi])
                        i = vector.tensor_copy(out=s2d[pi][:, pg * 512:(pg + 1) * 512],
                                               in_=mk.psP[b][:, :])
                        mk.sV.inc(i)
                    bankP[b], bankPe[b] = mk.sV.n, mk.sV
                else:
                    @block.vector
                    def _(vector):
                        selfwait(vector, mk.sV)
                        wait(vector, mk.sT, t_pg)
                        wait(vector, mk.sT, s2_mm[pi])
                        i = vector.tensor_copy(out=s2d[pi][:, pg * 512:(pg + 1) * 512],
                                        in_=mk.psP[b][:, :])
                        mk.sV.inc(i)
                    bankP[b], bankPe[b] = mk.sV.n, mk.sV
            sq_free[pi] = mk.sT.n
            t_s2_v, t_s2_s = mk.sV.n, mk.sS.n

            s2e = s2d[pi][0:64, :].rearrange("j (p i) -> j p i", i=64)
            s2o = s2d[pi][64:128, :].rearrange("j (p i) -> j p i", i=64)
            for h in range(2):
                @block.tensor
                def _(tensor):
                    wait(tensor, mk.sV, t_s2_v)
                    wait(tensor, mk.sS, t_s2_s)
                    wait(tensor, mk.sV, psy_sc)
                    wait(tensor, mk.sV, psy_tt)
                    for ip in range(32):
                        i0, i1 = 2 * ip, 2 * ip + 1
                        tensor.matmul(mk.psY[:, 0:512], s2e[:, :, i0],
                                      wo3[0:64, i0 // 2, h * 512:(h + 1) * 512],
                                      start=(ip == 0), stop=(ip == 31))
                        i = tensor.matmul(mk.psY[:, 512:1024], s2o[:, :, i1],
                                          wo3[64:128, i1 // 2, h * 512:(h + 1) * 512],
                                          start=(ip == 0), stop=(ip == 31))
                    mk.sT.inc(i)
                t_wo_mm = mk.sT.n

                @block.vector
                def _(vector):
                    selfwait(vector, mk.sV)
                    wait(vector, mk.sT, t_wo_mm)
                    i = vector.tensor_copy(out=yb[:, h * 512:(h + 1) * 512], in_=mk.psY[:, 0:512])
                    mk.sV.inc(i)
                psy_sc = mk.sS.n

                @block.vector
                def _(vector):
                    selfwait(vector, mk.sV)
                    i = vector.tensor_tensor(out=yb[:, h * 512:(h + 1) * 512],
                                             in0=yb[:, h * 512:(h + 1) * 512],
                                             in1=mk.psY[:, 512:1024], op=ALU.add)
                    mk.sV.inc(i)
                psy_tt = mk.sV.n
            s2_mm[pi] = mk.sT.n
            t_yb = mk.sV.n

            for c in range(8):
                a = c % 2
                slot = c // 2

                @block.tensor
                def _(tensor):
                    wait(tensor, mk.sV, t_yb)
                    if psa_eng[a] is not None:
                        wait(tensor, psa_eng[a], psa_free[a])
                    i = tensor.matmul(mk.psA[a][:, slot * 128:(slot + 1) * 128],
                                      yb[:, c * 128:(c + 1) * 128], mk.identf[:, :],
                                      start=True, stop=True)
                    mk.sT.inc(i)
                t_tp = mk.sT.n

                @block.vector
                def _(vector):
                    selfwait(vector, mk.sV)
                    wait(vector, mk.sT, t_tp)
                    waith(vector, t_loads)
                    waith(vector, xo_spill[pi])
                    i = vector.scalar_tensor_tensor(
                        out=xos[pi][:, c * 128:(c + 1) * 128],
                        in0=mk.psA[a][:, slot * 128:(slot + 1) * 128],
                        scalar=mk.bo_sb[:, c:c + 1],
                        in1=xts[pi][:, c * 128:(c + 1) * 128],
                        op0=ALU.add, op1=ALU.add)
                    mk.sV.inc(i)
                psa_free[a], psa_eng[a] = mk.sV.n, mk.sV
            t_res = mk.sV.n
            x_free[pi] = t_res

            @block.sync
            def _(sync):
                wait(sync, mk.sV, t_res)
                _qs = mk.qd.pre(sync)
                i = sync.dma_start(out=x1v[:, :, t * 128:(t + 1) * 128],
                                   in_=xos[pi].rearrange("q (c p) -> q c p", c=8))
                _h = mk.qd.post(i)
            xo_spill[pi] = mk.qd.last

    def p4(self, block, xin, xout, prep, ngroups=8, depth=3, no_reload=False):
        mk = self
        w1s3 = self.wA().rearrange("q (c f) -> q c f", c=8)
        w2s3 = self.wB().rearrange("q (c f) -> q c f", c=32)
        XG = self.vw(self.aX, 0, 128, 4096, F32)
        XG3 = XG.rearrange("q (c p) -> q c p", c=8)
        XNG = self.vw(self.aX, 16384, 128, 4096, BF16)
        XNG3 = XNG.rearrange("q (c p) -> q c p", c=8)
        HB = self.vw(self.aX, 24576, 128, 16384, BF16)
        TMP = self.vw(self.aX, 57344, 128, 512, F32)
        xinv = xin.rearrange("(c p) s -> p c s", p=128)
        xoutv = xout.rearrange("(c p) s -> p c s", p=128)

        if prep:
            self.p4_prep(block)
        elif no_reload:
            self._p4_wtok = (0, 0)
        else:
            hs_w = []

            @block.gpsimd
            def _(gpsimd):
                for i_ in range(4):
                    _qs = mk.qg.pre(gpsimd)
                    ins = gpsimd.dma_start(out=self.wA()[:, i_ * 8192:(i_ + 1) * 8192],
                                           in_=mk.w1ss[:, i_ * 8192:(i_ + 1) * 8192])
                    hs_w.append(mk.qg.post(ins))
                for i_ in range(4):
                    _qs = mk.qg.pre(gpsimd)
                    ins = gpsimd.dma_start(out=self.wB()[:, i_ * 8192:(i_ + 1) * 8192],
                                           in_=mk.w2ss[:, i_ * 8192:(i_ + 1) * 8192])
                    hs_w.append(mk.qg.post(ins))
            t_w = hs_w

            hs_al = []

            @block.sync
            def _(sync):
                _qs = mk.qd.pre(sync)
                i = sync.dma_start(out=mk.al1_sb[:, :], in_=mk.a1s[:, :])
                hs_al.append(mk.qd.post(i))
                _qs = mk.qd.pre(sync)
                i = sync.dma_start(out=mk.al2_sb[:, :], in_=mk.a2s[:, :])
                hs_al.append(mk.qd.post(i))
            t_al = hs_al
            self._p4_wtok = (t_w, t_al)

        xg_free = 0
        hb_free = 0
        bank4 = [0, 0, 0, 0]
        bank4e = [None, None, None, None]
        pb4 = [mk.psA[0], mk.psA[1], mk.psP[0], mk.psP[1]]
        for grp in range(ngroups):
            @block.sync
            def _(sync):
                waith(sync, xg_free)
                _qs = mk.qd.pre(sync)
                i = sync.dma_start(out=XG3, in_=xinv[:, :, grp * 512:(grp + 1) * 512])
                _h = mk.qd.post(i)
            t_xg = mk.qd.last

            t_norms = [mk.sV.n]
            for tau in range(4 if depth >= 1 else 0):
                def extra_wait(eng, _t=t_xg):
                    waith(eng, _t)

                def out_wait(eng, _t=hb_free):
                    wait(eng, mk.sT, _t)

                t_n = self.emit_ln(block, XG3[:, :, tau * 128:(tau + 1) * 128],
                                   XNG3[:, :, tau * 128:(tau + 1) * 128],
                                   extra_wait, out_wait)
                t_norms.append(t_n)
            t_norm = t_norms[-1]

            for hfc in range(32 if depth >= 2 else 0):
                b = hfc % 4

                @block.tensor
                def _(tensor):
                    if hfc == 0:
                        wait(tensor, mk.sV, t_norm)
                    if grp == 0 and hfc == 0 and not prep:
                        waith(tensor, self._p4_wtok[0])
                    if bank4e[b] is not None:
                        wait(tensor, bank4e[b], bank4[b])
                    for c in range(8):
                        i = tensor.matmul(pb4[b][:, 0:512], w1s3[:, c, hfc * 128:(hfc + 1) * 128],
                                          XNG3[:, c, :], start=(c == 0), stop=(c == 7))
                    mk.sT.inc(i)
                t_h = mk.sT.n

                @block.scalar
                def _(scalar):
                    selfwait(scalar, mk.sS)
                    wait(scalar, mk.sT, t_h)
                    if grp == 0 and hfc == 0 and not prep:
                        waith(scalar, self._p4_wtok[1])
                    if hfc == 0:
                        wait(scalar, mk.sT, hb_free)
                    i = scalar.activation(out=HB[:, hfc * 512:(hfc + 1) * 512],
                                          in_=pb4[b][:, 0:512], func=AF.Gelu,
                                          scale=mk.al1_sb[:, hfc:hfc + 1],
                                          bias=mk.b1_sb[:, hfc:hfc + 1])
                    mk.sS.inc(i)
                bank4[b], bank4e[b] = mk.sS.n, mk.sS
            t_hb = mk.sS.n

            for dc in range(8 if depth >= 3 else 0):
                b = dc % 4

                @block.tensor
                def _(tensor):
                    if dc == 0:
                        wait(tensor, mk.sS, t_hb)
                    if bank4e[b] is not None:
                        wait(tensor, bank4e[b], bank4[b])
                    for hc in range(32):
                        i = tensor.matmul(pb4[b][:, 0:512], w2s3[:, hc, dc * 128:(dc + 1) * 128],
                                          HB[:, hc * 512:(hc + 1) * 512],
                                          start=(hc == 0), stop=(hc == 31))
                    mk.sT.inc(i)
                t_o = mk.sT.n

                @block.vector
                def _(vector):
                    selfwait(vector, mk.sV)
                    wait(vector, mk.sT, t_o)
                    mk.sV.inc(vector.scalar_tensor_tensor(out=TMP[:, :], in0=pb4[b][:, 0:512],
                                                          scalar=mk.al2_sb[:, dc:dc + 1],
                                                          in1=XG3[:, dc, :],
                                                          op0=ALU.mult, op1=ALU.add))
                    selfwait(vector, mk.sV)
                    i = vector.tensor_scalar(out=XG3[:, dc, :], in0=TMP[:, :],
                                             scalar1=mk.b2_sb[:, dc:dc + 1], scalar2=None,
                                             op0=ALU.add)
                    mk.sV.inc(i)
                bank4[b], bank4e[b] = mk.sV.n, mk.sV
            hb_free = mk.sT.n
            t_x2 = mk.sV.n

            @block.sync
            def _(sync):
                wait(sync, mk.sV, t_x2)
                _qs = mk.qd.pre(sync)
                i = sync.dma_start(out=xoutv[:, :, grp * 512:(grp + 1) * 512], in_=XG3)
                _h = mk.qd.post(i)
            xg_free = mk.qd.last

    def p4_prep(self, block):
        mk = self
        STG = self.vw(self.aX, 20480, 128, 4096, F32)
        ABSB = self.vw(self.aX, 36864, 128, 4096, BF16)
        AROW = self.vw(self.aX, 53248, 1, 4096, F32)
        w1s3 = self.wA().rearrange("q (c f) -> q c f", c=8)
        w2s3 = self.wB().rearrange("q (c f) -> q c f", c=32)

        @block.vector
        def _(vector):
            selfwait(vector, mk.sV)
            i = vector.memset(AROW[0:1, :], 0.0)
            mk.sV.inc(i)
        t_z = mk.sV.n

        stg_free = 0
        absb_free = 0
        psy_free = [0, 0]
        for c in range(8):
            @block.sync
            def _(sync):
                wait(sync, mk.sS, stg_free)
                _qs = mk.qd.pre(sync)
                i = sync.dma_start(out=STG[:, :], in_=mk.w1t[:, c * 4096:(c + 1) * 4096])
                _h = mk.qd.post(i)
            t_stg = mk.qd.last

            @block.scalar
            def _(scalar):
                selfwait(scalar, mk.sS)
                waith(scalar, t_stg)
                scalar.activation(out=w1s3[:, c, :], in_=STG[:, :], func=AF.Sign)
                wait(scalar, mk.sT, absb_free)
                i = scalar.activation(out=ABSB[:, :], in_=STG[:, :], func=AF.Abs)
                mk.sS.inc(i)
            stg_free = mk.sS.n
            t_abs = mk.sS.n

            for sub in range(8):
                b = sub % 2

                @block.tensor
                def _(tensor):
                    wait(tensor, mk.sS, t_abs)
                    wait(tensor, mk.sV, psy_free[b])
                    i = tensor.matmul(mk.psY[0:1, b * 512:(b + 1) * 512], mk.onesb[:, 0:1],
                                      ABSB[:, sub * 512:(sub + 1) * 512], start=True, stop=True)
                    mk.sT.inc(i)
                t_mm = mk.sT.n

                @block.vector
                def _(vector):
                    selfwait(vector, mk.sV)
                    wait(vector, mk.sT, t_mm)
                    if c == 0 and sub < 2:
                        wait(vector, mk.sV, t_z)
                    selfwait(vector, mk.sV)
                    i = vector.tensor_tensor(out=AROW[0:1, sub * 512:(sub + 1) * 512],
                                             in0=AROW[0:1, sub * 512:(sub + 1) * 512],
                                             in1=mk.psY[0:1, b * 512:(b + 1) * 512], op=ALU.add)
                    mk.sV.inc(i)
                psy_free[b] = mk.sV.n
            absb_free = mk.sT.n
        t_arow = mk.sV.n

        @block.sync
        def _(sync):
            wait(sync, mk.sV, t_arow)
            _qs = mk.qd.pre(sync)
            i = sync.dma_start(out=mk.ar1s[0:1, :], in_=AROW[0:1, :])
            h_sp = mk.qd.post(i)
            _qs = mk.qd.pre(sync)
            waith(sync, h_sp)
            i = sync.dma_start(out=mk.al1_sb[:, :],
                               in_=mk.ar1s.rearrange("o (s p) -> (o p) s", p=128),
                               allow_slow_non_contiguous=True)
            _h = mk.qd.post(i)
        t_a1d = mk.qd.last

        @block.vector
        def _(vector):
            selfwait(vector, mk.sV)
            waith(vector, t_a1d)
            mk.sV.inc(vector.tensor_scalar(out=mk.al1_sb[:, :], in0=mk.al1_sb[:, :],
                                           scalar1=mk.c1024c[:, 0:1], scalar2=None, op0=ALU.mult))
            i = vector.memset(AROW[0:1, 0:1024], 0.0)
            mk.sV.inc(i)
        t_a1 = mk.sV.n

        stg_free = mk.sS.n
        absb_free = mk.sT.n
        psy_free = [0, 0]
        for c in range(32):
            @block.sync
            def _(sync):
                wait(sync, mk.sS, stg_free)
                _qs = mk.qd.pre(sync)
                i = sync.dma_start(out=STG[:, 0:1024], in_=mk.w2t[:, c * 1024:(c + 1) * 1024])
                _h = mk.qd.post(i)
            t_stg = mk.qd.last

            @block.scalar
            def _(scalar):
                selfwait(scalar, mk.sS)
                waith(scalar, t_stg)
                scalar.activation(out=w2s3[:, c, :], in_=STG[:, 0:1024], func=AF.Sign)
                wait(scalar, mk.sT, absb_free)
                i = scalar.activation(out=ABSB[:, 0:1024], in_=STG[:, 0:1024], func=AF.Abs)
                mk.sS.inc(i)
            stg_free = mk.sS.n
            t_abs = mk.sS.n

            for sub in range(2):
                b = sub % 2

                @block.tensor
                def _(tensor):
                    wait(tensor, mk.sS, t_abs)
                    wait(tensor, mk.sV, psy_free[b])
                    i = tensor.matmul(mk.psY[0:1, b * 512:(b + 1) * 512], mk.onesb[:, 0:1],
                                      ABSB[:, sub * 512:(sub + 1) * 512], start=True, stop=True)
                    mk.sT.inc(i)
                t_mm = mk.sT.n

                @block.vector
                def _(vector):
                    selfwait(vector, mk.sV)
                    wait(vector, mk.sT, t_mm)
                    if c == 0 and sub < 2:
                        wait(vector, mk.sV, t_a1)
                    selfwait(vector, mk.sV)
                    i = vector.tensor_tensor(out=AROW[0:1, sub * 512:(sub + 1) * 512],
                                             in0=AROW[0:1, sub * 512:(sub + 1) * 512],
                                             in1=mk.psY[0:1, b * 512:(b + 1) * 512], op=ALU.add)
                    mk.sV.inc(i)
                psy_free[b] = mk.sV.n
            absb_free = mk.sT.n
        t_arow2 = mk.sV.n

        @block.sync
        def _(sync):
            wait(sync, mk.sV, t_arow2)
            _qs = mk.qd.pre(sync)
            i = sync.dma_start(out=mk.ar2s[0:1, :], in_=AROW[0:1, 0:1024])
            h_sp = mk.qd.post(i)
            _qs = mk.qd.pre(sync)
            waith(sync, h_sp)
            i = sync.dma_start(out=mk.al2_sb[:, :],
                               in_=mk.ar2s.rearrange("o (s p) -> (o p) s", p=128),
                               allow_slow_non_contiguous=True)
            _h = mk.qd.post(i)
        t_a2d = mk.qd.last

        @block.vector
        def _(vector):
            selfwait(vector, mk.sV)
            waith(vector, t_a2d)
            i = vector.tensor_scalar(out=mk.al2_sb[:, :], in0=mk.al2_sb[:, :],
                                     scalar1=mk.c4096c[:, 0:1], scalar2=None, op0=ALU.mult)
            mk.sV.inc(i)
        t_a2 = mk.sV.n

        @block.sync
        def _(sync):
            wait(sync, mk.sS, mk.sS.n)
            for i_ in range(4):
                _qs = mk.qd.pre(sync)
                i = sync.dma_start(out=mk.w1ss[:, i_ * 8192:(i_ + 1) * 8192],
                                   in_=self.wA()[:, i_ * 8192:(i_ + 1) * 8192])
                _h = mk.qd.post(i)
            for i_ in range(4):
                _qs = mk.qd.pre(sync)
                i = sync.dma_start(out=mk.w2ss[:, i_ * 8192:(i_ + 1) * 8192],
                                   in_=self.wB()[:, i_ * 8192:(i_ + 1) * 8192])
                _h = mk.qd.post(i)
            wait(sync, mk.sV, t_a2)
            _qs = mk.qd.pre(sync)
            i = sync.dma_start(out=mk.a1s[:, :], in_=mk.al1_sb[:, :])
            _h = mk.qd.post(i)
            _qs = mk.qd.pre(sync)
            i = sync.dma_start(out=mk.a2s[:, :], in_=mk.al2_sb[:, :])
            _h = mk.qd.post(i)

    def p5a(self, block):
        mk = self
        XG = self.vw(self.aX, 0, 128, 4096, F32)
        XG3 = XG.rearrange("q (c p) -> q c p", c=8)
        XNG = self.vw(self.aX, 16384, 128, 4096, BF16)
        XNG3 = XNG.rearrange("q (c p) -> q c p", c=8)
        WSM = self.vw(self.aX, 24576, 128, 8192, BF16)
        WSM3 = WSM.rearrange("q (c f) -> q c f", c=8)
        QTG = self.vw(self.aX, 40960, 128, 4096, BF16)
        QTG3 = QTG.rearrange("q (f p) -> q f p", f=8)
        x2v = mk.x2s.rearrange("(c p) s -> p c s", p=128)
        xn1v = mk.xn1s.rearrange("(c p) s -> p c s", p=128)
        kT3 = self.wA().rearrange("q (f s) -> q f s", f=8)
        vb3 = self.wB().rearrange("q (t f) -> q t f", t=32)

        t_w = self.load_w(block, WSM, mk.wqb, 2)
        xg_rd_v = 0
        xg_rd_s = 0
        xng_mm = 0
        xng_sp = 0
        qtg_sp = 0
        bankA = [0, 0]
        bankAe = [None, None]
        for grp in range(8):
            @block.sync
            def _(sync):
                wait(sync, mk.sV, xg_rd_v)
                wait(sync, mk.sS, xg_rd_s)
                _qs = mk.qd.pre(sync)
                i = sync.dma_start(out=XG3, in_=x2v[:, :, grp * 512:(grp + 1) * 512])
                _h = mk.qd.post(i)
            t_xg = mk.qd.last

            for tau in range(4):
                def extra_wait(eng, _t=t_xg):
                    waith(eng, _t)

                def out_wait(eng, _a=xng_mm, _b=xng_sp):
                    wait(eng, mk.sT, _a)
                    waith(eng, _b)

                t_n = self.emit_ln(block, XG3[:, :, tau * 128:(tau + 1) * 128],
                                   XNG3[:, :, tau * 128:(tau + 1) * 128],
                                   extra_wait, out_wait)
            t_norm = t_n
            xg_rd_v = t_norm
            xg_rd_s = mk.sS.n

            @block.sync
            def _(sync):
                wait(sync, mk.sV, t_norm)
                _qs = mk.qd.pre(sync)
                i = sync.dma_start(out=xn1v[:, :, grp * 512:(grp + 1) * 512], in_=XNG3)
                _h = mk.qd.post(i)
            xng_sp = mk.qd.last

            for fc in range(8):
                b = fc % 2

                @block.tensor
                def _(tensor):
                    wait(tensor, mk.sV, t_norm)
                    if grp == 0 and fc == 0:
                        waith(tensor, t_w)
                    if bankAe[b] is not None:
                        wait(tensor, bankAe[b], bankA[b])
                    for c in range(8):
                        i = tensor.matmul(mk.psA[b][:, 0:512], WSM3[:, c, fc * 128:(fc + 1) * 128],
                                          XNG3[:, c, :], start=(c == 0), stop=(c == 7))
                    mk.sT.inc(i)
                t_p = mk.sT.n

                if fc % 2 == 0:
                    @block.vector
                    def _(vector):
                        selfwait(vector, mk.sV)
                        wait(vector, mk.sT, t_p)
                        waith(vector, qtg_sp)
                        i = vector.tensor_copy(out=QTG[:, fc * 512:(fc + 1) * 512],
                                               in_=mk.psA[b][:, 0:512])
                        mk.sV.inc(i)
                    bankA[b], bankAe[b] = mk.sV.n, mk.sV
                else:
                    @block.vector
                    def _(vector):
                        selfwait(vector, mk.sV)
                        wait(vector, mk.sT, t_p)
                        waith(vector, qtg_sp)
                        i = vector.tensor_copy(out=QTG[:, fc * 512:(fc + 1) * 512],
                                        in_=mk.psA[b][:, 0:512])
                        mk.sV.inc(i)
                    bankA[b], bankAe[b] = mk.sV.n, mk.sV
            xng_mm = mk.sT.n

            @block.sync
            def _(sync):
                wait(sync, mk.sV, mk.sV.n)
                wait(sync, mk.sS, mk.sS.n)
                _qs = mk.qd.pre(sync)
                i = sync.dma_start(out=mk.qts.rearrange("r (f s) -> r f s", f=8)[:, :, grp * 512:(grp + 1) * 512],
                                   in_=QTG3)
                _h = mk.qd.post(i)
            qtg_sp = mk.qd.last
        self.barrier(block)

        t_w = self.load_w(block, WSM, mk.wkb, 2)
        xng_free = 0
        xng_ld = 0
        bankA = [0, 0, 0, 0]
        bankAe = [None, None, None, None]
        for grp in range(8):
            @block.sync
            def _(sync):
                wait(sync, mk.sT, xng_free)
                _qs = mk.qd.pre(sync)
                i = sync.dma_start(out=XNG3, in_=xn1v[:, :, grp * 512:(grp + 1) * 512])
                _h = mk.qd.post(i)
            xng_ld = mk.qd.last

            for fc in range(8):
                b = fc % 2

                @block.tensor
                def _(tensor):
                    waith(tensor, xng_ld)
                    if grp == 0 and fc == 0:
                        waith(tensor, t_w)
                    if bankAe[b] is not None:
                        wait(tensor, bankAe[b], bankA[b])
                    for c in range(8):
                        i = tensor.matmul(mk.psA[b][:, 0:512], WSM3[:, c, fc * 128:(fc + 1) * 128],
                                          XNG3[:, c, :], start=(c == 0), stop=(c == 7))
                    mk.sT.inc(i)
                t_p = mk.sT.n

                if fc % 2 == 0:
                    @block.vector
                    def _(vector):
                        selfwait(vector, mk.sV)
                        wait(vector, mk.sT, t_p)
                        i = vector.tensor_copy(out=kT3[:, fc, grp * 512:(grp + 1) * 512],
                                               in_=mk.psA[b][:, 0:512])
                        mk.sV.inc(i)
                    bankA[b], bankAe[b] = mk.sV.n, mk.sV
                else:
                    @block.vector
                    def _(vector):
                        selfwait(vector, mk.sV)
                        wait(vector, mk.sT, t_p)
                        i = vector.tensor_copy(out=kT3[:, fc, grp * 512:(grp + 1) * 512],
                                        in_=mk.psA[b][:, 0:512])
                        mk.sV.inc(i)
                    bankA[b], bankAe[b] = mk.sV.n, mk.sV
            xng_free = mk.sT.n
        self.barrier(block)

        t_w = self.load_w(block, WSM, mk.wvb, 2)
        bankP = [0, 0]
        bankPe = [None, None]
        for grp in range(8):
            @block.sync
            def _(sync):
                wait(sync, mk.sT, xng_free)
                _qs = mk.qd.pre(sync)
                i = sync.dma_start(out=XNG3, in_=xn1v[:, :, grp * 512:(grp + 1) * 512])
                _h = mk.qd.post(i)
            xng_ld = mk.qd.last

            for tau in range(4):
                tt = grp * 4 + tau
                for half in range(2):
                    b = half

                    @block.tensor
                    def _(tensor):
                        waith(tensor, xng_ld)
                        if grp == 0 and tau == 0 and half == 0:
                            waith(tensor, t_w)
                        if bankPe[b] is not None:
                            wait(tensor, bankPe[b], bankP[b])
                        for c in range(8):
                            i = tensor.matmul(mk.psP[b][:, 0:512],
                                              XNG3[:, c, tau * 128:(tau + 1) * 128],
                                              WSM3[:, c, half * 512:(half + 1) * 512],
                                              start=(c == 0), stop=(c == 7))
                        mk.sT.inc(i)
                    t_p = mk.sT.n

                    if half == 0:
                        @block.vector
                        def _(vector):
                            selfwait(vector, mk.sV)
                            wait(vector, mk.sT, t_p)
                            i = vector.tensor_copy(out=vb3[:, tt, 0:512], in_=mk.psP[b][:, 0:512])
                            mk.sV.inc(i)
                        bankP[b], bankPe[b] = mk.sV.n, mk.sV
                    else:
                        @block.vector
                        def _(vector):
                            selfwait(vector, mk.sV)
                            wait(vector, mk.sT, t_p)
                            i = vector.tensor_copy(out=vb3[:, tt, 512:1024], in_=mk.psP[b][:, 0:512])
                            mk.sV.inc(i)
                        bankP[b], bankPe[b] = mk.sV.n, mk.sV
            xng_free = mk.sT.n

    def p5b(self, block):
        mk = self
        WOB = self.vw(self.aX, 0, 128, 8192, BF16)
        WOB3 = WOB.rearrange("q (c f) -> q c f", c=8)
        QT = [self.vw(self.aX, 16384, 128, 1024, BF16),
              self.vw(self.aX, 18432, 128, 1024, BF16)]
        SCO = self.vw(self.aX, 20480, 128, 4096, BF16)
        E = self.vw(self.aX, 28672, 128, 4096, BF16)
        OUTB = self.vw(self.aX, 36864, 128, 1024, BF16)
        OUTTG = self.vw(self.aX, 38912, 128, 4096, BF16)
        XTG = self.vw(self.aX, 47104, 128, 4096, F32)
        XTG4 = XTG.rearrange("q (t c p) -> q t c p", t=4, c=8)
        kT3 = self.wA().rearrange("q (f s) -> q f s", f=8)
        vb3 = self.wB().rearrange("q (t f) -> q t f", t=32)
        x2v = mk.x2s.rearrange("(c p) s -> p c s", p=128)
        x3v = mk.x3s.rearrange("(c p) s -> p c s", p=128)
        qtsv = mk.qts.rearrange("r (f s) -> r f s", f=8)

        t_w = self.load_w(block, WOB, mk.wob, 2)
        qt_free = [0, 0]
        xtg_sp = 0
        xtg_res = 0
        bankA = [[0, 0], [0, 0]]
        sco_free = 0
        e_free = 0
        ent_mm = 0
        psT_free = [0, 0]
        psT_eng = [None, None]
        psy_evac = 0
        outb_tp = 0
        outtg_mm = 0
        bankP = [0, 0]
        bankPe = [None, None]
        for qg in range(8):
            hs_xtg = []

            @block.sync
            def _(sync):
                waith(sync, xtg_sp)
                wait(sync, mk.sV, xtg_res)
                for tau4 in range(4):
                    _qs = mk.qd.pre(sync)
                    i = sync.dma_start(out=XTG4[:, tau4],
                                       in_=x2v[:, :, (qg * 4 + tau4) * 128:(qg * 4 + tau4 + 1) * 128])
                    hs_xtg.append(mk.qd.post(i))
            t_xtg = hs_xtg

            for tau in range(4):
                qi = qg * 4 + tau
                pi = tau % 2
                kp = (qi + 1) * 128
                nkc = (kp + 511) // 512

                @block.sync
                def _(sync):
                    wait(sync, mk.sT, qt_free[pi])
                    _qs = mk.qd.pre(sync)
                    i = sync.dma_start(out=QT[pi].rearrange("q (f p) -> q f p", f=8),
                                       in_=qtsv[:, :, qi * 128:(qi + 1) * 128])
                    _h = mk.qd.post(i)
                t_qt = mk.qd.last

                QT3 = QT[pi].rearrange("q (f p) -> q f p", f=8)
                for kc in range(nkc):
                    b = kc % 2
                    klen = min(512, kp - kc * 512)

                    @block.tensor
                    def _(tensor):
                        waith(tensor, t_qt)
                        if qg == 0 and tau == 0 and kc == 0:
                            waith(tensor, t_w)
                        wait(tensor, mk.sV, bankA[b][0])
                        wait(tensor, mk.sS, bankA[b][1])
                        for fc in range(8):
                            i = tensor.matmul(mk.psA[b][:, 0:klen], QT3[:, fc, :],
                                              kT3[:, fc, kc * 512:kc * 512 + klen],
                                              start=(fc == 0), stop=(fc == 7))
                        mk.sT.inc(i)
                    t_sc = mk.sT.n

                    @block.vector
                    def _(vector):
                        selfwait(vector, mk.sV)
                        wait(vector, mk.sT, t_sc)
                        wait(vector, mk.sT, ent_mm)
                        i = vector.tensor_copy(out=SCO[:, kc * 512:kc * 512 + klen],
                                        in_=mk.psA[b][:, 0:klen])
                        mk.sV.inc(i)
                    t_cp = mk.sS.n

                    @block.vector
                    def _(vector):
                        selfwait(vector, mk.sV)
                        wait(vector, mk.sT, t_sc)
                        i = vector.reduce_max(out=mk.negm[:, kc:kc + 1], in_=mk.psA[b][:, 0:klen],
                                              axis=AX.X, negate=True)
                        mk.sV.inc(i)
                    bankA[b] = [mk.sV.n, t_cp]

                @block.vector
                def _(vector):
                    selfwait(vector, mk.sV)
                    wait(vector, mk.sS, mk.sS.n)
                    selfwait(vector, mk.sV)
                    mk.sV.inc(vector.tensor_tensor(out=SCO[:, qi * 128:qi * 128 + 128],
                                                   in0=SCO[:, qi * 128:qi * 128 + 128],
                                                   in1=mk.cmaskb[:, :], op=ALU.add))
                    mk.sV.inc(vector.tensor_reduce(out=mk.nm1[:, 0:1], in_=mk.negm[:, 0:nkc],
                                                   axis=AX.X, op=ALU.min))
                    selfwait(vector, mk.sV)
                    i = vector.tensor_mul(out=mk.ebias[:, 0:1], in0=mk.nm1[:, 0:1],
                                          in1=mk.c32[:, 0:1])
                    mk.sV.inc(i)
                t_msk = mk.sV.n

                @block.scalar
                def _(scalar):
                    selfwait(scalar, mk.sS)
                    wait(scalar, mk.sV, t_msk)
                    wait(scalar, mk.sT, e_free)
                    i = scalar.activation(out=E[:, 0:kp], in_=SCO[:, 0:kp], func=AF.Exp,
                                          scale=mk.c32[:, 0:1], bias=mk.ebias[:, 0:1],
                                          accum_out=mk.edenom[:, 0:1])
                    mk.sS.inc(i)
                t_exp = mk.sS.n
                sco_free = t_exp

                @block.vector
                def _(vector):
                    selfwait(vector, mk.sV)
                    wait(vector, mk.sS, t_exp)
                    mk.sV.inc(vector.reciprocal(out=mk.erd[:, 0:1], in_=mk.edenom[:, 0:1]))
                    selfwait(vector, mk.sV)
                    i = vector.tensor_scalar(out=E[:, 0:kp], in0=E[:, 0:kp],
                                             scalar1=mk.erd[:, 0:1], scalar2=None, op0=ALU.mult)
                    mk.sV.inc(i)
                t_en = mk.sV.n

                nkb = qi + 1
                for w0 in range(0, nkb, 8):
                    ps_ = (w0 // 8) % 2
                    nw = min(8, nkb - w0)

                    @block.tensor
                    def _(tensor):
                        wait(tensor, mk.sV, t_en)
                        if psT_eng[ps_] is not None:
                            wait(tensor, psT_eng[ps_], psT_free[ps_])
                        for u in range(nw):
                            kb = w0 + u
                            i = tensor.transpose(mk.psT[ps_][:, u * 128:(u + 1) * 128],
                                                 E[:, kb * 128:(kb + 1) * 128],
                                                 mk.identb[:, :])
                        mk.sT.inc(i)
                    t_tw = mk.sT.n

                    if ps_ == 0:
                        @block.vector
                        def _(vector):
                            selfwait(vector, mk.sV)
                            wait(vector, mk.sT, t_tw)
                            wait(vector, mk.sS, sco_free)
                            wait(vector, mk.sT, ent_mm)
                            i = vector.tensor_copy(out=SCO[:, w0 * 128:(w0 + nw) * 128],
                                                   in_=mk.psT[ps_][:, 0:nw * 128])
                            mk.sV.inc(i)
                        psT_free[ps_], psT_eng[ps_] = mk.sV.n, mk.sV
                    else:
                        @block.vector
                        def _(vector):
                            selfwait(vector, mk.sV)
                            wait(vector, mk.sT, t_tw)
                            wait(vector, mk.sT, ent_mm)
                            i = vector.tensor_copy(out=SCO[:, w0 * 128:(w0 + nw) * 128],
                                            in_=mk.psT[ps_][:, 0:nw * 128])
                            mk.sV.inc(i)
                        psT_free[ps_], psT_eng[ps_] = mk.sV.n, mk.sV
                e_free = mk.sT.n
                t_ent_v, t_ent_s = mk.sV.n, mk.sS.n

                @block.tensor
                def _(tensor):
                    wait(tensor, mk.sV, t_ent_v)
                    wait(tensor, mk.sS, t_ent_s)
                    wait(tensor, mk.sV, psy_evac)
                    for kb in range(nkb):
                        for half in range(2):
                            i = tensor.matmul(mk.psY[:, half * 512:(half + 1) * 512],
                                              SCO[:, kb * 128:(kb + 1) * 128],
                                              vb3[:, kb, half * 512:(half + 1) * 512],
                                              start=(kb == 0), stop=(kb == nkb - 1))
                    mk.sT.inc(i)
                ent_mm = mk.sT.n

                @block.vector
                def _(vector):
                    selfwait(vector, mk.sV)
                    wait(vector, mk.sT, ent_mm)
                    wait(vector, mk.sT, outb_tp)
                    i = vector.tensor_copy(out=OUTB[:, :], in_=mk.psY[:, 0:1024])
                    mk.sV.inc(i)
                psy_evac = mk.sV.n
                t_outb = mk.sV.n

                for vc in range(8):
                    ps_ = vc % 2

                    @block.tensor
                    def _(tensor):
                        wait(tensor, mk.sV, t_outb)
                        if psT_eng[ps_] is not None:
                            wait(tensor, psT_eng[ps_], psT_free[ps_])
                        i = tensor.transpose(mk.psT[ps_][:, 0:128],
                                             OUTB[:, vc * 128:(vc + 1) * 128],
                                             mk.identb[:, :])
                        mk.sT.inc(i)
                    t_tp = mk.sT.n

                    if ps_ == 0:
                        @block.vector
                        def _(vector):
                            selfwait(vector, mk.sV)
                            wait(vector, mk.sT, t_tp)
                            wait(vector, mk.sT, outtg_mm)
                            i = vector.tensor_copy(
                                out=OUTTG[:, vc * 512 + tau * 128: vc * 512 + (tau + 1) * 128],
                                in_=mk.psT[ps_][:, 0:128])
                            mk.sV.inc(i)
                        psT_free[ps_], psT_eng[ps_] = mk.sV.n, mk.sV
                    else:
                        @block.vector
                        def _(vector):
                            selfwait(vector, mk.sV)
                            wait(vector, mk.sT, t_tp)
                            wait(vector, mk.sT, outtg_mm)
                            i = vector.tensor_copy(
                                out=OUTTG[:, vc * 512 + tau * 128: vc * 512 + (tau + 1) * 128],
                                in_=mk.psT[ps_][:, 0:128])
                            mk.sV.inc(i)
                        psT_free[ps_], psT_eng[ps_] = mk.sV.n, mk.sV
                outb_tp = mk.sT.n
                qt_free[pi] = mk.sT.n
            t_outtg_v, t_outtg_s = mk.sV.n, mk.sS.n

            for mc in range(8):
                b = mc % 2

                @block.tensor
                def _(tensor):
                    wait(tensor, mk.sV, t_outtg_v)
                    wait(tensor, mk.sS, t_outtg_s)
                    if bankPe[b] is not None:
                        wait(tensor, bankPe[b], bankP[b])
                    for vc in range(8):
                        i = tensor.matmul(mk.psP[b][:, 0:512], WOB3[:, vc, mc * 128:(mc + 1) * 128],
                                          OUTTG[:, vc * 512:(vc + 1) * 512],
                                          start=(vc == 0), stop=(vc == 7))
                    mk.sT.inc(i)
                t_y = mk.sT.n

                @block.vector
                def _(vector):
                    selfwait(vector, mk.sV)
                    wait(vector, mk.sT, t_y)
                    waith(vector, t_xtg)
                    ps4 = mk.psP[b][:, 0:512].rearrange("q (t p) -> q t p", t=4)
                    i = vector.tensor_tensor(out=XTG4[:, :, mc, :], in0=ps4,
                                             in1=XTG4[:, :, mc, :], op=ALU.add)
                    mk.sV.inc(i)
                bankP[b], bankPe[b] = mk.sV.n, mk.sV
            outtg_mm = mk.sT.n
            xtg_res = mk.sV.n

            hs_sp = []

            @block.sync
            def _(sync):
                wait(sync, mk.sV, xtg_res)
                for tau4 in range(4):
                    _qs = mk.qd.pre(sync)
                    i = sync.dma_start(out=x3v[:, :, (qg * 4 + tau4) * 128:(qg * 4 + tau4 + 1) * 128],
                                       in_=XTG4[:, tau4])
                    hs_sp.append(mk.qd.post(i))
            xtg_sp = hs_sp


def build_nc(stop_after=None, dbg=None):
    from contextlib import ExitStack
    nc = bass.Bass("TRN2", target_bir_lowering=False)
    es = ExitStack()
    m = MK(nc, es, stop_after)
    m.declare()
    dbg_out = None
    if dbg is not None:
        dbg_out = nc.dram_tensor("dbg", [128, 4096], F32, kind="ExternalOutput").ap()
    with nc.Block() as block:
        m.p0(block)
        phases = []
        phases.append(("p1", lambda: m.p1(block)))
        phases.append(("p2", lambda: m.p2(block)))
        phases.append(("p3a", lambda: m.p3a(block)))
        phases.append(("p3b", lambda: m.p3b(block)))
        phases.append(("p4prep", lambda: m.p4_prep(block)))
        phases.append(("p4", lambda: m.p4(block, m.x1s, m.x2s, prep=False)))
        phases.append(("p5a", lambda: m.p5a(block)))
        phases.append(("p5b", lambda: m.p5b(block)))
        phases.append(("p6", lambda: m.p4(block, m.x3s, m.out, prep=False)))
        for name, fn in phases:
            fn()
            m.barrier(block)
            if stop_after == name:
                break

        if dbg is not None:
            src_map = {
                "xn1": (m.xn1s, BF16), "sq": (m.sqs, BF16), "sv": (m.svs, BF16),
                "x1": (m.x1s, F32), "x2": (m.x2s, F32), "x3": (m.x3s, F32),
                "qts": (m.qts, BF16), "out": (m.out, F32),
                "ksum": (None, None), "w1ss": (m.w1ss, BF16), "a1": (None, None),
            }
            if dbg == "xng2":
                XNGd = m.vw(m.aX, 16384, 128, 4096, BF16)
                HBd = m.vw(m.aX, 24576, 128, 16384, BF16)

                @block.gpsimd
                def _(gpsimd):
                    m.qg.drain(gpsimd)
                    i = gpsimd.dma_start(out=dbg_out[:, 0:2048], in_=XNGd[:, 2048:4096])
                    _h = m.qg.post(i)
                    m.qg.drain(gpsimd)
                    i = gpsimd.dma_start(out=dbg_out[:, 2048:4096], in_=HBd[:, 2048:4096])
                    _h = m.qg.post(i)
            elif dbg == "w1sig":
                @block.gpsimd
                def _(gpsimd):
                    m.qg.drain(gpsimd)
                    i = gpsimd.dma_start(out=dbg_out[:, 0:4096], in_=m.wA()[:, 0:4096])
                    _h = m.qg.post(i)
            elif dbg == "xng_hb":
                XNGd = m.vw(m.aX, 16384, 128, 4096, BF16)
                HBd = m.vw(m.aX, 24576, 128, 16384, BF16)

                @block.gpsimd
                def _(gpsimd):
                    m.qg.drain(gpsimd)
                    i = gpsimd.dma_start(out=dbg_out[:, 0:2048], in_=XNGd[:, 0:2048])
                    _h = m.qg.post(i)
                    m.qg.drain(gpsimd)
                    i = gpsimd.dma_start(out=dbg_out[:, 2048:4096], in_=HBd[:, 0:2048])
                    _h = m.qg.post(i)
            elif dbg == "ksum":
                ks = m.vw(m.aX, 4096, 128, NT * 64, F32)

                @block.sync
                def _(sync):
                    m.qd.drain(sync)
                    i = sync.dma_start(out=dbg_out[:, 0:2048], in_=ks[:, :])
                    _h = m.qd.post(i)
            elif dbg == "a1":
                AROWd = m.vw(m.aX, 53248, 1, 4096, F32)
                ABSBd = m.vw(m.aX, 36864, 128, 4096, BF16)

                @block.sync
                def _(sync):
                    m.qd.drain(sync)
                    i = sync.dma_start(out=dbg_out[:, 0:32], in_=m.al1_sb[:, :])
                    _h = m.qd.post(i)
                    m.qd.drain(sync)
                    i = sync.dma_start(out=dbg_out[:, 32:40], in_=m.al2_sb[:, :])
                    _h = m.qd.post(i)
                    m.qd.drain(sync)
                    i = sync.dma_start(out=dbg_out[0:1, 1024:2048], in_=AROWd[0:1, 0:1024])
                    _h = m.qd.post(i)

                @block.gpsimd
                def _(gpsimd):
                    m.qd.drain(gpsimd)
                    m.qg.drain(gpsimd)
                    i = gpsimd.dma_start(out=dbg_out[:, 2048:2560], in_=ABSBd[:, 0:512])
                    _h = m.qg.post(i)
            else:
                src, sdt = src_map[dbg]
                rows, cols = src.shape
                bounce = m.vw(m.aX, 0, 128, 4096, sdt)
                n_el = rows * cols
                srcf = src.rearrange("a b -> (a b)").rearrange("(n p f) -> n p f", p=128, f=4096)
                nch = n_el // (128 * 4096)
                nch = min(nch, 8)
                dbf = dbg_out.rearrange("p (n f) -> n p f", n=8) if nch > 1 else None

                @block.gpsimd
                def _(gpsimd):
                    for ci in range(nch):
                        wait(gpsimd, m.sg, m.sg.n)
                        wait(gpsimd, m.sd, m.sd.n)
                        m.qg.drain(gpsimd)
                        i = gpsimd.dma_start(out=bounce[:, :], in_=srcf[ci, :, :])
                        _h = m.qg.post(i)
                        wait(gpsimd, m.sg, m.sg.n)
                        if nch > 1:
                            m.qg.drain(gpsimd)
                            i = gpsimd.dma_start(out=dbf[ci, :, 0:512], in_=bounce[:, 0:512])
                        else:
                            m.qg.drain(gpsimd)
                            i = gpsimd.dma_start(out=dbg_out[:, :], in_=bounce[:, :])
                        _h = m.qg.post(i)

        @block.sync
        def _(sync):
            m.qd.drain(sync)
            m.qg.drain(sync)
    es.close()
    return nc


DPERM = np.arange(4096).reshape(64, 64).T.reshape(-1)


def prep_inputs(inputs):
    f = np.ascontiguousarray
    w = {}
    w["wqa"] = f(inputs["L0_attn_a_Wq"].reshape(8, 128, 4096).transpose(1, 0, 2).reshape(128, 32768))
    wk = inputs["L0_attn_a_Wk"][:, DPERM]
    w["wka"] = f(wk.reshape(8, 128, 4096).transpose(1, 0, 2).reshape(128, 32768))
    w["wva"] = f(inputs["L0_attn_a_Wv"].reshape(8, 128, 4096).transpose(1, 0, 2).reshape(128, 32768))
    w["woa"] = f(inputs["L0_attn_a_Wo"].reshape(32, 128, 1024).transpose(1, 0, 2).reshape(128, 32768))
    w["boa"] = f(inputs["L0_attn_a_bo"].reshape(8, 128).T)
    w["wqb"] = f(inputs["L0_attn_b_Wq"].reshape(8, 128, 1024).transpose(1, 0, 2).reshape(128, 8192))
    w["wkb"] = f(inputs["L0_attn_b_Wk"].reshape(8, 128, 1024).transpose(1, 0, 2).reshape(128, 8192))
    w["wvb"] = f(inputs["L0_attn_b_Wv"].reshape(8, 128, 1024).transpose(1, 0, 2).reshape(128, 8192))
    w["wob"] = f(inputs["L0_attn_b_Wo"].reshape(8, 128, 1024).transpose(1, 0, 2).reshape(128, 8192))
    w["w1t"] = f(inputs["L0_ffn_w1"].T.reshape(8, 128, 4096).transpose(1, 0, 2).reshape(128, 32768))
    w["w2t"] = f(inputs["L0_ffn_w2"].T.reshape(32, 128, 1024).transpose(1, 0, 2).reshape(128, 32768))
    w["b1"] = f(inputs["L0_ffn_b1"].reshape(32, 128).T)
    w["b2"] = f(inputs["L0_ffn_b2"].reshape(8, 128).T)
    w["ident"] = np.eye(128, dtype=np.float32)
    cm = np.triu(np.ones((128, 128), dtype=np.float32), 1) * (-1e30)
    w["cmask"] = cm
    return w


def kernel(**inputs):
    w = prep_inputs(inputs)
    x = inputs["L0_attn_a_x"]
    nc = build_nc()
    in_maps = []
    for c in range(8):
        m = dict(w)
        m["x0"] = np.ascontiguousarray(x[c].T)
        in_maps.append(m)
    res = run_bass_kernel_spmd(nc, in_maps, core_ids=list(range(8)))
    outs = [np.ascontiguousarray(r["out"].T) for r in res.results]
    return np.stack(outs, 0)
